# version 02
# baseline (speedup 1.0000x reference)
import numpy as np
import ml_dtypes
import concourse.bass as bass
import concourse.mybir as mybir
from concourse.bass_utils import run_bass_kernel_spmd
from contextlib import ExitStack

F32 = mybir.dt.float32
BF16 = mybir.dt.bfloat16
AF = mybir.ActivationFunctionType
OP = mybir.AluOpType
AX = mybir.AxisListType

PHASES = "PAF"
PSTAGE = 9
K5 = 3
N = 2048
E = 2048
H = 16
D = 128
NEXP = 8
MLP = 4096
NT = 16
NG = 4
GS = 512
HALF = 1024
NEG = -1.0e30
ENGINES = ("tensor", "vector", "scalar", "gpsimd", "sync")


class Op:
    __slots__ = ("engine", "fn", "deps", "dma", "ndma", "lane", "ev", "needed", "tag")

    def __init__(self, engine, fn, deps, dma, ndma, tag):
        self.engine = engine
        self.fn = fn
        self.deps = [d for d in deps if d is not None]
        self.dma = dma
        self.ndma = ndma
        self.lane = None
        self.ev = None
        self.needed = False
        self.tag = tag


class Prog:

    NDMA_SEMS = 16

    def __init__(self, nc, ctx):
        self.nc = nc
        self.ops = []
        self.sems = {}
        self.sem_vals = {}
        self.dma_idx = {}
        for e in ENGINES:
            name = f"s_{e}_c"
            self.sems[name] = ctx.enter_context(nc.semaphore(name))
            self.sem_vals[name] = 0
            self.dma_idx[e] = 0
            for i in range(self.NDMA_SEMS):
                name = f"s_{e}_d{i}"
                self.sems[name] = ctx.enter_context(nc.semaphore(name))
                self.sem_vals[name] = 0
        self.last = {e: None for e in ENGINES}
        self.last_compute = {e: None for e in ENGINES}
        self.lane_last = {e: [None] * self.NDMA_SEMS for e in ENGINES}

    def op(self, engine, fn, deps=(), dma=False, ndma=1, tag=""):
        o = Op(engine, fn, list(deps), dma, ndma, tag)
        if engine in ("vector", "scalar", "gpsimd", "sync") and self.last[engine] is not None:
            o.deps.append(self.last[engine])
        if dma:
            o.lane = self.dma_idx[engine]
            self.dma_idx[engine] = (o.lane + 1) % self.NDMA_SEMS
            self.lane_last[engine][o.lane] = o
        else:
            self.last_compute[engine] = o
        self.ops.append(o)
        self.last[engine] = o
        return o

    def fence(self, engine):
        out = [o for o in self.lane_last[engine] if o is not None]
        if self.last_compute[engine] is not None:
            out.append(self.last_compute[engine])
        return out

    def fence_all(self):
        return [o for e in ENGINES for o in self.fence(e)]

    def emit_phase(self, block, start_idx):
        ops = self.ops[start_idx:]
        cur = set(id(o) for o in ops)
        for o in ops:
            for d in o.deps:
                if id(d) not in cur and d.ev is None:
                    raise RuntimeError(
                        f"dep {d.tag} of {o.tag} is from an already-emitted phase "
                        f"but was not pinned")
                d.needed = True
        for o in ops:
            if (o.needed or o.dma) and o.ev is None:
                if o.dma:
                    name = f"s_{o.engine}_d{o.lane}"
                    step = 16 * o.ndma
                else:
                    name = f"s_{o.engine}_c"
                    step = 1
                self.sem_vals[name] += step
                o.ev = (name, self.sem_vals[name])
        per_engine = {e: [] for e in ENGINES}
        for o in ops:
            per_engine[o.engine].append(o)
        for e in ENGINES:
            recs = per_engine[e]
            if not recs:
                continue
            sec = getattr(block, e)

            def section(eng, recs=recs):
                seen = {}
                for o in recs:
                    for d in o.deps:
                        if d.ev is None:
                            raise RuntimeError(f"dep of {o.tag} has no event: {d.tag}")
                        s, v = d.ev
                        if seen.get(s, 0) < v:
                            eng.wait_ge(self.sems[s], v)
                            seen[s] = v
                    if o.dma and o.ev is not None:
                        s, v = o.ev
                        prev = v - 16 * o.ndma
                        if prev > 0 and seen.get(s, 0) < prev:
                            eng.wait_ge(self.sems[s], prev)
                            seen[s] = prev
                    inst = o.fn(eng)
                    if o.ev is not None:
                        insts = inst if isinstance(inst, list) else [inst]
                        assert len(insts) == (o.ndma if o.dma else 1), o.tag
                        for i2 in insts:
                            i2.then_inc(self.sems[o.ev[0]], 16 if o.dma else 1)

            sec(section)
        return len(self.ops)


class Region:

    def __init__(self):
        self.w = None
        self.readers = []

    def wdeps(self):
        return ([self.w] if self.w is not None else []) + self.readers

    def on_write(self, op):
        self.w = op
        self.readers = []

    def rdeps(self):
        return [self.w] if self.w is not None else []

    def on_read(self, op):
        self.readers.append(op)


def rw(regions_read, regions_written, extra=()):
    deps = list(extra)
    for r in regions_read:
        deps += r.rdeps()
    for r in regions_written:
        deps += r.wdeps()
    return deps


def commit(op, regions_read, regions_written):
    for r in regions_read:
        r.on_read(op)
    for r in regions_written:
        r.on_write(op)
    return op


def build_program():
    nc = bass.Bass("TRN2", target_bir_lowering=False)

    x_d = nc.dram_tensor("x", [N, E], F32, kind="ExternalInput").ap()
    wq_d = nc.dram_tensor("Wq", [E, E], F32, kind="ExternalInput").ap()
    wkv_d = nc.dram_tensor("Wkv", [256, E], F32, kind="ExternalInput").ap()
    wkvh_d = nc.dram_tensor("Wkvh", [32, E], F32, kind="ExternalInput").ap()
    wg_d = nc.dram_tensor("Wg", [32, E], F32, kind="ExternalInput").ap()
    wout_d = nc.dram_tensor("Wout", [E, E], F32, kind="ExternalInput").ap()
    gate_d = nc.dram_tensor("gate", [E, NEXP], F32, kind="ExternalInput").ap()
    w0_d = nc.dram_tensor("w0", [NEXP, E, MLP], F32, kind="ExternalInput").ap()
    w1_d = nc.dram_tensor("w1", [NEXP, E, MLP], F32, kind="ExternalInput").ap()
    wo_d = nc.dram_tensor("wo", [NEXP, MLP, E], F32, kind="ExternalInput").ap()
    costk_d = nc.dram_tensor("cos_tok", [N, 64], BF16, kind="ExternalInput").ap()
    sintk_d = nc.dram_tensor("sin_tok", [N, 64], BF16, kind="ExternalInput").ap()
    cosT_d = nc.dram_tensor("cosT", [64, N], BF16, kind="ExternalInput").ap()
    sinT_d = nc.dram_tensor("sinT", [64, N], BF16, kind="ExternalInput").ap()
    mask4_d = nc.dram_tensor("mask4", [128, 2048], BF16, kind="ExternalInput").ap()
    maskT4_d = nc.dram_tensor("maskT4", [128, 2048], BF16, kind="ExternalInput").ap()
    d16_d = nc.dram_tensor("delta16", [16, 16 * 128], BF16, kind="ExternalInput").ap()
    d8_d = nc.dram_tensor("delta8", [8, 8 * 128], F32, kind="ExternalInput").ap()
    ones1_d = nc.dram_tensor("ones1", [1, 128], F32, kind="ExternalInput").ap()
    ident_d = nc.dram_tensor("ident", [128, 128], F32, kind="ExternalInput").ap()
    y_d = nc.dram_tensor("y", [N, E], F32, kind="ExternalOutput").ap()

    xnT_s = nc.dram_tensor("xnT_s", [16, 128, N], BF16, kind="Internal").ap()
    wqT_s = nc.dram_tensor("wqT_s", [16, 128, E], BF16, kind="Internal").ap()
    woutT_s = nc.dram_tensor("woutT_s", [16, 128, E], BF16, kind="Internal").ap()
    attn_s = nc.dram_tensor("attn_s", [N, E], F32, kind="Internal").ap()
    ffnT_s = nc.dram_tensor("ffnT_s", [E, N], F32, kind="Internal").ap()

    with ExitStack() as root:
        def sbt(ctx, name, shape, dt):
            return ctx.enter_context(nc.sbuf_tensor(name, shape, dt))

        PSB = [root.enter_context(nc.psum_tensor(f"psb{b}", [128, 512], F32))
               for b in range(8)]
        BK = [Region() for _ in range(8)]

        def pslice(c0, c1):
            b = c0 // 512
            assert (c1 - 1) // 512 == b, (c0, c1)
            return PSB[b][:, c0 - b * 512: c1 - b * 512]

        def banks(c0, c1):
            return [BK[b] for b in range(c0 // 512, (c1 + 511) // 512)]

        prog = Prog(nc, root)
        block = root.enter_context(nc.Block())
        emitted = 0

        d8_sb = sbt(root, "d8_sb", [8, 8 * 128], F32)
        ones1_sb = sbt(root, "ones1_sb", [1, 128], F32)
        ident_sb = sbt(root, "ident_sb", [128, 128], F32)
        combT_sb = sbt(root, "combT_sb", [8, NT * 128], F32)

        with ExitStack() as actx:
            xnT_sb = sbt(actx, "xnT_sb", [128, 16 * N], BF16)
            kpT_sb = sbt(actx, "kpT_sb", [128, NT * 128], BF16)
            vp_sb = sbt(actx, "vp_sb", [128, NT * 128], BF16)
            vhtok_sb = sbt(actx, "vhtok_sb", [128, NT * 16], BF16)
            vhtok32_sb = sbt(actx, "vhtok32_sb", [128, NT * 16], F32)
            khT_bf = sbt(actx, "khT_bf", [16, N], BF16)
            costk_sb = sbt(actx, "costk_sb", [128, NT * 64], BF16)
            sintk_sb = sbt(actx, "sintk_sb", [128, NT * 64], BF16)

            with ExitStack() as pctx:
                x_sb = [sbt(pctx, f"x_sb{i}", [128, E], F32) for i in range(2)]
                xntk_sb = [sbt(pctx, f"xntk{i}", [128, E], BF16) for i in range(2)]
                wld_sb = [sbt(pctx, f"wld{i}", [128, E], BF16) for i in range(2)]
                wstage_sb = [sbt(pctx, f"wstage{i}", [128, E], BF16) for i in range(2)]
                wkvT_sb = sbt(pctx, "wkvT_sb", [128, 16 * 256], BF16)
                wgT_sb = sbt(pctx, "wgT_sb", [128, 16 * 32], BF16)
                wkvhT_sb = sbt(pctx, "wkvhT_sb", [128, 16 * 32], BF16)
                gate_sb = sbt(pctx, "gate_sb", [128, 16 * 8], BF16)
                sumsq_sb = sbt(pctx, "sumsq_sb", [128, 16], F32)
                mean_sb = sbt(pctx, "mean_sb", [128, 1], F32)
                var_sb = sbt(pctx, "var_sb", [128, 1], F32)
                rstd_sb = sbt(pctx, "rstd_sb", [128, 1], F32)
                sq_sb = sbt(pctx, "sq_sb", [128, E], BF16)
                G = {}
                for nm in ("pa", "pb", "lgs", "Fx", "tmpa", "tmpb"):
                    G[nm] = sbt(pctx, "G_" + nm, [16, N], F32)
                vhT_bf = sbt(pctx, "vhT_bf", [16, N], BF16)
                kprot_sb = [sbt(pctx, f"kprot{i}", [128, 128], BF16) for i in range(2)]
                ktmp = [sbt(pctx, f"ktmp{i}", [128, 64], F32) for i in range(2)]
                eq1_sb = sbt(pctx, "eq1_sb", [128, NT * 8], F32)
                eq2_sb = sbt(pctx, "eq2_sb", [128, NT * 8], F32)
                comb_sb = sbt(pctx, "comb_sb", [128, NT * 8], F32)
                dcol_sb = sbt(pctx, "dcol_sb", [128, 16], F32)
                w1c_sb = sbt(pctx, "w1c_sb", [128, 16], F32)
                w2c_sb = sbt(pctx, "w2c_sb", [128, 16], F32)
                m1_sb = sbt(pctx, "m1_sb", [128, 1], F32)
                m2_sb = sbt(pctx, "m2_sb", [128, 1], F32)
                lrow_sb = sbt(pctx, "lrow_sb", [1, 512], F32)

                RX = [Region() for _ in range(2)]
                RXN = [Region() for _ in range(2)]
                RWL = [Region() for _ in range(2)]
                RWS = [Region() for _ in range(2)]
                RKP = [Region() for _ in range(2)]
                Rstats = Region()
                Rxnt = Region()
                Rscr = Region()

                const_ops = []

                def ld(dst, src):
                    o = prog.op("scalar", lambda e, d=dst, s=src: e.dma_start(out=d, in_=s),
                                dma=True, tag="cld")
                    const_ops.append(o)
                    return o
                ld(d8_sb[:], d8_d)
                ld(ones1_sb[:], ones1_d)
                ld(ident_sb[:], ident_d)
                ld(costk_sb[:].rearrange("p (c j) -> p c j", c=NT),
                   costk_d.rearrange("(c p) j -> p c j", p=128))
                ld(sintk_sb[:].rearrange("p (c j) -> p c j", c=NT),
                   sintk_d.rearrange("(c p) j -> p c j", p=128))
                op_consts = const_ops

                if PSTAGE >= 2:
                    wq_st = {}
                    wout_st_by_h = {h: [] for h in range(16)}
                    op_wout_tp = []
                    for c in range(16):
                        lop = prog.op("gpsimd",
                                      lambda e, c=c, b=c % 2: e.dma_start(
                                          out=wld_sb[b][:], in_=wq_d[c * 128:(c + 1) * 128, :]),
                                      deps=rw([], [RWL[c % 2]]), dma=True, tag=f"wqld{c}")
                        commit(lop, [], [RWL[c % 2]])
                        top = prog.op("sync",
                                      lambda e, b=c % 2: e.dma_start_transpose(
                                          wstage_sb[b][:].rearrange("p (k d) -> p k d", k=16),
                                          wld_sb[b][:]),
                                      deps=rw([RWL[c % 2]], [RWS[c % 2]]), dma=True, tag=f"wqtp{c}")
                        commit(top, [RWL[c % 2]], [RWS[c % 2]])
                        sop = prog.op("scalar",
                                      lambda e, c=c, b=c % 2: e.dma_start(
                                          out=wqT_s[c], in_=wstage_sb[b][:]),
                                      deps=rw([RWS[c % 2]], []), dma=True, tag=f"wqst{c}")
                        commit(sop, [RWS[c % 2]], [])
                        wq_st[c] = sop
                    for c in range(16):
                        lop = prog.op("gpsimd",
                                      lambda e, c=c, b=c % 2: e.dma_start(
                                          out=wld_sb[b][:], in_=wout_d[c * 128:(c + 1) * 128, :]),
                                      deps=rw([], [RWL[c % 2]]), dma=True, tag=f"wold{c}")
                        commit(lop, [], [RWL[c % 2]])
                        top = prog.op("sync",
                                      lambda e, b=c % 2: e.dma_start_transpose(
                                          wstage_sb[b][:].rearrange("p (k d) -> p k d", k=16),
                                          wld_sb[b][:]),
                                      deps=rw([RWL[c % 2]], [RWS[c % 2]]), dma=True, tag=f"wotp{c}")
                        commit(top, [RWL[c % 2]], [RWS[c % 2]])
                        op_wout_tp.append(top)
                        for h in range(16):
                            sop = prog.op("scalar",
                                          lambda e, c=c, h=h, b=c % 2: e.dma_start(
                                              out=woutT_s[h][:, c * 128:(c + 1) * 128],
                                              in_=wstage_sb[b][:, h * 128:(h + 1) * 128]),
                                          deps=rw([RWS[c % 2]], []), dma=True, tag=f"wost{c}_{h}")
                            commit(sop, [RWS[c % 2]], [])
                            wout_st_by_h[h].append(sop)
                if PSTAGE >= 3:
                    kv_ld = []
                    for c in range(2):
                        lop = prog.op("gpsimd",
                                      lambda e, c=c, b=c % 2: e.dma_start(
                                          out=wld_sb[b][:], in_=wkv_d[c * 128:(c + 1) * 128, :]),
                                      deps=rw([], [RWL[c % 2]]), dma=True, tag=f"kvld{c}")
                        commit(lop, [], [RWL[c % 2]])
                        top = prog.op("sync",
                                      lambda e, c=c, b=c % 2: e.dma_start_transpose(
                                          wkvT_sb[:].rearrange("p (k n) -> p k n", k=16)
                                          [:, :, c * 128:(c + 1) * 128],
                                          wld_sb[b][:]),
                                      deps=rw([RWL[c % 2]], []), dma=True, tag=f"kvtp{c}")
                        commit(top, [RWL[c % 2]], [])
                        kv_ld.append(top)
                    g32 = prog.op("gpsimd", lambda e: e.dma_start(out=wld_sb[0][0:32, :], in_=wg_d),
                                  deps=rw([], [RWL[0]]), dma=True, tag="g32")
                    commit(g32, [], [RWL[0]])
                    kvh32 = prog.op("gpsimd",
                                    lambda e: e.dma_start(out=wld_sb[1][0:32, :], in_=wkvh_d),
                                    deps=rw([], [RWL[1]]), dma=True, tag="kvh32")
                    commit(kvh32, [], [RWL[1]])
                    gtld = prog.op("gpsimd",
                                   lambda e: e.dma_start(
                                       out=gate_sb[:].rearrange("p (c k) -> p c k", c=16),
                                       in_=gate_d.rearrange("(c p) k -> p c k", p=128)),
                                   dma=True, tag="gtld")
                    wgtp = prog.op("sync",
                                   lambda e: e.dma_start_transpose(
                                       wgT_sb[:].rearrange("p (c k) -> p c k", c=16),
                                       wld_sb[0][0:32, :]),
                                   deps=[g32], dma=True, tag="wgtp")
                    commit(wgtp, [RWL[0]], [])
                    kvhtp = prog.op("sync",
                                    lambda e: e.dma_start_transpose(
                                        wkvhT_sb[:].rearrange("p (c k) -> p c k", c=16),
                                        wld_sb[1][0:32, :]),
                                    deps=[kvh32], dma=True, tag="kvhtp")
                    commit(kvhtp, [RWL[1]], [])

                if PSTAGE >= 4:
                    xn_tp = []
                    for t in range(NT):
                        lop = prog.op("scalar",
                                      lambda e, t=t, b=t % 2: e.dma_start(
                                          out=x_sb[b][:], in_=x_d[t * 128:(t + 1) * 128, :]),
                                      deps=rw([], [RX[t % 2]]), dma=True, tag=f"xld{t}")
                        commit(lop, [], [RX[t % 2]])
                        sqo = prog.op("scalar",
                                      lambda e, t=t, b=t % 2: e.activation(
                                          sq_sb[:], x_sb[b][:], AF.Square,
                                          accum_out=sumsq_sb[:, t:t + 1]),
                                      deps=rw([RX[t % 2]], []), tag=f"sq{t}")
                        commit(sqo, [RX[t % 2]], [])
                        vst = prog.op("vector",
                                      lambda e, b=t % 2: e.reduce_sum(mean_sb[:], x_sb[b][:], axis=AX.X),
                                      deps=rw([RX[t % 2]], [Rstats]), tag=f"vm{t}")
                        commit(vst, [RX[t % 2]], [Rstats])
                        prog.op("vector", lambda e: e.tensor_scalar(
                            mean_sb[:], mean_sb[:], 1.0 / E, None, OP.mult, OP.bypass), tag="vm2")
                        prog.op("vector", lambda e: e.tensor_tensor(
                            var_sb[:], mean_sb[:], mean_sb[:], OP.mult), tag="vm3")
                        vvar = prog.op("vector",
                                       lambda e, t=t: e.scalar_tensor_tensor(
                                           var_sb[:], sumsq_sb[:, t:t + 1], 1.0 / E, var_sb[:],
                                           OP.mult, OP.subtract),
                                       deps=[sqo], tag=f"vvar{t}")
                        vve = prog.op("vector", lambda e: e.tensor_scalar(
                            var_sb[:], var_sb[:], 1e-5, None, OP.add, OP.bypass), tag="veps")
                        sln = prog.op("scalar", lambda e: e.activation(rstd_sb[:], var_sb[:], AF.Ln),
                                      deps=[vve], tag=f"sln{t}")
                        sex = prog.op("scalar", lambda e: e.activation(
                            rstd_sb[:], rstd_sb[:], AF.Exp, scale=-0.5), tag=f"sex{t}")
                        vxn = prog.op("vector",
                                      lambda e, b=t % 2: e.tensor_scalar(
                                          xntk_sb[b][:], x_sb[b][:], mean_sb[:, 0:1],
                                          rstd_sb[:, 0:1], OP.subtract, OP.mult),
                                      deps=rw([RX[t % 2], Rstats], [RXN[t % 2]], extra=[sex]),
                                      tag=f"vxn{t}")
                        commit(vxn, [RX[t % 2], Rstats], [RXN[t % 2]])
                        commit(vxn, [], [RX[t % 2]])
                        top = prog.op("sync",
                                      lambda e, t=t, b=t % 2: e.dma_start_transpose(
                                          xnT_sb[:].rearrange("p (c n) -> p c n", c=16)
                                          [:, :, t * 128:(t + 1) * 128],
                                          xntk_sb[b][:]),
                                      deps=rw([RXN[t % 2]], [Rxnt]), dma=True, tag=f"xntp{t}")
                        commit(top, [RXN[t % 2]], [])
                        xn_tp.append(top)
                    for o in prog.fence("sync"):
                        Rxnt.on_write(o) if False else None
                    Rxnt.w = None
                    Rxnt.readers = []
                    xn_tp_fence = prog.fence("sync")

                    def xnT_store(e):
                        return [e.dma_start(
                            out=xnT_s[q * 8:(q + 1) * 8].rearrange("c p n -> p c n"),
                            in_=xnT_sb[:].rearrange("p (c n) -> p c n", c=16)
                            [:, q * 8:(q + 1) * 8, :]) for q in range(2)]
                    xns = prog.op("scalar", xnT_store, deps=xn_tp_fence, dma=True, ndma=2,
                                  tag="xnst")

                if PSTAGE >= 5:
                    dep_xnt = xn_tp_fence
                    kv_done = []
                    for t in range(NT):
                        def kvmm(e, t=t):
                            for ec in range(16):
                                inst = e.matmul(pslice(0, 256),
                                                xnT_sb[:, ec * N + t * 128: ec * N + t * 128 + 128],
                                                wkvT_sb[:, ec * 256:(ec + 1) * 256],
                                                start=(ec == 0), stop=(ec == 15))
                            return inst
                        mm = prog.op("tensor", kvmm,
                                     deps=rw([], [BK[0]], extra=dep_xnt + [kv_ld[1]] + op_consts),
                                     tag=f"kv{t}")
                        commit(mm, [], [BK[0]])
                        b = t % 2

                        if K5 == 1:
                            rp = prog.op("vector",
                                         lambda e, b=b: e.tensor_copy(
                                             kprot_sb[b][:], pslice(0, 128)),
                                         deps=rw([BK[0]], [RKP[b]], extra=[mm]),
                                         tag=f"ropek{t}cp")
                            commit(rp, [BK[0]], [RKP[b]])
                        elif K5 in (0, 4):
                            rp = mm
                        elif K5 == 5:
                            rp = prog.op("vector",
                                         lambda e, b=b: e.tensor_copy(
                                             kprot_sb[b][:], pslice(0, 128)),
                                         deps=rw([BK[0]], [RKP[b]], extra=[mm]),
                                         tag=f"ropek{t}cp")
                            commit(rp, [BK[0]], [RKP[b]])
                        if K5 >= 2:
                         prog.op("vector", lambda e, t=t: e.tensor_tensor(
                            ktmp[0][:], pslice(0, 64), costk_sb[:, t * 64:(t + 1) * 64], OP.mult),
                            deps=rw([BK[0]], [RKP[b]], extra=[mm]), tag=f"ropek{t}a")
                         prog.op("vector", lambda e, t=t: e.tensor_tensor(
                            ktmp[1][:], pslice(64, 128), sintk_sb[:, t * 64:(t + 1) * 64], OP.mult),
                            tag=f"ropek{t}b")
                         prog.op("vector", lambda e, b=b: e.tensor_tensor(
                            kprot_sb[b][:, 0:64], ktmp[0][:], ktmp[1][:], OP.subtract),
                            tag=f"ropek{t}c")
                         prog.op("vector", lambda e, t=t: e.tensor_tensor(
                            ktmp[0][:], pslice(64, 128), costk_sb[:, t * 64:(t + 1) * 64], OP.mult),
                            tag=f"ropek{t}d")
                         prog.op("vector", lambda e, t=t: e.tensor_tensor(
                            ktmp[1][:], pslice(0, 64), sintk_sb[:, t * 64:(t + 1) * 64], OP.mult),
                            tag=f"ropek{t}e")
                         rp = prog.op("vector", lambda e, b=b: e.tensor_tensor(
                            kprot_sb[b][:, 64:128], ktmp[0][:], ktmp[1][:], OP.add),
                            tag=f"ropek{t}f")
                         commit(rp, [BK[0]], [RKP[b]])
                        vpo = prog.op("vector",
                                      lambda e, t=t: e.tensor_copy(vp_sb[:, t * 128:(t + 1) * 128],
                                                                   pslice(128, 256)),
                                      deps=[mm], tag=f"vp{t}")
                        commit(vpo, [BK[0]], [])
                        tp = prog.op("sync",
                                     lambda e, t=t, b=b: e.dma_start_transpose(
                                         kpT_sb[:, t * 128:(t + 1) * 128], kprot_sb[b][:]),
                                     deps=rw([RKP[b]], []), dma=True, tag=f"kptp{t}")
                        commit(tp, [RKP[b]], [])
                        kv_done.append(tp)
                    kpT_ready = kv_done[-1]

                if PSTAGE >= 6:
                    comb_parts = []
                    for t in range(NT):
                        def gmm(e, t=t):
                            for ec in range(16):
                                inst = e.matmul(pslice(2560, 2568),
                                                xnT_sb[:, ec * N + t * 128: ec * N + t * 128 + 128],
                                                gate_sb[:, ec * 8:(ec + 1) * 8],
                                                start=(ec == 0), stop=(ec == 15))
                            return inst
                        mm = prog.op("tensor", gmm,
                                     deps=rw([], [BK[5]], extra=dep_xnt + [gtld]), tag=f"g8{t}")
                        commit(mm, [], [BK[5]])

                        prog.op("vector", lambda e: e.reduce_max(
                            m1_sb[:], pslice(2560, 2568), axis=AX.X), deps=[mm], tag=f"gv{t}a")
                        prog.op("vector", lambda e, t=t: e.tensor_scalar(
                            eq1_sb[:, t * 8:(t + 1) * 8], pslice(2560, 2568),
                            m1_sb[:, 0:1], None, OP.is_equal, OP.bypass), tag=f"gv{t}b")
                        prog.op("vector", lambda e, t=t: e.scalar_tensor_tensor(
                            comb_sb[:, t * 8:(t + 1) * 8], eq1_sb[:, t * 8:(t + 1) * 8], NEG,
                            pslice(2560, 2568), OP.mult, OP.add), tag=f"gv{t}c")
                        prog.op("vector", lambda e, t=t: e.reduce_max(
                            m2_sb[:], comb_sb[:, t * 8:(t + 1) * 8], axis=AX.X), tag=f"gv{t}d")
                        prog.op("vector", lambda e, t=t: e.tensor_scalar(
                            eq2_sb[:, t * 8:(t + 1) * 8], pslice(2560, 2568),
                            m2_sb[:, 0:1], None, OP.is_equal, OP.bypass), tag=f"gv{t}e")
                        gv = prog.op("vector", lambda e, t=t: e.tensor_tensor(
                            dcol_sb[:, t:t + 1], m1_sb[:], m2_sb[:], OP.subtract), tag=f"gv{t}f")
                        commit(gv, [BK[5]], [])
                        comb_parts.append(gv)
                    sig = prog.op("scalar", lambda e: e.activation(w1c_sb[:], dcol_sb[:], AF.Sigmoid),
                                  deps=[comb_parts[-1]], tag="sig")
                    vw2 = prog.op("vector", lambda e: e.tensor_scalar(
                        w2c_sb[:], w1c_sb[:], -1.0, 1.0, OP.mult, OP.add), deps=[sig], tag="vw2")
                    comb_done = []
                    for t in range(NT):
                        prog.op("vector",
                                lambda e, t=t: e.tensor_scalar(
                                    eq1_sb[:, t * 8:(t + 1) * 8], eq1_sb[:, t * 8:(t + 1) * 8],
                                    w1c_sb[:, t:t + 1], None, OP.mult, OP.bypass),
                                deps=[vw2], tag=f"c1{t}")
                        cop = prog.op("vector",
                                      lambda e, t=t: e.scalar_tensor_tensor(
                                          comb_sb[:, t * 8:(t + 1) * 8], eq2_sb[:, t * 8:(t + 1) * 8],
                                          w2c_sb[:, t:t + 1], eq1_sb[:, t * 8:(t + 1) * 8],
                                          OP.mult, OP.add),
                                      tag=f"c2{t}")
                        comb_done.append(cop)
                    ct_done = []
                    for t in range(NT):
                        tp = prog.op("tensor",
                                     lambda e, t=t: e.matmul(PSB[6][0:8, 0:128],
                                                             comb_sb[:, t * 8:(t + 1) * 8],
                                                             ident_sb[:], start=True, stop=True,
                                                             is_transpose=True),
                                     deps=rw([], [BK[6]], extra=[comb_done[t]] + op_consts),
                                     tag=f"ctT{t}")
                        commit(tp, [], [BK[6]])
                        cp = prog.op("vector",
                                     lambda e, t=t: e.tensor_copy(combT_sb[:, t * 128:(t + 1) * 128],
                                                                  PSB[6][0:8, 0:128]),
                                     deps=[tp], tag=f"ctC{t}")
                        commit(cp, [BK[6]], [])
                        ct_done.append(cp)

                if PSTAGE >= 7:
                    kh_ready = vh_ready = None
                    prev_chain_last = None
                    prev_ln1 = None
                    prev_vdc = None
                    for gi, outbf in enumerate([khT_bf, vhT_bf]):
                        off = 0 if gi == 0 else 16
                        evs = {}
                        for nm, wT in (("pa", wgT_sb), ("pb", wkvhT_sb)):
                            def smm(e, wT=wT, off=off):
                                for cc in range(4):
                                    for ec in range(16):
                                        inst = e.matmul(
                                            PSB[1 + cc][0:16, 0:512],
                                            wT[:, ec * 32 + off: ec * 32 + off + 16],
                                            xnT_sb[:, ec * N + cc * 512: ec * N + cc * 512 + 512],
                                            start=(ec == 0), stop=(ec == 15))
                                return inst
                            extra = dep_xnt + [wgtp, kvhtp]
                            if prev_chain_last is not None:
                                extra = extra + [prev_chain_last, prev_ln1, prev_vdc]
                            mm = prog.op("tensor", smm,
                                         deps=rw([], banks(512, 2560), extra=extra),
                                         tag=f"sm{gi}{nm}")
                            for b in banks(512, 2560):
                                b.on_write(mm)
                            ev = None
                            for cc in range(4):
                                ev = prog.op("vector",
                                             lambda e, nm=nm, cc=cc: e.tensor_copy(
                                                 G[nm][:, cc * 512:(cc + 1) * 512],
                                                 PSB[1 + cc][0:16, 0:512]),
                                             deps=([mm] if cc == 0 else []),
                                             tag=f"sme{gi}{nm}{cc}")
                                BK[1 + cc].on_read(ev)
                            evs[nm] = ev
                        small_evac = {"pa": evs["pa"], "pb": evs["pb"]}
                        gnm, pre = "pa", "pb"
                        sab = prog.op("scalar",
                                      lambda e, gnm=gnm: e.activation(G["tmpa"][:], G[gnm][:], AF.Abs),
                                      deps=[small_evac[gnm]] +
                                           ([prev_chain_last] if prev_chain_last else []),
                                      tag=f"abs{gi}")
                        prog.op("scalar", lambda e: e.activation(G["tmpa"][:], G["tmpa"][:],
                                                                 AF.Exp, scale=-1.0), tag=f"ex{gi}")
                        ln1 = prog.op("scalar", lambda e: e.activation(G["tmpa"][:], G["tmpa"][:],
                                                                       AF.Ln, bias=1.0), tag=f"ln{gi}")
                        vmin = prog.op("vector",
                                       lambda e, gnm=gnm: e.tensor_scalar(
                                           G["lgs"][:], G[gnm][:], 0.0, None, OP.min, OP.bypass),
                                       deps=[small_evac[gnm]] +
                                            ([prev_chain_last] if prev_chain_last else []),
                                       tag=f"mn{gi}")
                        vsub = prog.op("vector", lambda e: e.tensor_tensor(
                            G["lgs"][:], G["lgs"][:], G["tmpa"][:], OP.subtract),
                            deps=[ln1], tag=f"ls{gi}")
                        prog.op("vector", lambda e: e.memset(G["tmpb"][:, 0:1], 0.0), tag=f"z{gi}")
                        prog.op("vector", lambda e: e.tensor_copy(G["tmpb"][:, 1:N],
                                                                  G["lgs"][:, 0:N - 1]), tag=f"sh{gi}")
                        fsc = prog.op("vector", lambda e: e.tensor_tensor_scan(
                            G["Fx"][:], G["tmpb"][:], G["tmpb"][:], 0.0, OP.add, OP.bypass),
                            tag=f"fs{gi}")
                        sef = prog.op("scalar", lambda e: e.activation(G["tmpa"][:], G["Fx"][:], AF.Exp),
                                      deps=[fsc, vsub], tag=f"ef{gi}")
                        vdc = prog.op("vector",
                                      lambda e, pre=pre: e.tensor_tensor(
                                          G["tmpb"][:], G[pre][:], G["tmpa"][:], OP.mult),
                                      deps=[sef, small_evac[pre]], tag=f"dc{gi}")
                        vcs = prog.op("vector",
                                      lambda e: e.tensor_tensor_scan(
                                          G["lgs"][:], G["tmpb"][:], G["tmpb"][:], 0.0,
                                          OP.add, OP.bypass),
                                      tag=f"cs{gi}")
                        if gi == 0:
                            cpy = prog.op("vector",
                                          lambda e, outbf=outbf: e.tensor_scalar(
                                              outbf[:], G["lgs"][:], float(1.0 / np.sqrt(D)),
                                              None, OP.mult, OP.bypass),
                                          tag=f"obf{gi}")
                        else:
                            cpy = prog.op("vector",
                                          lambda e, outbf=outbf: e.tensor_copy(outbf[:], G["lgs"][:]),
                                          tag=f"obf{gi}")
                        if gi == 0:
                            kh_ready = cpy
                        else:
                            vh_ready = cpy
                        prev_chain_last = cpy
                        prev_ln1 = ln1
                        prev_vdc = vdc
                    vh_tp = prog.op("sync",
                                    lambda e: e.dma_start_transpose(
                                        vhtok_sb[:].rearrange("p (c j) -> p c j", c=NT), vhT_bf[:]),
                                    deps=[vh_ready], dma=True, tag="vhtp")
                    vh_tp = prog.op("vector",
                                    lambda e: e.tensor_copy(vhtok32_sb[:], vhtok_sb[:]),
                                    deps=[vh_tp], tag="vhtp32")

                if PSTAGE < 7:
                    kh_ready = kpT_ready = vh_tp = None
                    if PSTAGE < 5:
                        kpT_ready = None
                p_last = prog.fence_all()
                for o in (p_last + [kh_ready, kpT_ready, vh_tp] if PSTAGE >= 7 else p_last):
                    o.needed = True
                emitted = prog.emit_phase(block, emitted)
                for b in BK:
                    b.w = None
                    b.readers = []

            if "A" not in PHASES:
                return nc
            with ExitStack() as awctx:
                cosT_sb = sbt(awctx, "cosT_sb", [64, N], BF16)
                sinT_sb = sbt(awctx, "sinT_sb", [64, N], BF16)
                mask4_sb = sbt(awctx, "mask4_sb", [128, 2048], BF16)
                maskT4_sb = sbt(awctx, "maskT4_sb", [128, 2048], BF16)
                d16_sb = sbt(awctx, "d16_sb", [16, 16 * 128], BF16)
                wqh_sb = [sbt(awctx, f"wqh{i}", [128, E], BF16) for i in range(2)]
                woh_sb = [sbt(awctx, f"woh{i}", [128, HALF], BF16) for i in range(2)]
                qh_sb = [sbt(awctx, f"qh{i}", [128, GS], BF16) for i in range(2)]
                khK_sb = [sbt(awctx, f"khK{i}", [128, N], BF16) for i in range(2)]
                vh_sb = [sbt(awctx, f"vhb{i}", [128, NT * 128], BF16) for i in range(2)]
                pt_sb = [sbt(awctx, f"ptb{i}", [128, GS], BF16) for i in range(2)]
                stmp_sb = [sbt(awctx, f"stmp{i}", [128, GS], F32) for i in range(2)]
                oall_sb = sbt(awctx, "oall_sb", [128, H * GS], BF16)
                trash_sb = sbt(awctx, "trash_sb", [128, 2048], BF16)
                axb_sb = [sbt(awctx, f"axb{i}", [128, HALF], F32) for i in range(2)]
                negmx_sb = sbt(awctx, "negmx_sb", [128, 4], F32)
                nmx4_sb = sbt(awctx, "nmx4_sb", [128, 4], F32)
                sstage_sb = sbt(awctx, "sstage_sb", [128, 2048], F32)
                sume4_sb = sbt(awctx, "sume4_sb", [128, 4], F32)
                sume_sb = sbt(awctx, "sume_sb", [128, 4], F32)
                lsum_sb = sbt(awctx, "lsum_sb", [128, 4], F32)
                lrow2_sb = sbt(awctx, "lrow2_sb", [1, 512], F32)
                lsb_sb = sbt(awctx, "lsb_sb", [128, 512], F32)

                RWQ = [Region() for _ in range(2)]
                RWOH = [Region() for _ in range(2)]
                RQH = [Region() for _ in range(2)]
                RKH = [Region() for _ in range(2)]
                RVH = [Region() for _ in range(2)]
                RPT = [Region() for _ in range(2)]
                RST = [Region() for _ in range(2)]
                ROA = [Region() for _ in range(H)]
                RAXB = [Region() for _ in range(2)]
                RLS = Region()
                dep_P = p_last

                aconst_ops = []

                def lda(dst, src):
                    o = prog.op("scalar",
                                lambda e, d=dst, s=src: e.dma_start(out=d, in_=s),
                                deps=dep_P, dma=True, tag="alda")
                    aconst_ops.append(o)
                    return o
                lda(cosT_sb[:], cosT_d)
                lda(sinT_sb[:], sinT_d)
                lda(mask4_sb[:], mask4_d)
                lda(maskT4_sb[:], maskT4_d)
                lda(d16_sb[:], d16_d)

                for g in range(NG):
                    mend = GS * (g + 1)
                    nch = g + 1
                    for h in range(H):
                        ih = g * H + h
                        bwq = ih % 2
                        wql = prog.op("scalar",
                                      lambda e, h=h, b=bwq: e.dma_start(out=wqh_sb[b][:],
                                                                        in_=wqT_s[h]),
                                      deps=rw([], [RWQ[bwq]],
                                              extra=[wq_st[h]] +
                                                    (dep_P if ih == 0 else [])),
                                      dma=True, tag=f"wql{g}_{h}")
                        commit(wql, [], [RWQ[bwq]])

                        def qmm(e, g=g, b=bwq):
                            for ec in range(16):
                                inst = e.matmul(pslice(0, GS),
                                                wqh_sb[b][:, ec * 128:(ec + 1) * 128],
                                                xnT_sb[:, ec * N + g * GS: ec * N + g * GS + GS],
                                                start=(ec == 0), stop=(ec == 15))
                            return inst
                        qp = prog.op("tensor", qmm,
                                     deps=rw([RWQ[bwq]], [BK[0]],
                                             extra=(dep_P if ih == 0 else [])),
                                     tag=f"qp{g}_{h}")
                        commit(qp, [RWQ[bwq]], [BK[0]])

                        bq = ih % 2

                        prog.op("vector", lambda e, g=g: e.tensor_tensor(
                            stmp_sb[0][0:64, :], PSB[0][0:64, 0:GS],
                            cosT_sb[:, g * GS:(g + 1) * GS], OP.mult),
                            deps=rw([BK[0]], [RQH[bq], RST[0], RST[1]],
                                    extra=[qp] + (aconst_ops if ih == 0 else [])),
                            tag=f"rq{g}_{h}a")
                        prog.op("vector", lambda e, g=g: e.tensor_tensor(
                            stmp_sb[1][0:64, :], PSB[0][64:128, 0:GS],
                            sinT_sb[:, g * GS:(g + 1) * GS], OP.mult), tag=f"rq{g}_{h}b")
                        prog.op("vector", lambda e, b=bq: e.tensor_tensor(
                            qh_sb[b][0:64, :], stmp_sb[0][0:64, :], stmp_sb[1][0:64, :],
                            OP.subtract), tag=f"rq{g}_{h}c")
                        prog.op("vector", lambda e, g=g: e.tensor_tensor(
                            stmp_sb[0][0:64, :], PSB[0][64:128, 0:GS],
                            cosT_sb[:, g * GS:(g + 1) * GS], OP.mult), tag=f"rq{g}_{h}d")
                        prog.op("vector", lambda e, g=g: e.tensor_tensor(
                            stmp_sb[1][0:64, :], PSB[0][0:64, 0:GS],
                            sinT_sb[:, g * GS:(g + 1) * GS], OP.mult), tag=f"rq{g}_{h}e")
                        rq = prog.op("vector", lambda e, b=bq: e.tensor_tensor(
                            qh_sb[b][64:128, :], stmp_sb[0][0:64, :], stmp_sb[1][0:64, :],
                            OP.add), tag=f"rq{g}_{h}f")
                        commit(rq, [BK[0]], [RQH[bq]])
                        commit(rq, [], [RST[0]])
                        commit(rq, [], [RST[1]])

                        def khb2(e, nch=nch, h=h):
                            for c in range(nch):
                                inst = e.matmul(pslice(512 + c * 512, 512 + (c + 1) * 512),
                                                d16_sb[:, h * 128:(h + 1) * 128],
                                                khT_bf[:, c * 512:(c + 1) * 512],
                                                start=True, stop=True)
                            return inst
                        kb = prog.op("tensor", khb2,
                                     deps=rw([], banks(512, 512 + mend),
                                             extra=([kh_ready] + aconst_ops if ih == 0 else [])),
                                     tag=f"kb{g}_{h}")
                        for b in banks(512, 512 + mend):
                            b.on_write(kb)
                        bk2 = ih % 2
                        vkh = None
                        for c in range(nch):
                            vkh = prog.op("vector",
                                          lambda e, c=c, b=bk2: e.tensor_tensor(
                                              khK_sb[b][:, c * 512:(c + 1) * 512],
                                              kpT_sb[:, c * 512:(c + 1) * 512],
                                              PSB[1 + c][:, 0:512], OP.mult),
                                          deps=(rw([RKH[bk2]], [],
                                                extra=[kb] + ([kpT_ready] if ih == 0 else []))
                                                if c == 0 else [kb]),
                                          tag=f"vkh{g}_{h}_{c}")
                            BK[1 + c].on_read(vkh)
                        commit(vkh, [], [RKH[bk2]])

                        def vhmul(e, g=g, h=h, b=bk2):
                            for mt in range(4 * (g + 1)):
                                inst = e.tensor_scalar(
                                    vh_sb[b][:, mt * 128:(mt + 1) * 128],
                                    vp_sb[:, mt * 128:(mt + 1) * 128],
                                    vhtok32_sb[:, mt * 16 + h: mt * 16 + h + 1], None,
                                    OP.mult, OP.bypass)
                            return inst
                        vvh = prog.op("vector", vhmul,
                                      deps=rw([RVH[bk2]], [],
                                              extra=([vh_tp] if ih == 0 else [])),
                                      tag=f"vvh{g}_{h}")
                        commit(vvh, [], [RVH[bk2]])

                        exp_ops = []
                        for t in range(4):
                            def smm(e, t=t, nch=nch, b=bq, bk=bk2):
                                for c in range(nch):
                                    inst = e.matmul(pslice(512 + c * 512, 512 + (c + 1) * 512),
                                                    qh_sb[b][:, t * 128:(t + 1) * 128],
                                                    khK_sb[bk][:, c * 512:(c + 1) * 512],
                                                    start=True, stop=True)
                                return inst
                            s1 = prog.op("tensor", smm,
                                         deps=rw([RQH[bq], RKH[bk2]], banks(512, 512 + mend),
                                                 extra=[rq, vkh]),
                                         tag=f"s1_{g}_{h}_{t}")
                            for b in banks(512, 512 + mend):
                                b.on_write(s1)
                            commit(s1, [RQH[bq]], [])
                            commit(s1, [RKH[bk2]], [])

                            prog.op("vector", lambda e, t=t, g=g: e.tensor_tensor(
                                PSB[1 + g][:, 0:512], PSB[1 + g][:, 0:512],
                                mask4_sb[:, t * 512:(t + 1) * 512], OP.add),
                                deps=[s1] + (aconst_ops if ih == 0 and t == 0 else []),
                                tag=f"vmA{g}_{h}_{t}")
                            for c in range(nch):
                                vm = prog.op("vector",
                                             lambda e, c=c: e.reduce_max(
                                                 nmx4_sb[:, c:c + 1], PSB[1 + c][:, 0:512],
                                                 axis=AX.X),
                                             deps=([s1] if c > 0 else []),
                                             tag=f"vmc{g}_{h}_{t}_{c}")
                                BK[1 + c].on_read(vm)
                                BK[1 + g].on_write(vm) if False else None
                            vm = prog.op("vector", lambda e, t=t, nch=nch: e.reduce_max(
                                negmx_sb[:, t:t + 1], nmx4_sb[:, 0:nch], axis=AX.X,
                                negate=True), tag=f"vm{g}_{h}_{t}")
                            BK[1 + g].on_write(vm)
                            se = None
                            for c in range(nch):
                                vcp = prog.op("vector",
                                              lambda e, c=c: e.tensor_copy(
                                                  sstage_sb[:, c * 512:(c + 1) * 512],
                                                  PSB[1 + c][:, 0:512]),
                                              deps=[vm], tag=f"sec{g}_{h}_{t}_{c}")
                                BK[1 + c].on_read(vcp)
                                se = prog.op("scalar",
                                             lambda e, t=t, c=c: e.activation(
                                                 trash_sb[:, c * 512:(c + 1) * 512],
                                                 sstage_sb[:, c * 512:(c + 1) * 512],
                                                 AF.Exp, bias=negmx_sb[:, t:t + 1],
                                                 accum_out=sume4_sb[:, c:c + 1]),
                                             deps=[vcp], tag=f"se{g}_{h}_{t}_{c}")
                            vsm = prog.op("vector",
                                          lambda e, t=t, nch=nch: e.reduce_sum(
                                              sume_sb[:, t:t + 1], sume4_sb[:, 0:nch],
                                              axis=AX.X),
                                          deps=[se], tag=f"vsm{g}_{h}_{t}")
                            sl = prog.op("scalar",
                                         lambda e, t=t: e.activation(
                                             lsum_sb[:, t:t + 1], sume_sb[:, t:t + 1], AF.Ln),
                                         deps=[vsm], tag=f"sl{g}_{h}_{t}")
                            vl = prog.op("vector",
                                         lambda e, t=t: e.tensor_tensor(
                                             lsum_sb[:, t:t + 1], lsum_sb[:, t:t + 1],
                                             negmx_sb[:, t:t + 1], OP.subtract),
                                         deps=[sl], tag=f"vl{g}_{h}_{t}")
                            tt = prog.op("tensor",
                                         lambda e, t=t: e.matmul(
                                             PSB[6][0:1, t * 128:(t + 1) * 128],
                                             lsum_sb[:, t:t + 1], ident_sb[:],
                                             start=True, stop=True, is_transpose=True),
                                         deps=rw([], [BK[6]], extra=[vl]),
                                         tag=f"tt{g}_{h}_{t}")
                            commit(tt, [], [BK[6]]) if t == 0 else BK[6].on_write(tt)
                            exp_ops.append((se, tt))
                        slr = prog.op("vector",
                                      lambda e: e.tensor_copy(lrow2_sb[:], PSB[6][0:1, 0:512]),
                                      deps=rw([], [RLS], extra=[exp_ops[3][1]]),
                                      tag=f"slr{g}_{h}")
                        BK[6].on_read(slr)
                        commit(slr, [], [RLS])
                        tlb = prog.op("tensor",
                                      lambda e: e.matmul(pslice(3072, 3584), ones1_sb[:],
                                                         lrow2_sb[:], start=True, stop=True),
                                      deps=rw([RLS], [BK[6]], extra=[slr]), tag=f"tlb{g}_{h}")
                        commit(tlb, [RLS], [BK[6]])
                        slb = prog.op("vector",
                                      lambda e: e.tensor_copy(lsb_sb[:], pslice(3072, 3584)),
                                      deps=[tlb], tag=f"slb{g}_{h}")
                        BK[6].on_read(slb)
                        RLS.on_write(slb)

                        pv_last = None
                        for mt in range(4 * (g + 1)):
                            pb = mt % 2
                            cols = (2560, 3072) if pb == 0 else (3072, 3584)
                            bnk = BK[5] if pb == 0 else BK[6]
                            p2 = prog.op("tensor",
                                         lambda e, mt=mt, bk=bk2, b=bq, cols=cols: e.matmul(
                                             pslice(*cols),
                                             khK_sb[bk][:, mt * 128:(mt + 1) * 128],
                                             qh_sb[b][:, 0:GS], start=True, stop=True),
                                         deps=rw([RQH[bq], RKH[bk2]], [bnk],
                                                 extra=([slb] if pb == 1 else [])),
                                         tag=f"p2_{g}_{h}_{mt}")
                            commit(p2, [RQH[bq]], [])
                            commit(p2, [RKH[bk2]], [])
                            bnk.on_write(p2)
                            dv = [p2]
                            if mt >= 4 * g:
                                v = mt - 4 * g
                                mk = prog.op("vector",
                                             lambda e, v=v, cols=cols: e.tensor_tensor(
                                                 pslice(*cols), pslice(*cols),
                                                 maskT4_sb[:, v * 512:(v + 1) * 512], OP.add),
                                             deps=[p2], tag=f"mk{g}_{h}_{mt}")
                                bnk.on_read(mk)
                                bnk.on_write(mk)
                                dv = [mk]
                            vs = prog.op("vector",
                                         lambda e, pb=pb, cols=cols: e.tensor_tensor(
                                             stmp_sb[pb][:], pslice(*cols), lsb_sb[:],
                                             OP.subtract),
                                         deps=rw([RST[pb]], [], extra=dv + [slb]),
                                         tag=f"vs{g}_{h}_{mt}")
                            bnk.on_read(vs)
                            commit(vs, [], [RST[pb]])
                            sx = prog.op("scalar",
                                         lambda e, pb=pb: e.activation(
                                             pt_sb[pb][:], stmp_sb[pb][:], AF.Exp),
                                         deps=rw([RST[pb], RPT[pb]], []), tag=f"sx{g}_{h}_{mt}")
                            commit(sx, [RST[pb]], [])
                            commit(sx, [], [RPT[pb]])
                            pv = prog.op("tensor",
                                         lambda e, mt=mt, pb=pb, bk=bk2, g=g: e.matmul(
                                             pslice(3584, 4096),
                                             vh_sb[bk][:, mt * 128:(mt + 1) * 128],
                                             pt_sb[pb][:], start=(mt == 0),
                                             stop=(mt == 4 * (g + 1) - 1)),
                                         deps=rw([RPT[pb], RVH[bk2]], [BK[7]] if mt == 0 else [],
                                                 extra=[sx, vvh]),
                                         tag=f"pv{g}_{h}_{mt}")
                            commit(pv, [RPT[pb]], [])
                            commit(pv, [RVH[bk2]], [])
                            if mt == 0:
                                commit(pv, [], [BK[7]])
                            else:
                                BK[7].on_write(pv)
                            pv_last = pv
                        so = prog.op("vector",
                                     lambda e, h=h: e.tensor_copy(oall_sb[:, h * GS:(h + 1) * GS],
                                                                  pslice(3584, 4096)),
                                     deps=rw([], [ROA[h]], extra=[pv_last]), tag=f"so{g}_{h}")
                        BK[7].on_read(so)
                        commit(so, [], [ROA[h]])

                    evacs = []
                    for eh in range(2):
                        for h in range(H):
                            iwh = (g * 2 + eh) * H + h
                            bw = iwh % 2
                            wol = prog.op("scalar",
                                          lambda e, h=h, eh=eh, b=bw: e.dma_start(
                                              out=woh_sb[b][:],
                                              in_=woutT_s[h][:, eh * HALF:(eh + 1) * HALF]),
                                          deps=rw([], [RWOH[bw]],
                                                  extra=wout_st_by_h[h]), dma=True,
                                          tag=f"wol{g}_{eh}_{h}")
                            commit(wol, [], [RWOH[bw]])
                            for t in range(4):
                                def wmm(e, t=t, h=h, b=bw):
                                    for c2 in range(2):
                                        inst = e.matmul(
                                            pslice(t * 1024 + c2 * 512, t * 1024 + (c2 + 1) * 512),
                                            oall_sb[:, h * GS + t * 128: h * GS + t * 128 + 128],
                                            woh_sb[b][:, c2 * 512:(c2 + 1) * 512],
                                            start=(h == 0), stop=(h == H - 1))
                                    return inst
                                wm = prog.op("tensor", wmm,
                                             deps=rw([RWOH[bw], ROA[h]],
                                                     banks(t * 1024, (t + 1) * 1024) if h == 0 else [],
                                                     extra=[wol]),
                                             tag=f"wm{g}_{eh}_{h}_{t}")
                                commit(wm, [RWOH[bw]], [])
                                commit(wm, [ROA[h]], [])
                                if h == 0:
                                    for b in banks(t * 1024, (t + 1) * 1024):
                                        b.on_write(wm)
                                else:
                                    for b in banks(t * 1024, (t + 1) * 1024):
                                        b.on_write(wm)
                        for t in range(4):
                            ba = (eh * 4 + t) % 2
                            ev = prog.op("vector",
                                         lambda e, t=t, b=ba: e.tensor_copy(
                                             axb_sb[b][:, 0:512],
                                             pslice(t * 1024, t * 1024 + 512)),
                                         deps=rw([], [RAXB[ba]],
                                                 extra=[prog.last["tensor"]]),
                                         tag=f"ev{g}_{eh}_{t}a")
                            BK[2 * t].on_read(ev)
                            ev = prog.op("vector",
                                         lambda e, t=t, b=ba: e.tensor_copy(
                                             axb_sb[b][:, 512:1024],
                                             pslice(t * 1024 + 512, (t + 1) * 1024)),
                                         tag=f"ev{g}_{eh}_{t}b")
                            BK[2 * t + 1].on_read(ev)
                            commit(ev, [], [RAXB[ba]])
                            st = prog.op("scalar",
                                         lambda e, g=g, eh=eh, t=t, b=ba: e.dma_start(
                                             out=attn_s[g * GS + t * 128: g * GS + (t + 1) * 128,
                                                        eh * HALF:(eh + 1) * HALF],
                                             in_=axb_sb[b][:]),
                                         deps=[ev], dma=True, tag=f"st{g}_{eh}_{t}")
                            commit(st, [RAXB[ba]], [])
                            evacs.append(ev)
                a_last = prog.fence_all()
                for o in a_last:
                    o.needed = True
                emitted = prog.emit_phase(block, emitted)
                for b in BK:
                    b.w = None
                    b.readers = []

        if "F" not in PHASES:
            return nc
        with ExitStack() as fctx:
            inter_sb = sbt(fctx, "inter_sb", [128, 32 * HALF], BF16)
            xnh_sb = sbt(fctx, "xnh_sb", [128, 16 * HALF], BF16)
            w0_sb = [sbt(fctx, f"w0b{i}", [128, 16 * 256], BF16) for i in range(2)]
            w1_sb = [sbt(fctx, f"w1b{i}", [128, 16 * 256], BF16) for i in range(2)]
            wo_sb = [sbt(fctx, f"wob{i}", [128, 32 * 128], BF16) for i in range(2)]
            cbc_sb = sbt(fctx, "cbc_sb", [128, HALF], F32)
            silu_sb = [sbt(fctx, f"silu{i}", [128, HALF], F32) for i in range(2)]
            wev_sb = sbt(fctx, "wev_sb", [128, HALF], F32)
            axf_sb = sbt(fctx, "axf_sb", [128, E], F32)
            xf_sb = sbt(fctx, "xf_sb", [128, E], F32)
            fcol_sb = sbt(fctx, "fcol_sb", [128, 16 * 128], F32)

            RXH = Region()
            RW0 = [Region() for _ in range(2)]
            RW1 = [Region() for _ in range(2)]
            RWO = [Region() for _ in range(2)]
            RCB = Region()
            RSI = [Region() for _ in range(2)]
            RIN = [Region() for _ in range(32)]
            RWE = Region()
            RFFS = Region()
            RAXF = Region()
            RXF = Region()
            RFC = Region()

            dep_A = a_last
            y_stores = []
            for half in range(2):
                def xnh_ld(e, half=half):
                    return [e.dma_start(
                        out=xnh_sb[:].rearrange("p (c n) -> p c n", c=16)[:, q * 8:(q + 1) * 8, :],
                        in_=xnT_s[q * 8:(q + 1) * 8, :, half * HALF:(half + 1) * HALF]
                        .rearrange("c p n -> p c n")) for q in range(2)]
                xl = prog.op("scalar", xnh_ld,
                             deps=rw([], [RXH], extra=(dep_A if half == 0 else [])),
                             dma=True, ndma=2, tag=f"xnh{half}")
                commit(xl, [], [RXH])
                for ex in range(NEXP):
                    iex = half * NEXP + ex
                    cb = prog.op("tensor",
                                 lambda e, ex=ex, half=half: [
                                     e.matmul(pslice(0, 512), d8_sb[:, ex * 128:(ex + 1) * 128],
                                              combT_sb[:, half * HALF: half * HALF + 512],
                                              start=True, stop=True),
                                     e.matmul(pslice(512, 1024), d8_sb[:, ex * 128:(ex + 1) * 128],
                                              combT_sb[:, half * HALF + 512: half * HALF + 1024],
                                              start=True, stop=True)][-1],
                                 deps=rw([], banks(0, 1024),
                                         extra=(dep_A if half == 0 and ex == 0 else [])),
                                 tag=f"cb{half}_{ex}")
                    for b in banks(0, 1024):
                        b.on_write(cb)
                    scb = prog.op("vector",
                                  lambda e: e.tensor_copy(cbc_sb[:, 0:512], pslice(0, 512)),
                                  deps=rw([], [RCB], extra=[cb]), tag=f"scb{half}_{ex}a")
                    BK[0].on_read(scb)
                    scb = prog.op("vector",
                                  lambda e: e.tensor_copy(cbc_sb[:, 512:1024], pslice(512, 1024)),
                                  tag=f"scb{half}_{ex}b")
                    BK[1].on_read(scb)
                    commit(scb, [], [RCB])

                    for mb in range(16):
                        bwb = mb % 2
                        l0 = prog.op("gpsimd",
                                     lambda e, ex=ex, mb=mb, b=bwb: [
                                         e.dma_start(
                                             out=w0_sb[b][:].rearrange("p (c m) -> p c m", c=16)[:, 0:8, :],
                                             in_=w0_d[ex].rearrange("(c p) m -> p c m", p=128)
                                             [:, 0:8, mb * 256:(mb + 1) * 256]),
                                         e.dma_start(
                                             out=w0_sb[b][:].rearrange("p (c m) -> p c m", c=16)[:, 8:16, :],
                                             in_=w0_d[ex].rearrange("(c p) m -> p c m", p=128)
                                             [:, 8:16, mb * 256:(mb + 1) * 256])],
                                     deps=rw([], [RW0[bwb]],
                                             extra=(dep_A if iex == 0 and mb == 0 else [])),
                                     dma=True, ndma=2, tag=f"l0_{iex}_{mb}")
                        commit(l0, [], [RW0[bwb]])
                        l1 = prog.op("gpsimd",
                                     lambda e, ex=ex, mb=mb, b=bwb: [
                                         e.dma_start(
                                             out=w1_sb[b][:].rearrange("p (c m) -> p c m", c=16)[:, 0:8, :],
                                             in_=w1_d[ex].rearrange("(c p) m -> p c m", p=128)
                                             [:, 0:8, mb * 256:(mb + 1) * 256]),
                                         e.dma_start(
                                             out=w1_sb[b][:].rearrange("p (c m) -> p c m", c=16)[:, 8:16, :],
                                             in_=w1_d[ex].rearrange("(c p) m -> p c m", p=128)
                                             [:, 8:16, mb * 256:(mb + 1) * 256])],
                                     deps=rw([], [RW1[bwb]]), dma=True, ndma=2,
                                     tag=f"l1_{iex}_{mb}")
                        commit(l1, [], [RW1[bwb]])
                        for mi in range(2):
                            mt = mb * 2 + mi
                            par = mt % 2
                            c0 = 0 if par == 0 else 2048
                            c1 = 1024 if par == 0 else 3072

                            def h0mm(e, b=bwb, mi=mi, c0=c0):
                                for ec in range(16):
                                    for c2 in range(2):
                                        inst = e.matmul(
                                            pslice(c0 + c2 * 512, c0 + (c2 + 1) * 512),
                                            w0_sb[b][:, ec * 256 + mi * 128: ec * 256 + mi * 128 + 128],
                                            xnh_sb[:, ec * HALF + c2 * 512: ec * HALF + c2 * 512 + 512],
                                            start=(ec == 0), stop=(ec == 15))
                                return inst
                            extra0 = [l0, xl] + ([scb] if (par == 0) else [])
                            hm0 = prog.op("tensor", h0mm,
                                          deps=rw([RW0[bwb], RXH], banks(c0, c0 + 1024),
                                                  extra=extra0),
                                          tag=f"h0_{iex}_{mt}")
                            commit(hm0, [RW0[bwb]], [])
                            commit(hm0, [RXH], [])
                            for b in banks(c0, c0 + 1024):
                                b.on_write(hm0)

                            def h1mm(e, b=bwb, mi=mi, c1=c1):
                                for ec in range(16):
                                    for c2 in range(2):
                                        inst = e.matmul(
                                            pslice(c1 + c2 * 512, c1 + (c2 + 1) * 512),
                                            w1_sb[b][:, ec * 256 + mi * 128: ec * 256 + mi * 128 + 128],
                                            xnh_sb[:, ec * HALF + c2 * 512: ec * HALF + c2 * 512 + 512],
                                            start=(ec == 0), stop=(ec == 15))
                                return inst
                            hm1 = prog.op("tensor", h1mm,
                                          deps=rw([RW1[bwb], RXH], banks(c1, c1 + 1024),
                                                  extra=[l1]),
                                          tag=f"h1_{iex}_{mt}")
                            commit(hm1, [RW1[bwb]], [])
                            commit(hm1, [RXH], [])
                            for b in banks(c1, c1 + 1024):
                                b.on_write(hm1)
                            sic = prog.op("vector",
                                          lambda e, par=par, c0=c0: e.tensor_copy(
                                              silu_sb[par][:, 0:512],
                                              pslice(c0, c0 + 512)),
                                          deps=rw([], [RSI[par]], extra=[hm0]),
                                          tag=f"sic_{iex}_{mt}a")
                            BK[c0 // 512].on_read(sic)
                            sic = prog.op("vector",
                                          lambda e, par=par, c0=c0: e.tensor_copy(
                                              silu_sb[par][:, 512:1024],
                                              pslice(c0 + 512, c0 + 1024)),
                                          tag=f"sic_{iex}_{mt}b")
                            BK[c0 // 512 + 1].on_read(sic)
                            si = prog.op("scalar",
                                         lambda e, par=par: e.activation(
                                             silu_sb[par][:], silu_sb[par][:], AF.Silu),
                                         deps=[sic], tag=f"si_{iex}_{mt}")
                            commit(si, [], [RSI[par]])
                            v1 = prog.op("vector",
                                         lambda e, par=par, c1=c1: e.tensor_tensor(
                                             silu_sb[par][:, 0:512], silu_sb[par][:, 0:512],
                                             pslice(c1, c1 + 512), OP.mult),
                                         deps=rw([RSI[par]], [], extra=[si, hm1]),
                                         tag=f"v1_{iex}_{mt}a")
                            BK[c1 // 512].on_read(v1)
                            v1 = prog.op("vector",
                                         lambda e, par=par, c1=c1: e.tensor_tensor(
                                             silu_sb[par][:, 512:1024], silu_sb[par][:, 512:1024],
                                             pslice(c1 + 512, c1 + 1024), OP.mult),
                                         tag=f"v1_{iex}_{mt}b")
                            BK[c1 // 512 + 1].on_read(v1)
                            commit(v1, [RSI[par]], [RSI[par]])
                            v2 = prog.op("vector",
                                         lambda e, mt=mt, par=par: e.tensor_tensor(
                                             inter_sb[:, mt * HALF:(mt + 1) * HALF],
                                             silu_sb[par][:], cbc_sb[:], OP.mult),
                                         deps=rw([RSI[par], RCB], [RIN[mt]]),
                                         tag=f"v2_{iex}_{mt}")
                            commit(v2, [RSI[par]], [])
                            commit(v2, [RCB], [])
                            commit(v2, [], [RIN[mt]])

                    first_acc_dep = [RFFS.w] if RFFS.w is not None else []
                    for dt in range(16):
                        bwo = dt % 2
                        lwo = prog.op("gpsimd",
                                      lambda e, ex=ex, dt=dt, b=bwo: [
                                          e.dma_start(
                                              out=wo_sb[b][:].rearrange("p (c d) -> p c d", c=32)
                                              [:, q * 8:(q + 1) * 8, :],
                                              in_=wo_d[ex].rearrange("(c p) d -> p c d", p=128)
                                              [:, q * 8:(q + 1) * 8, dt * 128:(dt + 1) * 128])
                                          for q in range(4)],
                                      deps=rw([], [RWO[bwo]]), dma=True, ndma=4,
                                      tag=f"lwo{iex}_{dt}")
                        commit(lwo, [], [RWO[bwo]])
                        pc = (dt % 2) * 1024

                        def womm(e, b=bwo, pc=pc):
                            for mt in range(32):
                                for c2 in range(2):
                                    inst = e.matmul(
                                        pslice(pc + c2 * 512, pc + (c2 + 1) * 512),
                                        wo_sb[b][:, mt * 128:(mt + 1) * 128],
                                        inter_sb[:, mt * HALF + c2 * 512: mt * HALF + c2 * 512 + 512],
                                        start=(mt == 0), stop=(mt == 31))
                                return_inst = inst
                            return return_inst
                        wdeps = rw([RWO[bwo]] + RIN, banks(pc, pc + 1024), extra=[lwo])
                        wm = prog.op("tensor", womm, deps=wdeps, tag=f"wo{iex}_{dt}")
                        commit(wm, [RWO[bwo]], [])
                        for r in RIN:
                            r.on_read(wm)
                        for b in banks(pc, pc + 1024):
                            b.on_write(wm)
                        we = prog.op("vector",
                                     lambda e, pc=pc: e.tensor_copy(wev_sb[:, 0:512],
                                                                    pslice(pc, pc + 512)),
                                     deps=rw([], [RWE], extra=[wm]), tag=f"we{iex}_{dt}a")
                        BK[pc // 512].on_read(we)
                        we = prog.op("vector",
                                     lambda e, pc=pc: e.tensor_copy(wev_sb[:, 512:1024],
                                                                    pslice(pc + 512, pc + 1024)),
                                     tag=f"we{iex}_{dt}b")
                        BK[pc // 512 + 1].on_read(we)
                        commit(we, [], [RWE])
                        ac = prog.op("gpsimd",
                                     lambda e, half=half, ex=ex, dt=dt: e.dma_start(
                                         out=ffnT_s[dt * 128:(dt + 1) * 128,
                                                    half * HALF:(half + 1) * HALF],
                                         in_=wev_sb[:],
                                         **({} if ex == 0 else {"accum_op": OP.add})),
                                     deps=[we] + (first_acc_dep if dt == 0 else []),
                                     dma=True, tag=f"ac{iex}_{dt}")
                        commit(ac, [RWE], [])
                        if dt == 15:
                            RFFS.on_write(ac)

                ffn_done = RFFS.w
                for t in range(8):
                    row0 = half * HALF + t * 128
                    def fcol_ld(e, row0=row0):
                        return [e.dma_start(
                            out=fcol_sb[:].rearrange("p (c n) -> p c n", c=16)
                            [:, q * 8:(q + 1) * 8, :],
                            in_=ffnT_s[q * 1024:(q + 1) * 1024, row0:row0 + 128]
                            .rearrange("(c p) n -> p c n", p=128)) for q in range(2)]
                    fl = prog.op("scalar", fcol_ld,
                                 deps=rw([], [RFC], extra=[ffn_done]), dma=True, ndma=2,
                                 tag=f"fl{half}_{t}")
                    commit(fl, [], [RFC])
                    al = prog.op("scalar",
                                 lambda e, row0=row0: e.dma_start(
                                     out=axf_sb[:], in_=attn_s[row0:row0 + 128, :]),
                                 deps=rw([], [RAXF]), dma=True, tag=f"al{half}_{t}")
                    commit(al, [], [RAXF])
                    xlo = prog.op("scalar",
                                  lambda e, row0=row0: e.dma_start(
                                      out=xf_sb[:], in_=x_d[row0:row0 + 128, :]),
                                  deps=rw([], [RXF]), dma=True, tag=f"xl{half}_{t}")
                    commit(xlo, [], [RXF])

                    def ftp(e):
                        for c in range(16):
                            inst = e.matmul(pslice(c * 128, (c + 1) * 128),
                                            fcol_sb[:, c * 128:(c + 1) * 128], ident_sb[:],
                                            start=True, stop=True, is_transpose=True)
                        return inst
                    tf = prog.op("tensor", ftp,
                                 deps=rw([RFC], banks(0, 2048)), tag=f"tf{half}_{t}")
                    commit(tf, [RFC], [])
                    for b in banks(0, 2048):
                        b.on_write(tf)
                    va = None
                    for q in range(4):
                        va = prog.op("vector",
                                     lambda e, q=q: e.tensor_tensor(
                                         axf_sb[:, q * 512:(q + 1) * 512],
                                         pslice(q * 512, (q + 1) * 512),
                                         axf_sb[:, q * 512:(q + 1) * 512], OP.add),
                                     deps=(rw([RAXF], [], extra=[tf, al]) if q == 0 else []),
                                     tag=f"va{half}_{t}_{q}")
                        BK[q].on_read(va)
                    commit(va, [RAXF], [RAXF])
                    vb = prog.op("vector",
                                 lambda e: e.tensor_tensor(axf_sb[:], axf_sb[:], xf_sb[:],
                                                           OP.add),
                                 deps=rw([RXF], [], extra=[xlo]), tag=f"vb{half}_{t}")
                    commit(vb, [RXF], [RAXF])
                    ys = prog.op("scalar",
                                 lambda e, row0=row0: e.dma_start(
                                     out=y_d[row0:row0 + 128, :], in_=axf_sb[:]),
                                 deps=[vb], dma=True, tag=f"ys{half}_{t}")
                    commit(ys, [RAXF], [])
                    y_stores.append(ys)
            prog.op("scalar", lambda e: e.nop(), deps=[y_stores[-1]], tag="fin")
            emitted = prog.emit_phase(block, emitted)

    return nc


def _consts():
    bf = ml_dtypes.bfloat16
    inv = (10000.0 ** (-np.arange(0, D, 2, dtype=np.float64) / D))
    ang = np.arange(N, dtype=np.float64)[:, None] * inv[None, :]
    cos = np.cos(ang).astype(np.float32)
    sin = np.sin(ang).astype(np.float32)
    cos_tok = cos.astype(bf)
    sin_tok = sin.astype(bf)
    cosT = cos.T.copy().astype(bf)
    sinT = sin.T.copy().astype(bf)
    p = np.arange(128)[:, None]
    c = np.arange(512)[None, :]
    mask4 = np.zeros((128, 4, 512), np.float32)
    maskT4 = np.zeros((128, 4, 512), np.float32)
    for v in range(4):
        vis = (c <= v * 128 + p) & True
        vis = (c < v * 128) | ((c >= v * 128) & (c < (v + 1) * 128) & ((c - v * 128) <= p))
        mask4[:, v, :] = np.where(vis, 0.0, NEG)
        t_ = c // 128
        jj = c % 128
        visT = (t_ > v) | ((t_ == v) & (jj >= p))
        maskT4[:, v, :] = np.where(visT, 0.0, NEG)
    mask4 = mask4.reshape(128, 2048).astype(bf)
    maskT4 = maskT4.reshape(128, 2048).astype(bf)
    d16 = np.zeros((16, 16 * 128), np.float32)
    for h in range(16):
        d16[h, h * 128:(h + 1) * 128] = 1.0
    d8 = np.zeros((8, 8 * 128), np.float32)
    for ex in range(8):
        d8[ex, ex * 128:(ex + 1) * 128] = 1.0
    return {
        "cos_tok": cos_tok, "sin_tok": sin_tok, "cosT": cosT, "sinT": sinT,
        "mask4": mask4, "maskT4": maskT4, "delta16": d16.astype(bf),
        "delta8": d8.astype(np.float32),
        "ones1": np.ones((1, 128), np.float32),
        "ident": np.eye(128, dtype=np.float32),
    }


def kernel(L0_attn_x, L0_attn_Wq, L0_attn_Wkv, L0_attn_Wkvh, L0_attn_Wg,
           L0_attn_Wout, L0_ffn_gate_kernel, L0_ffn_w0_kernel, L0_ffn_w1_kernel,
           L0_ffn_wo_kernel):
    x = np.ascontiguousarray(np.asarray(L0_attn_x, dtype=np.float32))
    shared = {
        "Wq": np.ascontiguousarray(np.asarray(L0_attn_Wq, np.float32)),
        "Wkv": np.ascontiguousarray(np.asarray(L0_attn_Wkv, np.float32)),
        "Wkvh": np.ascontiguousarray(np.asarray(L0_attn_Wkvh, np.float32)),
        "Wg": np.ascontiguousarray(np.asarray(L0_attn_Wg, np.float32)),
        "Wout": np.ascontiguousarray(np.asarray(L0_attn_Wout, np.float32)),
        "gate": np.ascontiguousarray(np.asarray(L0_ffn_gate_kernel, np.float32)),
        "w0": np.ascontiguousarray(np.asarray(L0_ffn_w0_kernel, np.float32)),
        "w1": np.ascontiguousarray(np.asarray(L0_ffn_w1_kernel, np.float32)),
        "wo": np.ascontiguousarray(np.asarray(L0_ffn_wo_kernel, np.float32)),
    }
    shared.update(_consts())
    nc = build_program()
    in_maps = [dict(shared, x=np.ascontiguousarray(x[i])) for i in range(8)]
    res = run_bass_kernel_spmd(nc, in_maps, core_ids=list(range(8)))
    return np.stack([np.asarray(r["y"], np.float32) for r in res.results], axis=0)


# version 03
# speedup vs baseline: 1.0502x; 1.0502x over previous
import numpy as np
import ml_dtypes
import concourse.bass as bass
import concourse.mybir as mybir
from concourse.bass_utils import run_bass_kernel_spmd
from contextlib import ExitStack

F32 = mybir.dt.float32
BF16 = mybir.dt.bfloat16
AF = mybir.ActivationFunctionType
OP = mybir.AluOpType
AX = mybir.AxisListType

PHASES = "PAF"
PSTAGE = 9
K5 = 3
N = 2048
E = 2048
H = 16
D = 128
NEXP = 8
MLP = 4096
NT = 16
NG = 4
GS = 512
HALF = 1024
NEG = -1.0e30
ENGINES = ("tensor", "vector", "scalar", "gpsimd", "sync")


class Op:
    __slots__ = ("engine", "fn", "deps", "dma", "ndma", "lane", "ev", "needed", "tag")

    def __init__(self, engine, fn, deps, dma, ndma, tag):
        self.engine = engine
        self.fn = fn
        self.deps = [d for d in deps if d is not None]
        self.dma = dma
        self.ndma = ndma
        self.lane = None
        self.ev = None
        self.needed = False
        self.tag = tag


class Prog:

    NDMA_SEMS = 16

    def __init__(self, nc, ctx):
        self.nc = nc
        self.ops = []
        self.sems = {}
        self.sem_vals = {}
        self.dma_idx = {}
        for e in ENGINES:
            name = f"s_{e}_c"
            self.sems[name] = ctx.enter_context(nc.semaphore(name))
            self.sem_vals[name] = 0
            self.dma_idx[e] = 0
            for i in range(self.NDMA_SEMS):
                name = f"s_{e}_d{i}"
                self.sems[name] = ctx.enter_context(nc.semaphore(name))
                self.sem_vals[name] = 0
        self.last = {e: None for e in ENGINES}
        self.last_compute = {e: None for e in ENGINES}
        self.lane_last = {e: [None] * self.NDMA_SEMS for e in ENGINES}

    def op(self, engine, fn, deps=(), dma=False, ndma=1, tag=""):
        o = Op(engine, fn, list(deps), dma, ndma, tag)
        if engine == "sync" and self.last[engine] is not None:
            o.deps.append(self.last[engine])
        elif engine in ("vector", "scalar") and not dma \
                and self.last_compute[engine] is not None:
            o.deps.append(self.last_compute[engine])
        if dma:
            o.lane = self.dma_idx[engine]
            self.dma_idx[engine] = (o.lane + 1) % self.NDMA_SEMS
            self.lane_last[engine][o.lane] = o
        else:
            self.last_compute[engine] = o
        self.ops.append(o)
        self.last[engine] = o
        return o

    def fence(self, engine):
        out = [o for o in self.lane_last[engine] if o is not None]
        if self.last_compute[engine] is not None:
            out.append(self.last_compute[engine])
        return out

    def fence_all(self):
        return [o for e in ENGINES for o in self.fence(e)]

    def emit_phase(self, block, start_idx):
        ops = self.ops[start_idx:]
        cur = set(id(o) for o in ops)
        for o in ops:
            for d in o.deps:
                if id(d) not in cur and d.ev is None:
                    raise RuntimeError(
                        f"dep {d.tag} of {o.tag} is from an already-emitted phase "
                        f"but was not pinned")
                d.needed = True
        for o in ops:
            if (o.needed or o.dma) and o.ev is None:
                if o.dma:
                    name = f"s_{o.engine}_d{o.lane}"
                    step = 16 * o.ndma
                else:
                    name = f"s_{o.engine}_c"
                    step = 1
                self.sem_vals[name] += step
                o.ev = (name, self.sem_vals[name])
        per_engine = {e: [] for e in ENGINES}
        for o in ops:
            per_engine[o.engine].append(o)
        for e in ENGINES:
            recs = per_engine[e]
            if not recs:
                continue
            sec = getattr(block, e)

            def section(eng, recs=recs):
                seen = {}
                for o in recs:
                    for d in o.deps:
                        if d.ev is None:
                            raise RuntimeError(f"dep of {o.tag} has no event: {d.tag}")
                        s, v = d.ev
                        if seen.get(s, 0) < v:
                            eng.wait_ge(self.sems[s], v)
                            seen[s] = v
                    if o.dma and o.ev is not None:
                        s, v = o.ev
                        prev = v - 16 * o.ndma
                        if prev > 0 and seen.get(s, 0) < prev:
                            eng.wait_ge(self.sems[s], prev)
                            seen[s] = prev
                    inst = o.fn(eng)
                    if o.ev is not None:
                        insts = inst if isinstance(inst, list) else [inst]
                        assert len(insts) == (o.ndma if o.dma else 1), o.tag
                        for i2 in insts:
                            i2.then_inc(self.sems[o.ev[0]], 16 if o.dma else 1)

            sec(section)
        return len(self.ops)


class Region:

    def __init__(self):
        self.w = None
        self.readers = []

    def wdeps(self):
        return ([self.w] if self.w is not None else []) + self.readers

    def on_write(self, op):
        self.w = op
        self.readers = []

    def rdeps(self):
        return [self.w] if self.w is not None else []

    def on_read(self, op):
        self.readers.append(op)


def rw(regions_read, regions_written, extra=()):
    deps = list(extra)
    for r in regions_read:
        deps += r.rdeps()
    for r in regions_written:
        deps += r.wdeps()
    return deps


def commit(op, regions_read, regions_written):
    for r in regions_read:
        r.on_read(op)
    for r in regions_written:
        r.on_write(op)
    return op


def build_program():
    nc = bass.Bass("TRN2", target_bir_lowering=False)

    x_d = nc.dram_tensor("x", [N, E], F32, kind="ExternalInput").ap()
    wq_d = nc.dram_tensor("Wq", [E, E], F32, kind="ExternalInput").ap()
    wkv_d = nc.dram_tensor("Wkv", [256, E], F32, kind="ExternalInput").ap()
    wkvh_d = nc.dram_tensor("Wkvh", [32, E], F32, kind="ExternalInput").ap()
    wg_d = nc.dram_tensor("Wg", [32, E], F32, kind="ExternalInput").ap()
    wout_d = nc.dram_tensor("Wout", [E, E], F32, kind="ExternalInput").ap()
    gate_d = nc.dram_tensor("gate", [E, NEXP], F32, kind="ExternalInput").ap()
    w0_d = nc.dram_tensor("w0", [NEXP, E, MLP], F32, kind="ExternalInput").ap()
    w1_d = nc.dram_tensor("w1", [NEXP, E, MLP], F32, kind="ExternalInput").ap()
    wo_d = nc.dram_tensor("wo", [NEXP, MLP, E], F32, kind="ExternalInput").ap()
    costk_d = nc.dram_tensor("cos_tok", [N, 64], BF16, kind="ExternalInput").ap()
    sintk_d = nc.dram_tensor("sin_tok", [N, 64], BF16, kind="ExternalInput").ap()
    cosT_d = nc.dram_tensor("cosT", [64, N], BF16, kind="ExternalInput").ap()
    sinT_d = nc.dram_tensor("sinT", [64, N], BF16, kind="ExternalInput").ap()
    mask4_d = nc.dram_tensor("mask4", [128, 2048], BF16, kind="ExternalInput").ap()
    maskT4_d = nc.dram_tensor("maskT4", [128, 2048], BF16, kind="ExternalInput").ap()
    d16_d = nc.dram_tensor("delta16", [16, 16 * 128], BF16, kind="ExternalInput").ap()
    d8_d = nc.dram_tensor("delta8", [8, 8 * 128], F32, kind="ExternalInput").ap()
    ones1_d = nc.dram_tensor("ones1", [1, 128], F32, kind="ExternalInput").ap()
    ident_d = nc.dram_tensor("ident", [128, 128], F32, kind="ExternalInput").ap()
    y_d = nc.dram_tensor("y", [N, E], F32, kind="ExternalOutput").ap()

    xnT_s = nc.dram_tensor("xnT_s", [16, 128, N], BF16, kind="Internal").ap()
    wqT_s = nc.dram_tensor("wqT_s", [16, 128, E], BF16, kind="Internal").ap()
    woutT_s = nc.dram_tensor("woutT_s", [16, 128, E], BF16, kind="Internal").ap()
    attn_s = nc.dram_tensor("attn_s", [N, E], F32, kind="Internal").ap()
    ffnT_s = nc.dram_tensor("ffnT_s", [E, N], F32, kind="Internal").ap()

    with ExitStack() as root:
        def sbt(ctx, name, shape, dt):
            return ctx.enter_context(nc.sbuf_tensor(name, shape, dt))

        PSB = [root.enter_context(nc.psum_tensor(f"psb{b}", [128, 512], F32))
               for b in range(8)]
        BK = [Region() for _ in range(8)]

        def pslice(c0, c1):
            b = c0 // 512
            assert (c1 - 1) // 512 == b, (c0, c1)
            return PSB[b][:, c0 - b * 512: c1 - b * 512]

        def banks(c0, c1):
            return [BK[b] for b in range(c0 // 512, (c1 + 511) // 512)]

        prog = Prog(nc, root)
        block = root.enter_context(nc.Block())
        emitted = 0

        d8_sb = sbt(root, "d8_sb", [8, 8 * 128], F32)
        ones1_sb = sbt(root, "ones1_sb", [1, 128], F32)
        ident_sb = sbt(root, "ident_sb", [128, 128], F32)
        combT_sb = sbt(root, "combT_sb", [8, NT * 128], F32)

        with ExitStack() as actx:
            xnT_sb = sbt(actx, "xnT_sb", [128, 16 * N], BF16)
            kpT_sb = sbt(actx, "kpT_sb", [128, NT * 128], BF16)
            vp_sb = sbt(actx, "vp_sb", [128, NT * 128], BF16)
            vhtok_sb = sbt(actx, "vhtok_sb", [128, NT * 16], BF16)
            vhtok32_sb = sbt(actx, "vhtok32_sb", [128, NT * 16], F32)
            khT_bf = sbt(actx, "khT_bf", [16, N], BF16)
            costk_sb = sbt(actx, "costk_sb", [128, NT * 64], BF16)
            sintk_sb = sbt(actx, "sintk_sb", [128, NT * 64], BF16)

            with ExitStack() as pctx:
                x_sb = [sbt(pctx, f"x_sb{i}", [128, E], F32) for i in range(2)]
                xntk_sb = [sbt(pctx, f"xntk{i}", [128, E], BF16) for i in range(2)]
                wld_sb = [sbt(pctx, f"wld{i}", [128, E], BF16) for i in range(2)]
                wstage_sb = [sbt(pctx, f"wstage{i}", [128, E], BF16) for i in range(2)]
                wkvT_sb = sbt(pctx, "wkvT_sb", [128, 16 * 256], BF16)
                wgT_sb = sbt(pctx, "wgT_sb", [128, 16 * 32], BF16)
                wkvhT_sb = sbt(pctx, "wkvhT_sb", [128, 16 * 32], BF16)
                gate_sb = sbt(pctx, "gate_sb", [128, 16 * 8], BF16)
                sumsq_sb = sbt(pctx, "sumsq_sb", [128, 16], F32)
                mean_sb = sbt(pctx, "mean_sb", [128, 1], F32)
                var_sb = sbt(pctx, "var_sb", [128, 1], F32)
                rstd_sb = sbt(pctx, "rstd_sb", [128, 1], F32)
                sq_sb = sbt(pctx, "sq_sb", [128, E], BF16)
                G = {}
                for nm in ("pa", "pb", "lgs", "Fx", "tmpa", "tmpb"):
                    G[nm] = sbt(pctx, "G_" + nm, [16, N], F32)
                vhT_bf = sbt(pctx, "vhT_bf", [16, N], BF16)
                kprot_sb = [sbt(pctx, f"kprot{i}", [128, 128], BF16) for i in range(2)]
                ktmp = [sbt(pctx, f"ktmp{i}", [128, 64], F32) for i in range(2)]
                eq1_sb = sbt(pctx, "eq1_sb", [128, NT * 8], F32)
                eq2_sb = sbt(pctx, "eq2_sb", [128, NT * 8], F32)
                comb_sb = sbt(pctx, "comb_sb", [128, NT * 8], F32)
                dcol_sb = sbt(pctx, "dcol_sb", [128, 16], F32)
                w1c_sb = sbt(pctx, "w1c_sb", [128, 16], F32)
                w2c_sb = sbt(pctx, "w2c_sb", [128, 16], F32)
                m1_sb = sbt(pctx, "m1_sb", [128, 1], F32)
                m2_sb = sbt(pctx, "m2_sb", [128, 1], F32)
                lrow_sb = sbt(pctx, "lrow_sb", [1, 512], F32)

                RX = [Region() for _ in range(2)]
                RXN = [Region() for _ in range(2)]
                RWL = [Region() for _ in range(2)]
                RWS = [Region() for _ in range(2)]
                RKP = [Region() for _ in range(2)]
                Rstats = Region()
                Rxnt = Region()
                Rscr = Region()

                const_ops = []

                def ld(dst, src):
                    o = prog.op("scalar", lambda e, d=dst, s=src: e.dma_start(out=d, in_=s),
                                dma=True, tag="cld")
                    const_ops.append(o)
                    return o
                ld(d8_sb[:], d8_d)
                ld(ones1_sb[:], ones1_d)
                ld(ident_sb[:], ident_d)
                ld(costk_sb[:].rearrange("p (c j) -> p c j", c=NT),
                   costk_d.rearrange("(c p) j -> p c j", p=128))
                ld(sintk_sb[:].rearrange("p (c j) -> p c j", c=NT),
                   sintk_d.rearrange("(c p) j -> p c j", p=128))
                op_consts = const_ops

                if PSTAGE >= 2:
                    wq_st = {}
                    wout_st_by_h = {h: [] for h in range(16)}
                    op_wout_tp = []
                    for c in range(16):
                        lop = prog.op("gpsimd",
                                      lambda e, c=c, b=c % 2: e.dma_start(
                                          out=wld_sb[b][:], in_=wq_d[c * 128:(c + 1) * 128, :]),
                                      deps=rw([], [RWL[c % 2]]), dma=True, tag=f"wqld{c}")
                        commit(lop, [], [RWL[c % 2]])
                        top = prog.op("sync",
                                      lambda e, b=c % 2: e.dma_start_transpose(
                                          wstage_sb[b][:].rearrange("p (k d) -> p k d", k=16),
                                          wld_sb[b][:]),
                                      deps=rw([RWL[c % 2]], [RWS[c % 2]]), dma=True, tag=f"wqtp{c}")
                        commit(top, [RWL[c % 2]], [RWS[c % 2]])
                        sop = prog.op("scalar",
                                      lambda e, c=c, b=c % 2: e.dma_start(
                                          out=wqT_s[c], in_=wstage_sb[b][:]),
                                      deps=rw([RWS[c % 2]], []), dma=True, tag=f"wqst{c}")
                        commit(sop, [RWS[c % 2]], [])
                        wq_st[c] = sop
                    for c in range(16):
                        lop = prog.op("gpsimd",
                                      lambda e, c=c, b=c % 2: e.dma_start(
                                          out=wld_sb[b][:], in_=wout_d[c * 128:(c + 1) * 128, :]),
                                      deps=rw([], [RWL[c % 2]]), dma=True, tag=f"wold{c}")
                        commit(lop, [], [RWL[c % 2]])
                        top = prog.op("sync",
                                      lambda e, b=c % 2: e.dma_start_transpose(
                                          wstage_sb[b][:].rearrange("p (k d) -> p k d", k=16),
                                          wld_sb[b][:]),
                                      deps=rw([RWL[c % 2]], [RWS[c % 2]]), dma=True, tag=f"wotp{c}")
                        commit(top, [RWL[c % 2]], [RWS[c % 2]])
                        op_wout_tp.append(top)
                        for h in range(16):
                            sop = prog.op("scalar",
                                          lambda e, c=c, h=h, b=c % 2: e.dma_start(
                                              out=woutT_s[h][:, c * 128:(c + 1) * 128],
                                              in_=wstage_sb[b][:, h * 128:(h + 1) * 128]),
                                          deps=rw([RWS[c % 2]], []), dma=True, tag=f"wost{c}_{h}")
                            commit(sop, [RWS[c % 2]], [])
                            wout_st_by_h[h].append(sop)
                if PSTAGE >= 3:
                    kv_ld = []
                    for c in range(2):
                        lop = prog.op("gpsimd",
                                      lambda e, c=c, b=c % 2: e.dma_start(
                                          out=wld_sb[b][:], in_=wkv_d[c * 128:(c + 1) * 128, :]),
                                      deps=rw([], [RWL[c % 2]]), dma=True, tag=f"kvld{c}")
                        commit(lop, [], [RWL[c % 2]])
                        top = prog.op("sync",
                                      lambda e, c=c, b=c % 2: e.dma_start_transpose(
                                          wkvT_sb[:].rearrange("p (k n) -> p k n", k=16)
                                          [:, :, c * 128:(c + 1) * 128],
                                          wld_sb[b][:]),
                                      deps=rw([RWL[c % 2]], []), dma=True, tag=f"kvtp{c}")
                        commit(top, [RWL[c % 2]], [])
                        kv_ld.append(top)
                    g32 = prog.op("gpsimd", lambda e: e.dma_start(out=wld_sb[0][0:32, :], in_=wg_d),
                                  deps=rw([], [RWL[0]]), dma=True, tag="g32")
                    commit(g32, [], [RWL[0]])
                    kvh32 = prog.op("gpsimd",
                                    lambda e: e.dma_start(out=wld_sb[1][0:32, :], in_=wkvh_d),
                                    deps=rw([], [RWL[1]]), dma=True, tag="kvh32")
                    commit(kvh32, [], [RWL[1]])
                    gtld = prog.op("gpsimd",
                                   lambda e: e.dma_start(
                                       out=gate_sb[:].rearrange("p (c k) -> p c k", c=16),
                                       in_=gate_d.rearrange("(c p) k -> p c k", p=128)),
                                   dma=True, tag="gtld")
                    wgtp = prog.op("sync",
                                   lambda e: e.dma_start_transpose(
                                       wgT_sb[:].rearrange("p (c k) -> p c k", c=16),
                                       wld_sb[0][0:32, :]),
                                   deps=[g32], dma=True, tag="wgtp")
                    commit(wgtp, [RWL[0]], [])
                    kvhtp = prog.op("sync",
                                    lambda e: e.dma_start_transpose(
                                        wkvhT_sb[:].rearrange("p (c k) -> p c k", c=16),
                                        wld_sb[1][0:32, :]),
                                    deps=[kvh32], dma=True, tag="kvhtp")
                    commit(kvhtp, [RWL[1]], [])

                if PSTAGE >= 4:
                    xn_tp = []
                    for t in range(NT):
                        lop = prog.op("scalar",
                                      lambda e, t=t, b=t % 2: e.dma_start(
                                          out=x_sb[b][:], in_=x_d[t * 128:(t + 1) * 128, :]),
                                      deps=rw([], [RX[t % 2]]), dma=True, tag=f"xld{t}")
                        commit(lop, [], [RX[t % 2]])
                        sqo = prog.op("scalar",
                                      lambda e, t=t, b=t % 2: e.activation(
                                          sq_sb[:], x_sb[b][:], AF.Square,
                                          accum_out=sumsq_sb[:, t:t + 1]),
                                      deps=rw([RX[t % 2]], []), tag=f"sq{t}")
                        commit(sqo, [RX[t % 2]], [])
                        vst = prog.op("vector",
                                      lambda e, b=t % 2: e.reduce_sum(mean_sb[:], x_sb[b][:], axis=AX.X),
                                      deps=rw([RX[t % 2]], [Rstats]), tag=f"vm{t}")
                        commit(vst, [RX[t % 2]], [Rstats])
                        prog.op("vector", lambda e: e.tensor_scalar(
                            mean_sb[:], mean_sb[:], 1.0 / E, None, OP.mult, OP.bypass), tag="vm2")
                        prog.op("vector", lambda e: e.tensor_tensor(
                            var_sb[:], mean_sb[:], mean_sb[:], OP.mult), tag="vm3")
                        vvar = prog.op("vector",
                                       lambda e, t=t: e.scalar_tensor_tensor(
                                           var_sb[:], sumsq_sb[:, t:t + 1], 1.0 / E, var_sb[:],
                                           OP.mult, OP.subtract),
                                       deps=[sqo], tag=f"vvar{t}")
                        vve = prog.op("vector", lambda e: e.tensor_scalar(
                            var_sb[:], var_sb[:], 1e-5, None, OP.add, OP.bypass), tag="veps")
                        sln = prog.op("scalar", lambda e: e.activation(rstd_sb[:], var_sb[:], AF.Ln),
                                      deps=[vve], tag=f"sln{t}")
                        sex = prog.op("scalar", lambda e: e.activation(
                            rstd_sb[:], rstd_sb[:], AF.Exp, scale=-0.5), tag=f"sex{t}")
                        vxn = prog.op("vector",
                                      lambda e, b=t % 2: e.tensor_scalar(
                                          xntk_sb[b][:], x_sb[b][:], mean_sb[:, 0:1],
                                          rstd_sb[:, 0:1], OP.subtract, OP.mult),
                                      deps=rw([RX[t % 2], Rstats], [RXN[t % 2]], extra=[sex]),
                                      tag=f"vxn{t}")
                        commit(vxn, [RX[t % 2], Rstats], [RXN[t % 2]])
                        commit(vxn, [], [RX[t % 2]])
                        top = prog.op("sync",
                                      lambda e, t=t, b=t % 2: e.dma_start_transpose(
                                          xnT_sb[:].rearrange("p (c n) -> p c n", c=16)
                                          [:, :, t * 128:(t + 1) * 128],
                                          xntk_sb[b][:]),
                                      deps=rw([RXN[t % 2]], [Rxnt]), dma=True, tag=f"xntp{t}")
                        commit(top, [RXN[t % 2]], [])
                        xn_tp.append(top)
                    for o in prog.fence("sync"):
                        Rxnt.on_write(o) if False else None
                    Rxnt.w = None
                    Rxnt.readers = []
                    xn_tp_fence = prog.fence("sync")

                    def xnT_store(e):
                        return [e.dma_start(
                            out=xnT_s[q * 8:(q + 1) * 8].rearrange("c p n -> p c n"),
                            in_=xnT_sb[:].rearrange("p (c n) -> p c n", c=16)
                            [:, q * 8:(q + 1) * 8, :]) for q in range(2)]
                    xns = prog.op("scalar", xnT_store, deps=xn_tp_fence, dma=True, ndma=2,
                                  tag="xnst")

                if PSTAGE >= 5:
                    dep_xnt = xn_tp_fence
                    kv_done = []
                    for t in range(NT):
                        def kvmm(e, t=t):
                            for ec in range(16):
                                inst = e.matmul(pslice(0, 256),
                                                xnT_sb[:, ec * N + t * 128: ec * N + t * 128 + 128],
                                                wkvT_sb[:, ec * 256:(ec + 1) * 256],
                                                start=(ec == 0), stop=(ec == 15))
                            return inst
                        mm = prog.op("tensor", kvmm,
                                     deps=rw([], [BK[0]], extra=dep_xnt + [kv_ld[1]] + op_consts),
                                     tag=f"kv{t}")
                        commit(mm, [], [BK[0]])
                        b = t % 2

                        if K5 == 1:
                            rp = prog.op("vector",
                                         lambda e, b=b: e.tensor_copy(
                                             kprot_sb[b][:], pslice(0, 128)),
                                         deps=rw([BK[0]], [RKP[b]], extra=[mm]),
                                         tag=f"ropek{t}cp")
                            commit(rp, [BK[0]], [RKP[b]])
                        elif K5 in (0, 4):
                            rp = mm
                        elif K5 == 5:
                            rp = prog.op("vector",
                                         lambda e, b=b: e.tensor_copy(
                                             kprot_sb[b][:], pslice(0, 128)),
                                         deps=rw([BK[0]], [RKP[b]], extra=[mm]),
                                         tag=f"ropek{t}cp")
                            commit(rp, [BK[0]], [RKP[b]])
                        if K5 >= 2:
                         prog.op("vector", lambda e, t=t: e.tensor_tensor(
                            ktmp[0][:], pslice(0, 64), costk_sb[:, t * 64:(t + 1) * 64], OP.mult),
                            deps=rw([BK[0]], [RKP[b]], extra=[mm]), tag=f"ropek{t}a")
                         prog.op("vector", lambda e, t=t: e.tensor_tensor(
                            ktmp[1][:], pslice(64, 128), sintk_sb[:, t * 64:(t + 1) * 64], OP.mult),
                            tag=f"ropek{t}b")
                         prog.op("vector", lambda e, b=b: e.tensor_tensor(
                            kprot_sb[b][:, 0:64], ktmp[0][:], ktmp[1][:], OP.subtract),
                            tag=f"ropek{t}c")
                         prog.op("vector", lambda e, t=t: e.tensor_tensor(
                            ktmp[0][:], pslice(64, 128), costk_sb[:, t * 64:(t + 1) * 64], OP.mult),
                            tag=f"ropek{t}d")
                         prog.op("vector", lambda e, t=t: e.tensor_tensor(
                            ktmp[1][:], pslice(0, 64), sintk_sb[:, t * 64:(t + 1) * 64], OP.mult),
                            tag=f"ropek{t}e")
                         rp = prog.op("vector", lambda e, b=b: e.tensor_tensor(
                            kprot_sb[b][:, 64:128], ktmp[0][:], ktmp[1][:], OP.add),
                            tag=f"ropek{t}f")
                         commit(rp, [BK[0]], [RKP[b]])
                        vpo = prog.op("vector",
                                      lambda e, t=t: e.tensor_copy(vp_sb[:, t * 128:(t + 1) * 128],
                                                                   pslice(128, 256)),
                                      deps=[mm], tag=f"vp{t}")
                        commit(vpo, [BK[0]], [])
                        tp = prog.op("sync",
                                     lambda e, t=t, b=b: e.dma_start_transpose(
                                         kpT_sb[:, t * 128:(t + 1) * 128], kprot_sb[b][:]),
                                     deps=rw([RKP[b]], []), dma=True, tag=f"kptp{t}")
                        commit(tp, [RKP[b]], [])
                        kv_done.append(tp)
                    kpT_ready = kv_done[-1]

                if PSTAGE >= 6:
                    comb_parts = []
                    for t in range(NT):
                        def gmm(e, t=t):
                            for ec in range(16):
                                inst = e.matmul(pslice(2560, 2568),
                                                xnT_sb[:, ec * N + t * 128: ec * N + t * 128 + 128],
                                                gate_sb[:, ec * 8:(ec + 1) * 8],
                                                start=(ec == 0), stop=(ec == 15))
                            return inst
                        mm = prog.op("tensor", gmm,
                                     deps=rw([], [BK[5]], extra=dep_xnt + [gtld]), tag=f"g8{t}")
                        commit(mm, [], [BK[5]])

                        prog.op("vector", lambda e: e.reduce_max(
                            m1_sb[:], pslice(2560, 2568), axis=AX.X), deps=[mm], tag=f"gv{t}a")
                        prog.op("vector", lambda e, t=t: e.tensor_scalar(
                            eq1_sb[:, t * 8:(t + 1) * 8], pslice(2560, 2568),
                            m1_sb[:, 0:1], None, OP.is_equal, OP.bypass), tag=f"gv{t}b")
                        prog.op("vector", lambda e, t=t: e.scalar_tensor_tensor(
                            comb_sb[:, t * 8:(t + 1) * 8], eq1_sb[:, t * 8:(t + 1) * 8], NEG,
                            pslice(2560, 2568), OP.mult, OP.add), tag=f"gv{t}c")
                        prog.op("vector", lambda e, t=t: e.reduce_max(
                            m2_sb[:], comb_sb[:, t * 8:(t + 1) * 8], axis=AX.X), tag=f"gv{t}d")
                        prog.op("vector", lambda e, t=t: e.tensor_scalar(
                            eq2_sb[:, t * 8:(t + 1) * 8], pslice(2560, 2568),
                            m2_sb[:, 0:1], None, OP.is_equal, OP.bypass), tag=f"gv{t}e")
                        gv = prog.op("vector", lambda e, t=t: e.tensor_tensor(
                            dcol_sb[:, t:t + 1], m1_sb[:], m2_sb[:], OP.subtract), tag=f"gv{t}f")
                        commit(gv, [BK[5]], [])
                        comb_parts.append(gv)
                    sig = prog.op("scalar", lambda e: e.activation(w1c_sb[:], dcol_sb[:], AF.Sigmoid),
                                  deps=[comb_parts[-1]], tag="sig")
                    vw2 = prog.op("vector", lambda e: e.tensor_scalar(
                        w2c_sb[:], w1c_sb[:], -1.0, 1.0, OP.mult, OP.add), deps=[sig], tag="vw2")
                    comb_done = []
                    for t in range(NT):
                        prog.op("vector",
                                lambda e, t=t: e.tensor_scalar(
                                    eq1_sb[:, t * 8:(t + 1) * 8], eq1_sb[:, t * 8:(t + 1) * 8],
                                    w1c_sb[:, t:t + 1], None, OP.mult, OP.bypass),
                                deps=[vw2], tag=f"c1{t}")
                        cop = prog.op("vector",
                                      lambda e, t=t: e.scalar_tensor_tensor(
                                          comb_sb[:, t * 8:(t + 1) * 8], eq2_sb[:, t * 8:(t + 1) * 8],
                                          w2c_sb[:, t:t + 1], eq1_sb[:, t * 8:(t + 1) * 8],
                                          OP.mult, OP.add),
                                      tag=f"c2{t}")
                        comb_done.append(cop)
                    ct_done = []
                    for t in range(NT):
                        tp = prog.op("tensor",
                                     lambda e, t=t: e.matmul(PSB[6][0:8, 0:128],
                                                             comb_sb[:, t * 8:(t + 1) * 8],
                                                             ident_sb[:], start=True, stop=True,
                                                             is_transpose=True),
                                     deps=rw([], [BK[6]], extra=[comb_done[t]] + op_consts),
                                     tag=f"ctT{t}")
                        commit(tp, [], [BK[6]])
                        cp = prog.op("vector",
                                     lambda e, t=t: e.tensor_copy(combT_sb[:, t * 128:(t + 1) * 128],
                                                                  PSB[6][0:8, 0:128]),
                                     deps=[tp], tag=f"ctC{t}")
                        commit(cp, [BK[6]], [])
                        ct_done.append(cp)

                if PSTAGE >= 7:
                    kh_ready = vh_ready = None
                    prev_chain_last = None
                    prev_ln1 = None
                    prev_vdc = None
                    for gi, outbf in enumerate([khT_bf, vhT_bf]):
                        off = 0 if gi == 0 else 16
                        evs = {}
                        for nm, wT in (("pa", wgT_sb), ("pb", wkvhT_sb)):
                            def smm(e, wT=wT, off=off):
                                for cc in range(4):
                                    for ec in range(16):
                                        inst = e.matmul(
                                            PSB[1 + cc][0:16, 0:512],
                                            wT[:, ec * 32 + off: ec * 32 + off + 16],
                                            xnT_sb[:, ec * N + cc * 512: ec * N + cc * 512 + 512],
                                            start=(ec == 0), stop=(ec == 15))
                                return inst
                            extra = dep_xnt + [wgtp, kvhtp]
                            if prev_chain_last is not None:
                                extra = extra + [prev_chain_last, prev_ln1, prev_vdc]
                            mm = prog.op("tensor", smm,
                                         deps=rw([], banks(512, 2560), extra=extra),
                                         tag=f"sm{gi}{nm}")
                            for b in banks(512, 2560):
                                b.on_write(mm)
                            ev = None
                            for cc in range(4):
                                ev = prog.op("vector",
                                             lambda e, nm=nm, cc=cc: e.tensor_copy(
                                                 G[nm][:, cc * 512:(cc + 1) * 512],
                                                 PSB[1 + cc][0:16, 0:512]),
                                             deps=([mm] if cc == 0 else []),
                                             tag=f"sme{gi}{nm}{cc}")
                                BK[1 + cc].on_read(ev)
                            evs[nm] = ev
                        small_evac = {"pa": evs["pa"], "pb": evs["pb"]}
                        gnm, pre = "pa", "pb"
                        sab = prog.op("scalar",
                                      lambda e, gnm=gnm: e.activation(G["tmpa"][:], G[gnm][:], AF.Abs),
                                      deps=[small_evac[gnm]] +
                                           ([prev_chain_last] if prev_chain_last else []),
                                      tag=f"abs{gi}")
                        prog.op("scalar", lambda e: e.activation(G["tmpa"][:], G["tmpa"][:],
                                                                 AF.Exp, scale=-1.0), tag=f"ex{gi}")
                        ln1 = prog.op("scalar", lambda e: e.activation(G["tmpa"][:], G["tmpa"][:],
                                                                       AF.Ln, bias=1.0), tag=f"ln{gi}")
                        vmin = prog.op("vector",
                                       lambda e, gnm=gnm: e.tensor_scalar(
                                           G["lgs"][:], G[gnm][:], 0.0, None, OP.min, OP.bypass),
                                       deps=[small_evac[gnm]] +
                                            ([prev_chain_last] if prev_chain_last else []),
                                       tag=f"mn{gi}")
                        vsub = prog.op("vector", lambda e: e.tensor_tensor(
                            G["lgs"][:], G["lgs"][:], G["tmpa"][:], OP.subtract),
                            deps=[ln1], tag=f"ls{gi}")
                        prog.op("vector", lambda e: e.memset(G["tmpb"][:, 0:1], 0.0), tag=f"z{gi}")
                        prog.op("vector", lambda e: e.tensor_copy(G["tmpb"][:, 1:N],
                                                                  G["lgs"][:, 0:N - 1]), tag=f"sh{gi}")
                        fsc = prog.op("vector", lambda e: e.tensor_tensor_scan(
                            G["Fx"][:], G["tmpb"][:], G["tmpb"][:], 0.0, OP.add, OP.bypass),
                            tag=f"fs{gi}")
                        sef = prog.op("scalar", lambda e: e.activation(G["tmpa"][:], G["Fx"][:], AF.Exp),
                                      deps=[fsc, vsub], tag=f"ef{gi}")
                        vdc = prog.op("vector",
                                      lambda e, pre=pre: e.tensor_tensor(
                                          G["tmpb"][:], G[pre][:], G["tmpa"][:], OP.mult),
                                      deps=[sef, small_evac[pre]], tag=f"dc{gi}")
                        vcs = prog.op("vector",
                                      lambda e: e.tensor_tensor_scan(
                                          G["lgs"][:], G["tmpb"][:], G["tmpb"][:], 0.0,
                                          OP.add, OP.bypass),
                                      tag=f"cs{gi}")
                        if gi == 0:
                            cpy = prog.op("vector",
                                          lambda e, outbf=outbf: e.tensor_scalar(
                                              outbf[:], G["lgs"][:], float(1.0 / np.sqrt(D)),
                                              None, OP.mult, OP.bypass),
                                          tag=f"obf{gi}")
                        else:
                            cpy = prog.op("vector",
                                          lambda e, outbf=outbf: e.tensor_copy(outbf[:], G["lgs"][:]),
                                          tag=f"obf{gi}")
                        if gi == 0:
                            kh_ready = cpy
                        else:
                            vh_ready = cpy
                        prev_chain_last = cpy
                        prev_ln1 = ln1
                        prev_vdc = vdc
                    vh_tp = prog.op("sync",
                                    lambda e: e.dma_start_transpose(
                                        vhtok_sb[:].rearrange("p (c j) -> p c j", c=NT), vhT_bf[:]),
                                    deps=[vh_ready], dma=True, tag="vhtp")
                    vh_tp = prog.op("vector",
                                    lambda e: e.tensor_copy(vhtok32_sb[:], vhtok_sb[:]),
                                    deps=[vh_tp], tag="vhtp32")

                if PSTAGE < 7:
                    kh_ready = kpT_ready = vh_tp = None
                    if PSTAGE < 5:
                        kpT_ready = None
                p_last = prog.fence_all()
                for o in (p_last + [kh_ready, kpT_ready, vh_tp] if PSTAGE >= 7 else p_last):
                    o.needed = True
                emitted = prog.emit_phase(block, emitted)
                for b in BK:
                    b.w = None
                    b.readers = []

            if "A" not in PHASES:
                return nc
            with ExitStack() as awctx:
                cosT_sb = sbt(awctx, "cosT_sb", [64, N], BF16)
                sinT_sb = sbt(awctx, "sinT_sb", [64, N], BF16)
                mask4_sb = sbt(awctx, "mask4_sb", [128, 2048], BF16)
                maskT4_sb = sbt(awctx, "maskT4_sb", [128, 2048], BF16)
                d16_sb = sbt(awctx, "d16_sb", [16, 16 * 128], BF16)
                wqh_sb = [sbt(awctx, f"wqh{i}", [128, E], BF16) for i in range(2)]
                woh_sb = [sbt(awctx, f"woh{i}", [128, HALF], BF16) for i in range(2)]
                qh_sb = [sbt(awctx, f"qh{i}", [128, GS], BF16) for i in range(2)]
                khK_sb = [sbt(awctx, f"khK{i}", [128, N], BF16) for i in range(2)]
                vh_sb = [sbt(awctx, f"vhb{i}", [128, NT * 128], BF16) for i in range(2)]
                pt_sb = [sbt(awctx, f"ptb{i}", [128, GS], BF16) for i in range(2)]
                stmp_sb = [sbt(awctx, f"stmp{i}", [128, GS], F32) for i in range(2)]
                oall_sb = sbt(awctx, "oall_sb", [128, H * GS], BF16)
                trash_sb = sbt(awctx, "trash_sb", [128, 2048], BF16)
                axb_sb = [sbt(awctx, f"axb{i}", [128, HALF], F32) for i in range(2)]
                negmx_sb = sbt(awctx, "negmx_sb", [128, 4], F32)
                nmx4_sb = sbt(awctx, "nmx4_sb", [128, 4], F32)
                sstage_sb = sbt(awctx, "sstage_sb", [128, 2048], F32)
                sume4_sb = sbt(awctx, "sume4_sb", [128, 4], F32)
                sume_sb = sbt(awctx, "sume_sb", [128, 4], F32)
                lsum_sb = sbt(awctx, "lsum_sb", [128, 4], F32)
                lrow2_sb = sbt(awctx, "lrow2_sb", [1, 512], F32)
                lsb_sb = sbt(awctx, "lsb_sb", [128, 512], F32)

                RWQ = [Region() for _ in range(2)]
                RWOH = [Region() for _ in range(2)]
                RQH = [Region() for _ in range(2)]
                RKH = [Region() for _ in range(2)]
                RVH = [Region() for _ in range(2)]
                RPT = [Region() for _ in range(2)]
                RST = [Region() for _ in range(2)]
                ROA = [Region() for _ in range(H)]
                RAXB = [Region() for _ in range(2)]
                RLS = Region()
                dep_P = p_last

                aconst_ops = []

                def lda(dst, src):
                    o = prog.op("scalar",
                                lambda e, d=dst, s=src: e.dma_start(out=d, in_=s),
                                deps=dep_P, dma=True, tag="alda")
                    aconst_ops.append(o)
                    return o
                lda(cosT_sb[:], cosT_d)
                lda(sinT_sb[:], sinT_d)
                lda(mask4_sb[:], mask4_d)
                lda(maskT4_sb[:], maskT4_d)
                lda(d16_sb[:], d16_d)

                for g in range(NG):
                    mend = GS * (g + 1)
                    nch = g + 1
                    for h in range(H):
                        ih = g * H + h
                        bwq = ih % 2
                        wql = prog.op("scalar",
                                      lambda e, h=h, b=bwq: e.dma_start(out=wqh_sb[b][:],
                                                                        in_=wqT_s[h]),
                                      deps=rw([], [RWQ[bwq]],
                                              extra=[wq_st[h]] +
                                                    (dep_P if ih == 0 else [])),
                                      dma=True, tag=f"wql{g}_{h}")
                        commit(wql, [], [RWQ[bwq]])

                        def qmm(e, g=g, b=bwq):
                            for ec in range(16):
                                inst = e.matmul(pslice(0, GS),
                                                wqh_sb[b][:, ec * 128:(ec + 1) * 128],
                                                xnT_sb[:, ec * N + g * GS: ec * N + g * GS + GS],
                                                start=(ec == 0), stop=(ec == 15))
                            return inst
                        qp = prog.op("tensor", qmm,
                                     deps=rw([RWQ[bwq]], [BK[0]],
                                             extra=(dep_P if ih == 0 else [])),
                                     tag=f"qp{g}_{h}")
                        commit(qp, [RWQ[bwq]], [BK[0]])

                        bq = ih % 2

                        prog.op("vector", lambda e, g=g: e.tensor_tensor(
                            stmp_sb[0][0:64, :], PSB[0][0:64, 0:GS],
                            cosT_sb[:, g * GS:(g + 1) * GS], OP.mult),
                            deps=rw([BK[0]], [RQH[bq], RST[0], RST[1]],
                                    extra=[qp] + (aconst_ops if ih == 0 else [])),
                            tag=f"rq{g}_{h}a")
                        prog.op("vector", lambda e, g=g: e.tensor_tensor(
                            stmp_sb[1][0:64, :], PSB[0][64:128, 0:GS],
                            sinT_sb[:, g * GS:(g + 1) * GS], OP.mult), tag=f"rq{g}_{h}b")
                        prog.op("vector", lambda e, b=bq: e.tensor_tensor(
                            qh_sb[b][0:64, :], stmp_sb[0][0:64, :], stmp_sb[1][0:64, :],
                            OP.subtract), tag=f"rq{g}_{h}c")
                        prog.op("vector", lambda e, g=g: e.tensor_tensor(
                            stmp_sb[0][0:64, :], PSB[0][64:128, 0:GS],
                            cosT_sb[:, g * GS:(g + 1) * GS], OP.mult), tag=f"rq{g}_{h}d")
                        prog.op("vector", lambda e, g=g: e.tensor_tensor(
                            stmp_sb[1][0:64, :], PSB[0][0:64, 0:GS],
                            sinT_sb[:, g * GS:(g + 1) * GS], OP.mult), tag=f"rq{g}_{h}e")
                        rq = prog.op("vector", lambda e, b=bq: e.tensor_tensor(
                            qh_sb[b][64:128, :], stmp_sb[0][0:64, :], stmp_sb[1][0:64, :],
                            OP.add), tag=f"rq{g}_{h}f")
                        commit(rq, [BK[0]], [RQH[bq]])
                        commit(rq, [], [RST[0]])
                        commit(rq, [], [RST[1]])

                        def khb2(e, nch=nch, h=h):
                            for c in range(nch):
                                inst = e.matmul(pslice(512 + c * 512, 512 + (c + 1) * 512),
                                                d16_sb[:, h * 128:(h + 1) * 128],
                                                khT_bf[:, c * 512:(c + 1) * 512],
                                                start=True, stop=True)
                            return inst
                        kb = prog.op("tensor", khb2,
                                     deps=rw([], banks(512, 512 + mend),
                                             extra=([kh_ready] + aconst_ops if ih == 0 else [])),
                                     tag=f"kb{g}_{h}")
                        for b in banks(512, 512 + mend):
                            b.on_write(kb)
                        bk2 = ih % 2
                        vkh = None
                        for c in range(nch):
                            vkh = prog.op("vector",
                                          lambda e, c=c, b=bk2: e.tensor_tensor(
                                              khK_sb[b][:, c * 512:(c + 1) * 512],
                                              kpT_sb[:, c * 512:(c + 1) * 512],
                                              PSB[1 + c][:, 0:512], OP.mult),
                                          deps=(rw([RKH[bk2]], [],
                                                extra=[kb] + ([kpT_ready] if ih == 0 else []))
                                                if c == 0 else [kb]),
                                          tag=f"vkh{g}_{h}_{c}")
                            BK[1 + c].on_read(vkh)
                        commit(vkh, [], [RKH[bk2]])

                        def vhmul(e, g=g, h=h, b=bk2):
                            for mt in range(4 * (g + 1)):
                                inst = e.tensor_scalar(
                                    vh_sb[b][:, mt * 128:(mt + 1) * 128],
                                    vp_sb[:, mt * 128:(mt + 1) * 128],
                                    vhtok32_sb[:, mt * 16 + h: mt * 16 + h + 1], None,
                                    OP.mult, OP.bypass)
                            return inst
                        vvh = prog.op("vector", vhmul,
                                      deps=rw([RVH[bk2]], [],
                                              extra=([vh_tp] if ih == 0 else [])),
                                      tag=f"vvh{g}_{h}")
                        commit(vvh, [], [RVH[bk2]])

                        exp_ops = []
                        for t in range(4):
                            def smm(e, t=t, nch=nch, b=bq, bk=bk2):
                                for c in range(nch):
                                    inst = e.matmul(pslice(512 + c * 512, 512 + (c + 1) * 512),
                                                    qh_sb[b][:, t * 128:(t + 1) * 128],
                                                    khK_sb[bk][:, c * 512:(c + 1) * 512],
                                                    start=True, stop=True)
                                return inst
                            s1 = prog.op("tensor", smm,
                                         deps=rw([RQH[bq], RKH[bk2]], banks(512, 512 + mend),
                                                 extra=[rq, vkh]),
                                         tag=f"s1_{g}_{h}_{t}")
                            for b in banks(512, 512 + mend):
                                b.on_write(s1)
                            commit(s1, [RQH[bq]], [])
                            commit(s1, [RKH[bk2]], [])

                            prog.op("vector", lambda e, t=t, g=g: e.tensor_tensor(
                                PSB[1 + g][:, 0:512], PSB[1 + g][:, 0:512],
                                mask4_sb[:, t * 512:(t + 1) * 512], OP.add),
                                deps=[s1] + (aconst_ops if ih == 0 and t == 0 else []),
                                tag=f"vmA{g}_{h}_{t}")
                            for c in range(nch):
                                vm = prog.op("vector",
                                             lambda e, c=c: e.reduce_max(
                                                 nmx4_sb[:, c:c + 1], PSB[1 + c][:, 0:512],
                                                 axis=AX.X),
                                             deps=([s1] if c > 0 else []),
                                             tag=f"vmc{g}_{h}_{t}_{c}")
                                BK[1 + c].on_read(vm)
                                BK[1 + g].on_write(vm) if False else None
                            vm = prog.op("vector", lambda e, t=t, nch=nch: e.reduce_max(
                                negmx_sb[:, t:t + 1], nmx4_sb[:, 0:nch], axis=AX.X,
                                negate=True), tag=f"vm{g}_{h}_{t}")
                            BK[1 + g].on_write(vm)
                            se = None
                            for c in range(nch):
                                vcp = prog.op("vector",
                                              lambda e, c=c: e.tensor_copy(
                                                  sstage_sb[:, c * 512:(c + 1) * 512],
                                                  PSB[1 + c][:, 0:512]),
                                              deps=[vm], tag=f"sec{g}_{h}_{t}_{c}")
                                BK[1 + c].on_read(vcp)
                                se = prog.op("scalar",
                                             lambda e, t=t, c=c: e.activation(
                                                 trash_sb[:, c * 512:(c + 1) * 512],
                                                 sstage_sb[:, c * 512:(c + 1) * 512],
                                                 AF.Exp, bias=negmx_sb[:, t:t + 1],
                                                 accum_out=sume4_sb[:, c:c + 1]),
                                             deps=[vcp], tag=f"se{g}_{h}_{t}_{c}")
                            vsm = prog.op("vector",
                                          lambda e, t=t, nch=nch: e.reduce_sum(
                                              sume_sb[:, t:t + 1], sume4_sb[:, 0:nch],
                                              axis=AX.X),
                                          deps=[se], tag=f"vsm{g}_{h}_{t}")
                            sl = prog.op("scalar",
                                         lambda e, t=t: e.activation(
                                             lsum_sb[:, t:t + 1], sume_sb[:, t:t + 1], AF.Ln),
                                         deps=[vsm], tag=f"sl{g}_{h}_{t}")
                            vl = prog.op("vector",
                                         lambda e, t=t: e.tensor_tensor(
                                             lsum_sb[:, t:t + 1], lsum_sb[:, t:t + 1],
                                             negmx_sb[:, t:t + 1], OP.subtract),
                                         deps=[sl], tag=f"vl{g}_{h}_{t}")
                            tt = prog.op("tensor",
                                         lambda e, t=t: e.matmul(
                                             PSB[6][0:1, t * 128:(t + 1) * 128],
                                             lsum_sb[:, t:t + 1], ident_sb[:],
                                             start=True, stop=True, is_transpose=True),
                                         deps=rw([], [BK[6]], extra=[vl]),
                                         tag=f"tt{g}_{h}_{t}")
                            commit(tt, [], [BK[6]]) if t == 0 else BK[6].on_write(tt)
                            exp_ops.append((se, tt))
                        slr = prog.op("vector",
                                      lambda e: e.tensor_copy(lrow2_sb[:], PSB[6][0:1, 0:512]),
                                      deps=rw([], [RLS], extra=[exp_ops[3][1]]),
                                      tag=f"slr{g}_{h}")
                        BK[6].on_read(slr)
                        commit(slr, [], [RLS])
                        tlb = prog.op("tensor",
                                      lambda e: e.matmul(pslice(3072, 3584), ones1_sb[:],
                                                         lrow2_sb[:], start=True, stop=True),
                                      deps=rw([RLS], [BK[6]], extra=[slr]), tag=f"tlb{g}_{h}")
                        commit(tlb, [RLS], [BK[6]])
                        slb = prog.op("vector",
                                      lambda e: e.tensor_copy(lsb_sb[:], pslice(3072, 3584)),
                                      deps=[tlb], tag=f"slb{g}_{h}")
                        BK[6].on_read(slb)
                        RLS.on_write(slb)

                        pv_last = None
                        for mt in range(4 * (g + 1)):
                            pb = mt % 2
                            cols = (2560, 3072) if pb == 0 else (3072, 3584)
                            bnk = BK[5] if pb == 0 else BK[6]
                            p2 = prog.op("tensor",
                                         lambda e, mt=mt, bk=bk2, b=bq, cols=cols: e.matmul(
                                             pslice(*cols),
                                             khK_sb[bk][:, mt * 128:(mt + 1) * 128],
                                             qh_sb[b][:, 0:GS], start=True, stop=True),
                                         deps=rw([RQH[bq], RKH[bk2]], [bnk],
                                                 extra=([slb] if pb == 1 else [])),
                                         tag=f"p2_{g}_{h}_{mt}")
                            commit(p2, [RQH[bq]], [])
                            commit(p2, [RKH[bk2]], [])
                            bnk.on_write(p2)
                            dv = [p2]
                            if mt >= 4 * g:
                                v = mt - 4 * g
                                mk = prog.op("vector",
                                             lambda e, v=v, cols=cols: e.tensor_tensor(
                                                 pslice(*cols), pslice(*cols),
                                                 maskT4_sb[:, v * 512:(v + 1) * 512], OP.add),
                                             deps=[p2], tag=f"mk{g}_{h}_{mt}")
                                bnk.on_read(mk)
                                bnk.on_write(mk)
                                dv = [mk]
                            vs = prog.op("vector",
                                         lambda e, pb=pb, cols=cols: e.tensor_tensor(
                                             stmp_sb[pb][:], pslice(*cols), lsb_sb[:],
                                             OP.subtract),
                                         deps=rw([RST[pb]], [], extra=dv + [slb]),
                                         tag=f"vs{g}_{h}_{mt}")
                            bnk.on_read(vs)
                            commit(vs, [], [RST[pb]])
                            sx = prog.op("scalar",
                                         lambda e, pb=pb: e.activation(
                                             pt_sb[pb][:], stmp_sb[pb][:], AF.Exp),
                                         deps=rw([RST[pb], RPT[pb]], []), tag=f"sx{g}_{h}_{mt}")
                            commit(sx, [RST[pb]], [])
                            commit(sx, [], [RPT[pb]])
                            pv = prog.op("tensor",
                                         lambda e, mt=mt, pb=pb, bk=bk2, g=g: e.matmul(
                                             pslice(3584, 4096),
                                             vh_sb[bk][:, mt * 128:(mt + 1) * 128],
                                             pt_sb[pb][:], start=(mt == 0),
                                             stop=(mt == 4 * (g + 1) - 1)),
                                         deps=rw([RPT[pb], RVH[bk2]], [BK[7]] if mt == 0 else [],
                                                 extra=[sx, vvh]),
                                         tag=f"pv{g}_{h}_{mt}")
                            commit(pv, [RPT[pb]], [])
                            commit(pv, [RVH[bk2]], [])
                            if mt == 0:
                                commit(pv, [], [BK[7]])
                            else:
                                BK[7].on_write(pv)
                            pv_last = pv
                        so = prog.op("vector",
                                     lambda e, h=h: e.tensor_copy(oall_sb[:, h * GS:(h + 1) * GS],
                                                                  pslice(3584, 4096)),
                                     deps=rw([], [ROA[h]], extra=[pv_last]), tag=f"so{g}_{h}")
                        BK[7].on_read(so)
                        commit(so, [], [ROA[h]])

                    evacs = []
                    for eh in range(2):
                        for h in range(H):
                            iwh = (g * 2 + eh) * H + h
                            bw = iwh % 2
                            wol = prog.op("scalar",
                                          lambda e, h=h, eh=eh, b=bw: e.dma_start(
                                              out=woh_sb[b][:],
                                              in_=woutT_s[h][:, eh * HALF:(eh + 1) * HALF]),
                                          deps=rw([], [RWOH[bw]],
                                                  extra=wout_st_by_h[h]), dma=True,
                                          tag=f"wol{g}_{eh}_{h}")
                            commit(wol, [], [RWOH[bw]])
                            for t in range(4):
                                def wmm(e, t=t, h=h, b=bw):
                                    for c2 in range(2):
                                        inst = e.matmul(
                                            pslice(t * 1024 + c2 * 512, t * 1024 + (c2 + 1) * 512),
                                            oall_sb[:, h * GS + t * 128: h * GS + t * 128 + 128],
                                            woh_sb[b][:, c2 * 512:(c2 + 1) * 512],
                                            start=(h == 0), stop=(h == H - 1))
                                    return inst
                                wm = prog.op("tensor", wmm,
                                             deps=rw([RWOH[bw], ROA[h]],
                                                     banks(t * 1024, (t + 1) * 1024) if h == 0 else [],
                                                     extra=[wol]),
                                             tag=f"wm{g}_{eh}_{h}_{t}")
                                commit(wm, [RWOH[bw]], [])
                                commit(wm, [ROA[h]], [])
                                if h == 0:
                                    for b in banks(t * 1024, (t + 1) * 1024):
                                        b.on_write(wm)
                                else:
                                    for b in banks(t * 1024, (t + 1) * 1024):
                                        b.on_write(wm)
                        for t in range(4):
                            ba = (eh * 4 + t) % 2
                            ev = prog.op("vector",
                                         lambda e, t=t, b=ba: e.tensor_copy(
                                             axb_sb[b][:, 0:512],
                                             pslice(t * 1024, t * 1024 + 512)),
                                         deps=rw([], [RAXB[ba]],
                                                 extra=[prog.last["tensor"]]),
                                         tag=f"ev{g}_{eh}_{t}a")
                            BK[2 * t].on_read(ev)
                            ev = prog.op("vector",
                                         lambda e, t=t, b=ba: e.tensor_copy(
                                             axb_sb[b][:, 512:1024],
                                             pslice(t * 1024 + 512, (t + 1) * 1024)),
                                         tag=f"ev{g}_{eh}_{t}b")
                            BK[2 * t + 1].on_read(ev)
                            commit(ev, [], [RAXB[ba]])
                            st = prog.op("scalar",
                                         lambda e, g=g, eh=eh, t=t, b=ba: e.dma_start(
                                             out=attn_s[g * GS + t * 128: g * GS + (t + 1) * 128,
                                                        eh * HALF:(eh + 1) * HALF],
                                             in_=axb_sb[b][:]),
                                         deps=[ev], dma=True, tag=f"st{g}_{eh}_{t}")
                            commit(st, [RAXB[ba]], [])
                            evacs.append(ev)
                a_last = prog.fence_all()
                for o in a_last:
                    o.needed = True
                emitted = prog.emit_phase(block, emitted)
                for b in BK:
                    b.w = None
                    b.readers = []

        if "F" not in PHASES:
            return nc
        with ExitStack() as fctx:
            inter_sb = sbt(fctx, "inter_sb", [128, 32 * HALF], BF16)
            xnh_sb = sbt(fctx, "xnh_sb", [128, 16 * HALF], BF16)
            w0_sb = [sbt(fctx, f"w0b{i}", [128, 16 * 256], BF16) for i in range(2)]
            w1_sb = [sbt(fctx, f"w1b{i}", [128, 16 * 256], BF16) for i in range(2)]
            wo_sb = [sbt(fctx, f"wob{i}", [128, 32 * 128], BF16) for i in range(2)]
            cbc_sb = sbt(fctx, "cbc_sb", [128, HALF], F32)
            silu_sb = [sbt(fctx, f"silu{i}", [128, HALF], F32) for i in range(2)]
            wev_sb = sbt(fctx, "wev_sb", [128, HALF], F32)
            axf_sb = sbt(fctx, "axf_sb", [128, E], F32)
            xf_sb = sbt(fctx, "xf_sb", [128, E], F32)
            fcol_sb = sbt(fctx, "fcol_sb", [128, 16 * 128], F32)

            RXH = Region()
            RW0 = [Region() for _ in range(2)]
            RW1 = [Region() for _ in range(2)]
            RWO = [Region() for _ in range(2)]
            RCB = Region()
            RSI = [Region() for _ in range(2)]
            RIN = [Region() for _ in range(32)]
            RWE = Region()
            RFFS = Region()
            RAXF = Region()
            RXF = Region()
            RFC = Region()

            dep_A = a_last
            y_stores = []
            for half in range(2):
                def xnh_ld(e, half=half):
                    return [e.dma_start(
                        out=xnh_sb[:].rearrange("p (c n) -> p c n", c=16)[:, q * 8:(q + 1) * 8, :],
                        in_=xnT_s[q * 8:(q + 1) * 8, :, half * HALF:(half + 1) * HALF]
                        .rearrange("c p n -> p c n")) for q in range(2)]
                xl = prog.op("scalar", xnh_ld,
                             deps=rw([], [RXH], extra=(dep_A if half == 0 else [])),
                             dma=True, ndma=2, tag=f"xnh{half}")
                commit(xl, [], [RXH])
                for ex in range(NEXP):
                    iex = half * NEXP + ex
                    cb = prog.op("tensor",
                                 lambda e, ex=ex, half=half: [
                                     e.matmul(pslice(0, 512), d8_sb[:, ex * 128:(ex + 1) * 128],
                                              combT_sb[:, half * HALF: half * HALF + 512],
                                              start=True, stop=True),
                                     e.matmul(pslice(512, 1024), d8_sb[:, ex * 128:(ex + 1) * 128],
                                              combT_sb[:, half * HALF + 512: half * HALF + 1024],
                                              start=True, stop=True)][-1],
                                 deps=rw([], banks(0, 1024),
                                         extra=(dep_A if half == 0 and ex == 0 else [])),
                                 tag=f"cb{half}_{ex}")
                    for b in banks(0, 1024):
                        b.on_write(cb)
                    scb = prog.op("vector",
                                  lambda e: e.tensor_copy(cbc_sb[:, 0:512], pslice(0, 512)),
                                  deps=rw([], [RCB], extra=[cb]), tag=f"scb{half}_{ex}a")
                    BK[0].on_read(scb)
                    scb = prog.op("vector",
                                  lambda e: e.tensor_copy(cbc_sb[:, 512:1024], pslice(512, 1024)),
                                  tag=f"scb{half}_{ex}b")
                    BK[1].on_read(scb)
                    commit(scb, [], [RCB])

                    for mb in range(16):
                        bwb = mb % 2
                        l0 = prog.op("gpsimd",
                                     lambda e, ex=ex, mb=mb, b=bwb: [
                                         e.dma_start(
                                             out=w0_sb[b][:].rearrange("p (c m) -> p c m", c=16)[:, 0:8, :],
                                             in_=w0_d[ex].rearrange("(c p) m -> p c m", p=128)
                                             [:, 0:8, mb * 256:(mb + 1) * 256]),
                                         e.dma_start(
                                             out=w0_sb[b][:].rearrange("p (c m) -> p c m", c=16)[:, 8:16, :],
                                             in_=w0_d[ex].rearrange("(c p) m -> p c m", p=128)
                                             [:, 8:16, mb * 256:(mb + 1) * 256])],
                                     deps=rw([], [RW0[bwb]],
                                             extra=(dep_A if iex == 0 and mb == 0 else [])),
                                     dma=True, ndma=2, tag=f"l0_{iex}_{mb}")
                        commit(l0, [], [RW0[bwb]])
                        l1 = prog.op("gpsimd",
                                     lambda e, ex=ex, mb=mb, b=bwb: [
                                         e.dma_start(
                                             out=w1_sb[b][:].rearrange("p (c m) -> p c m", c=16)[:, 0:8, :],
                                             in_=w1_d[ex].rearrange("(c p) m -> p c m", p=128)
                                             [:, 0:8, mb * 256:(mb + 1) * 256]),
                                         e.dma_start(
                                             out=w1_sb[b][:].rearrange("p (c m) -> p c m", c=16)[:, 8:16, :],
                                             in_=w1_d[ex].rearrange("(c p) m -> p c m", p=128)
                                             [:, 8:16, mb * 256:(mb + 1) * 256])],
                                     deps=rw([], [RW1[bwb]]), dma=True, ndma=2,
                                     tag=f"l1_{iex}_{mb}")
                        commit(l1, [], [RW1[bwb]])
                        for mi in range(2):
                            mt = mb * 2 + mi
                            par = mt % 2
                            c0 = 0 if par == 0 else 2048
                            c1 = 1024 if par == 0 else 3072

                            def h0mm(e, b=bwb, mi=mi, c0=c0):
                                for ec in range(16):
                                    for c2 in range(2):
                                        inst = e.matmul(
                                            pslice(c0 + c2 * 512, c0 + (c2 + 1) * 512),
                                            w0_sb[b][:, ec * 256 + mi * 128: ec * 256 + mi * 128 + 128],
                                            xnh_sb[:, ec * HALF + c2 * 512: ec * HALF + c2 * 512 + 512],
                                            start=(ec == 0), stop=(ec == 15))
                                return inst
                            extra0 = [l0, xl] + ([scb] if (par == 0) else [])
                            hm0 = prog.op("tensor", h0mm,
                                          deps=rw([RW0[bwb], RXH], banks(c0, c0 + 1024),
                                                  extra=extra0),
                                          tag=f"h0_{iex}_{mt}")
                            commit(hm0, [RW0[bwb]], [])
                            commit(hm0, [RXH], [])
                            for b in banks(c0, c0 + 1024):
                                b.on_write(hm0)

                            def h1mm(e, b=bwb, mi=mi, c1=c1):
                                for ec in range(16):
                                    for c2 in range(2):
                                        inst = e.matmul(
                                            pslice(c1 + c2 * 512, c1 + (c2 + 1) * 512),
                                            w1_sb[b][:, ec * 256 + mi * 128: ec * 256 + mi * 128 + 128],
                                            xnh_sb[:, ec * HALF + c2 * 512: ec * HALF + c2 * 512 + 512],
                                            start=(ec == 0), stop=(ec == 15))
                                return inst
                            hm1 = prog.op("tensor", h1mm,
                                          deps=rw([RW1[bwb], RXH], banks(c1, c1 + 1024),
                                                  extra=[l1]),
                                          tag=f"h1_{iex}_{mt}")
                            commit(hm1, [RW1[bwb]], [])
                            commit(hm1, [RXH], [])
                            for b in banks(c1, c1 + 1024):
                                b.on_write(hm1)
                            sic = prog.op("vector",
                                          lambda e, par=par, c0=c0: e.tensor_copy(
                                              silu_sb[par][:, 0:512],
                                              pslice(c0, c0 + 512)),
                                          deps=rw([], [RSI[par]], extra=[hm0]),
                                          tag=f"sic_{iex}_{mt}a")
                            BK[c0 // 512].on_read(sic)
                            sic = prog.op("vector",
                                          lambda e, par=par, c0=c0: e.tensor_copy(
                                              silu_sb[par][:, 512:1024],
                                              pslice(c0 + 512, c0 + 1024)),
                                          tag=f"sic_{iex}_{mt}b")
                            BK[c0 // 512 + 1].on_read(sic)
                            si = prog.op("scalar",
                                         lambda e, par=par: e.activation(
                                             silu_sb[par][:], silu_sb[par][:], AF.Silu),
                                         deps=[sic], tag=f"si_{iex}_{mt}")
                            commit(si, [], [RSI[par]])
                            v1 = prog.op("vector",
                                         lambda e, par=par, c1=c1: e.tensor_tensor(
                                             silu_sb[par][:, 0:512], silu_sb[par][:, 0:512],
                                             pslice(c1, c1 + 512), OP.mult),
                                         deps=rw([RSI[par]], [], extra=[si, hm1]),
                                         tag=f"v1_{iex}_{mt}a")
                            BK[c1 // 512].on_read(v1)
                            v1 = prog.op("vector",
                                         lambda e, par=par, c1=c1: e.tensor_tensor(
                                             silu_sb[par][:, 512:1024], silu_sb[par][:, 512:1024],
                                             pslice(c1 + 512, c1 + 1024), OP.mult),
                                         tag=f"v1_{iex}_{mt}b")
                            BK[c1 // 512 + 1].on_read(v1)
                            commit(v1, [RSI[par]], [RSI[par]])
                            v2 = prog.op("vector",
                                         lambda e, mt=mt, par=par: e.tensor_tensor(
                                             inter_sb[:, mt * HALF:(mt + 1) * HALF],
                                             silu_sb[par][:], cbc_sb[:], OP.mult),
                                         deps=rw([RSI[par], RCB], [RIN[mt]]),
                                         tag=f"v2_{iex}_{mt}")
                            commit(v2, [RSI[par]], [])
                            commit(v2, [RCB], [])
                            commit(v2, [], [RIN[mt]])

                    first_acc_dep = [RFFS.w] if RFFS.w is not None else []
                    for dt in range(16):
                        bwo = dt % 2
                        lwo = prog.op("gpsimd",
                                      lambda e, ex=ex, dt=dt, b=bwo: [
                                          e.dma_start(
                                              out=wo_sb[b][:].rearrange("p (c d) -> p c d", c=32)
                                              [:, q * 8:(q + 1) * 8, :],
                                              in_=wo_d[ex].rearrange("(c p) d -> p c d", p=128)
                                              [:, q * 8:(q + 1) * 8, dt * 128:(dt + 1) * 128])
                                          for q in range(4)],
                                      deps=rw([], [RWO[bwo]]), dma=True, ndma=4,
                                      tag=f"lwo{iex}_{dt}")
                        commit(lwo, [], [RWO[bwo]])
                        pc = (dt % 2) * 1024

                        def womm(e, b=bwo, pc=pc):
                            for mt in range(32):
                                for c2 in range(2):
                                    inst = e.matmul(
                                        pslice(pc + c2 * 512, pc + (c2 + 1) * 512),
                                        wo_sb[b][:, mt * 128:(mt + 1) * 128],
                                        inter_sb[:, mt * HALF + c2 * 512: mt * HALF + c2 * 512 + 512],
                                        start=(mt == 0), stop=(mt == 31))
                                return_inst = inst
                            return return_inst
                        wdeps = rw([RWO[bwo]] + RIN, banks(pc, pc + 1024), extra=[lwo])
                        wm = prog.op("tensor", womm, deps=wdeps, tag=f"wo{iex}_{dt}")
                        commit(wm, [RWO[bwo]], [])
                        for r in RIN:
                            r.on_read(wm)
                        for b in banks(pc, pc + 1024):
                            b.on_write(wm)
                        we = prog.op("vector",
                                     lambda e, pc=pc: e.tensor_copy(wev_sb[:, 0:512],
                                                                    pslice(pc, pc + 512)),
                                     deps=rw([], [RWE], extra=[wm]), tag=f"we{iex}_{dt}a")
                        BK[pc // 512].on_read(we)
                        we = prog.op("vector",
                                     lambda e, pc=pc: e.tensor_copy(wev_sb[:, 512:1024],
                                                                    pslice(pc + 512, pc + 1024)),
                                     tag=f"we{iex}_{dt}b")
                        BK[pc // 512 + 1].on_read(we)
                        commit(we, [], [RWE])
                        ac = prog.op("gpsimd",
                                     lambda e, half=half, ex=ex, dt=dt: e.dma_start(
                                         out=ffnT_s[dt * 128:(dt + 1) * 128,
                                                    half * HALF:(half + 1) * HALF],
                                         in_=wev_sb[:],
                                         **({} if ex == 0 else {"accum_op": OP.add})),
                                     deps=[we] + (first_acc_dep if dt == 0 else []),
                                     dma=True, tag=f"ac{iex}_{dt}")
                        commit(ac, [RWE], [])
                        if dt == 15:
                            RFFS.on_write(ac)

                ffn_done = RFFS.w
                for t in range(8):
                    row0 = half * HALF + t * 128
                    def fcol_ld(e, row0=row0):
                        return [e.dma_start(
                            out=fcol_sb[:].rearrange("p (c n) -> p c n", c=16)
                            [:, q * 8:(q + 1) * 8, :],
                            in_=ffnT_s[q * 1024:(q + 1) * 1024, row0:row0 + 128]
                            .rearrange("(c p) n -> p c n", p=128)) for q in range(2)]
                    fl = prog.op("scalar", fcol_ld,
                                 deps=rw([], [RFC], extra=[ffn_done]), dma=True, ndma=2,
                                 tag=f"fl{half}_{t}")
                    commit(fl, [], [RFC])
                    al = prog.op("scalar",
                                 lambda e, row0=row0: e.dma_start(
                                     out=axf_sb[:], in_=attn_s[row0:row0 + 128, :]),
                                 deps=rw([], [RAXF]), dma=True, tag=f"al{half}_{t}")
                    commit(al, [], [RAXF])
                    xlo = prog.op("scalar",
                                  lambda e, row0=row0: e.dma_start(
                                      out=xf_sb[:], in_=x_d[row0:row0 + 128, :]),
                                  deps=rw([], [RXF]), dma=True, tag=f"xl{half}_{t}")
                    commit(xlo, [], [RXF])

                    def ftp(e):
                        for c in range(16):
                            inst = e.matmul(pslice(c * 128, (c + 1) * 128),
                                            fcol_sb[:, c * 128:(c + 1) * 128], ident_sb[:],
                                            start=True, stop=True, is_transpose=True)
                        return inst
                    tf = prog.op("tensor", ftp,
                                 deps=rw([RFC], banks(0, 2048)), tag=f"tf{half}_{t}")
                    commit(tf, [RFC], [])
                    for b in banks(0, 2048):
                        b.on_write(tf)
                    va = None
                    for q in range(4):
                        va = prog.op("vector",
                                     lambda e, q=q: e.tensor_tensor(
                                         axf_sb[:, q * 512:(q + 1) * 512],
                                         pslice(q * 512, (q + 1) * 512),
                                         axf_sb[:, q * 512:(q + 1) * 512], OP.add),
                                     deps=(rw([RAXF], [], extra=[tf, al]) if q == 0 else []),
                                     tag=f"va{half}_{t}_{q}")
                        BK[q].on_read(va)
                    commit(va, [RAXF], [RAXF])
                    vb = prog.op("vector",
                                 lambda e: e.tensor_tensor(axf_sb[:], axf_sb[:], xf_sb[:],
                                                           OP.add),
                                 deps=rw([RXF], [], extra=[xlo]), tag=f"vb{half}_{t}")
                    commit(vb, [RXF], [RAXF])
                    ys = prog.op("scalar",
                                 lambda e, row0=row0: e.dma_start(
                                     out=y_d[row0:row0 + 128, :], in_=axf_sb[:]),
                                 deps=[vb], dma=True, tag=f"ys{half}_{t}")
                    commit(ys, [RAXF], [])
                    y_stores.append(ys)
            prog.op("scalar", lambda e: e.nop(), deps=[y_stores[-1]], tag="fin")
            emitted = prog.emit_phase(block, emitted)

    return nc


def _consts():
    bf = ml_dtypes.bfloat16
    inv = (10000.0 ** (-np.arange(0, D, 2, dtype=np.float64) / D))
    ang = np.arange(N, dtype=np.float64)[:, None] * inv[None, :]
    cos = np.cos(ang).astype(np.float32)
    sin = np.sin(ang).astype(np.float32)
    cos_tok = cos.astype(bf)
    sin_tok = sin.astype(bf)
    cosT = cos.T.copy().astype(bf)
    sinT = sin.T.copy().astype(bf)
    p = np.arange(128)[:, None]
    c = np.arange(512)[None, :]
    mask4 = np.zeros((128, 4, 512), np.float32)
    maskT4 = np.zeros((128, 4, 512), np.float32)
    for v in range(4):
        vis = (c <= v * 128 + p) & True
        vis = (c < v * 128) | ((c >= v * 128) & (c < (v + 1) * 128) & ((c - v * 128) <= p))
        mask4[:, v, :] = np.where(vis, 0.0, NEG)
        t_ = c // 128
        jj = c % 128
        visT = (t_ > v) | ((t_ == v) & (jj >= p))
        maskT4[:, v, :] = np.where(visT, 0.0, NEG)
    mask4 = mask4.reshape(128, 2048).astype(bf)
    maskT4 = maskT4.reshape(128, 2048).astype(bf)
    d16 = np.zeros((16, 16 * 128), np.float32)
    for h in range(16):
        d16[h, h * 128:(h + 1) * 128] = 1.0
    d8 = np.zeros((8, 8 * 128), np.float32)
    for ex in range(8):
        d8[ex, ex * 128:(ex + 1) * 128] = 1.0
    return {
        "cos_tok": cos_tok, "sin_tok": sin_tok, "cosT": cosT, "sinT": sinT,
        "mask4": mask4, "maskT4": maskT4, "delta16": d16.astype(bf),
        "delta8": d8.astype(np.float32),
        "ones1": np.ones((1, 128), np.float32),
        "ident": np.eye(128, dtype=np.float32),
    }


def kernel(L0_attn_x, L0_attn_Wq, L0_attn_Wkv, L0_attn_Wkvh, L0_attn_Wg,
           L0_attn_Wout, L0_ffn_gate_kernel, L0_ffn_w0_kernel, L0_ffn_w1_kernel,
           L0_ffn_wo_kernel):
    x = np.ascontiguousarray(np.asarray(L0_attn_x, dtype=np.float32))
    shared = {
        "Wq": np.ascontiguousarray(np.asarray(L0_attn_Wq, np.float32)),
        "Wkv": np.ascontiguousarray(np.asarray(L0_attn_Wkv, np.float32)),
        "Wkvh": np.ascontiguousarray(np.asarray(L0_attn_Wkvh, np.float32)),
        "Wg": np.ascontiguousarray(np.asarray(L0_attn_Wg, np.float32)),
        "Wout": np.ascontiguousarray(np.asarray(L0_attn_Wout, np.float32)),
        "gate": np.ascontiguousarray(np.asarray(L0_ffn_gate_kernel, np.float32)),
        "w0": np.ascontiguousarray(np.asarray(L0_ffn_w0_kernel, np.float32)),
        "w1": np.ascontiguousarray(np.asarray(L0_ffn_w1_kernel, np.float32)),
        "wo": np.ascontiguousarray(np.asarray(L0_ffn_wo_kernel, np.float32)),
    }
    shared.update(_consts())
    nc = build_program()
    in_maps = [dict(shared, x=np.ascontiguousarray(x[i])) for i in range(8)]
    res = run_bass_kernel_spmd(nc, in_maps, core_ids=list(range(8)))
    return np.stack([np.asarray(r["y"], np.float32) for r in res.results], axis=0)


# version 04
# speedup vs baseline: 1.0632x; 1.0124x over previous
import numpy as np
import ml_dtypes
import concourse.bass as bass
import concourse.mybir as mybir
from concourse.bass_utils import run_bass_kernel_spmd
from contextlib import ExitStack

F32 = mybir.dt.float32
BF16 = mybir.dt.bfloat16
AF = mybir.ActivationFunctionType
OP = mybir.AluOpType
AX = mybir.AxisListType

PHASES = "PAF"
PSTAGE = 9
K5 = 3
N = 2048
E = 2048
H = 16
D = 128
NEXP = 8
MLP = 4096
NT = 16
NG = 4
GS = 512
HALF = 1024
NEG = -1.0e30
ENGINES = ("tensor", "vector", "scalar", "gpsimd", "sync")


class Op:
    __slots__ = ("engine", "fn", "deps", "dma", "ndma", "lane", "ev", "needed", "tag")

    def __init__(self, engine, fn, deps, dma, ndma, tag):
        self.engine = engine
        self.fn = fn
        self.deps = [d for d in deps if d is not None]
        self.dma = dma
        self.ndma = ndma
        self.lane = None
        self.ev = None
        self.needed = False
        self.tag = tag


class Prog:

    NDMA_SEMS = 16

    def __init__(self, nc, ctx):
        self.nc = nc
        self.ops = []
        self.sems = {}
        self.sem_vals = {}
        self.dma_idx = {}
        for e in ENGINES:
            name = f"s_{e}_c"
            self.sems[name] = ctx.enter_context(nc.semaphore(name))
            self.sem_vals[name] = 0
            self.dma_idx[e] = 0
            for i in range(self.NDMA_SEMS):
                name = f"s_{e}_d{i}"
                self.sems[name] = ctx.enter_context(nc.semaphore(name))
                self.sem_vals[name] = 0
        self.last = {e: None for e in ENGINES}
        self.last_compute = {e: None for e in ENGINES}
        self.lane_last = {e: [None] * self.NDMA_SEMS for e in ENGINES}

    def op(self, engine, fn, deps=(), dma=False, ndma=1, tag=""):
        o = Op(engine, fn, list(deps), dma, ndma, tag)
        if engine == "sync" and self.last[engine] is not None:
            o.deps.append(self.last[engine])
        elif engine in ("vector", "scalar") and not dma \
                and self.last_compute[engine] is not None:
            o.deps.append(self.last_compute[engine])
        if dma:
            o.lane = self.dma_idx[engine]
            self.dma_idx[engine] = (o.lane + 1) % self.NDMA_SEMS
            self.lane_last[engine][o.lane] = o
        else:
            self.last_compute[engine] = o
        self.ops.append(o)
        self.last[engine] = o
        return o

    def fence(self, engine):
        out = [o for o in self.lane_last[engine] if o is not None]
        if self.last_compute[engine] is not None:
            out.append(self.last_compute[engine])
        return out

    def fence_all(self):
        return [o for e in ENGINES for o in self.fence(e)]

    def emit_phase(self, block, start_idx):
        ops = self.ops[start_idx:]
        cur = set(id(o) for o in ops)
        for o in ops:
            for d in o.deps:
                if id(d) not in cur and d.ev is None:
                    raise RuntimeError(
                        f"dep {d.tag} of {o.tag} is from an already-emitted phase "
                        f"but was not pinned")
                d.needed = True
        for o in ops:
            if (o.needed or o.dma) and o.ev is None:
                if o.dma:
                    name = f"s_{o.engine}_d{o.lane}"
                    step = 16 * o.ndma
                else:
                    name = f"s_{o.engine}_c"
                    step = 1
                self.sem_vals[name] += step
                o.ev = (name, self.sem_vals[name])
        per_engine = {e: [] for e in ENGINES}
        for o in ops:
            per_engine[o.engine].append(o)
        for e in ENGINES:
            recs = per_engine[e]
            if not recs:
                continue
            sec = getattr(block, e)

            def section(eng, recs=recs):
                seen = {}
                for o in recs:
                    for d in o.deps:
                        if d.ev is None:
                            raise RuntimeError(f"dep of {o.tag} has no event: {d.tag}")
                        s, v = d.ev
                        if seen.get(s, 0) < v:
                            eng.wait_ge(self.sems[s], v)
                            seen[s] = v
                    if o.dma and o.ev is not None:
                        s, v = o.ev
                        prev = v - 16 * o.ndma
                        if prev > 0 and seen.get(s, 0) < prev:
                            eng.wait_ge(self.sems[s], prev)
                            seen[s] = prev
                    inst = o.fn(eng)
                    if o.ev is not None:
                        insts = inst if isinstance(inst, list) else [inst]
                        assert len(insts) == (o.ndma if o.dma else 1), o.tag
                        for i2 in insts:
                            i2.then_inc(self.sems[o.ev[0]], 16 if o.dma else 1)

            sec(section)
        return len(self.ops)


class Region:

    def __init__(self):
        self.w = None
        self.readers = []

    def wdeps(self):
        return ([self.w] if self.w is not None else []) + self.readers

    def on_write(self, op):
        self.w = op
        self.readers = []

    def rdeps(self):
        return [self.w] if self.w is not None else []

    def on_read(self, op):
        self.readers.append(op)


def rw(regions_read, regions_written, extra=()):
    deps = list(extra)
    for r in regions_read:
        deps += r.rdeps()
    for r in regions_written:
        deps += r.wdeps()
    return deps


def commit(op, regions_read, regions_written):
    for r in regions_read:
        r.on_read(op)
    for r in regions_written:
        r.on_write(op)
    return op


def build_program():
    nc = bass.Bass("TRN2", target_bir_lowering=False)

    x_d = nc.dram_tensor("x", [N, E], F32, kind="ExternalInput").ap()
    wq_d = nc.dram_tensor("Wq", [E, E], F32, kind="ExternalInput").ap()
    wkv_d = nc.dram_tensor("Wkv", [256, E], F32, kind="ExternalInput").ap()
    wkvh_d = nc.dram_tensor("Wkvh", [32, E], F32, kind="ExternalInput").ap()
    wg_d = nc.dram_tensor("Wg", [32, E], F32, kind="ExternalInput").ap()
    wout_d = nc.dram_tensor("Wout", [E, E], F32, kind="ExternalInput").ap()
    gate_d = nc.dram_tensor("gate", [E, NEXP], F32, kind="ExternalInput").ap()
    w0_d = nc.dram_tensor("w0", [NEXP, E, MLP], F32, kind="ExternalInput").ap()
    w1_d = nc.dram_tensor("w1", [NEXP, E, MLP], F32, kind="ExternalInput").ap()
    wo_d = nc.dram_tensor("wo", [NEXP, MLP, E], F32, kind="ExternalInput").ap()
    costk_d = nc.dram_tensor("cos_tok", [N, 64], BF16, kind="ExternalInput").ap()
    sintk_d = nc.dram_tensor("sin_tok", [N, 64], BF16, kind="ExternalInput").ap()
    cosT_d = nc.dram_tensor("cosT", [64, N], BF16, kind="ExternalInput").ap()
    sinT_d = nc.dram_tensor("sinT", [64, N], BF16, kind="ExternalInput").ap()
    mask4_d = nc.dram_tensor("mask4", [128, 2048], BF16, kind="ExternalInput").ap()
    maskT4_d = nc.dram_tensor("maskT4", [128, 2048], BF16, kind="ExternalInput").ap()
    d16_d = nc.dram_tensor("delta16", [16, 16 * 128], BF16, kind="ExternalInput").ap()
    d8_d = nc.dram_tensor("delta8", [8, 8 * 128], F32, kind="ExternalInput").ap()
    ones1_d = nc.dram_tensor("ones1", [1, 128], F32, kind="ExternalInput").ap()
    ident_d = nc.dram_tensor("ident", [128, 128], F32, kind="ExternalInput").ap()
    y_d = nc.dram_tensor("y", [N, E], F32, kind="ExternalOutput").ap()

    xnT_s = nc.dram_tensor("xnT_s", [16, 128, N], BF16, kind="Internal").ap()
    wqT_s = nc.dram_tensor("wqT_s", [16, 128, E], BF16, kind="Internal").ap()
    woutT_s = nc.dram_tensor("woutT_s", [16, 128, E], BF16, kind="Internal").ap()
    w0b_s = nc.dram_tensor("w0b_s", [NEXP, 16, 128, 4096], BF16, kind="Internal").ap()
    w1b_s = nc.dram_tensor("w1b_s", [NEXP, 16, 128, 4096], BF16, kind="Internal").ap()
    wob_s = nc.dram_tensor("wob_s", [NEXP, 16, 128, 4096], BF16, kind="Internal").ap()
    attn_s = nc.dram_tensor("attn_s", [N, E], F32, kind="Internal").ap()
    ffnT_s = nc.dram_tensor("ffnT_s", [E, N], F32, kind="Internal").ap()

    with ExitStack() as root:
        def sbt(ctx, name, shape, dt):
            return ctx.enter_context(nc.sbuf_tensor(name, shape, dt))

        PSB = [root.enter_context(nc.psum_tensor(f"psb{b}", [128, 512], F32))
               for b in range(8)]
        BK = [Region() for _ in range(8)]

        def pslice(c0, c1):
            b = c0 // 512
            assert (c1 - 1) // 512 == b, (c0, c1)
            return PSB[b][:, c0 - b * 512: c1 - b * 512]

        def banks(c0, c1):
            return [BK[b] for b in range(c0 // 512, (c1 + 511) // 512)]

        prog = Prog(nc, root)
        block = root.enter_context(nc.Block())
        emitted = 0

        d8_sb = sbt(root, "d8_sb", [8, 8 * 128], F32)
        ones1_sb = sbt(root, "ones1_sb", [1, 128], F32)
        ident_sb = sbt(root, "ident_sb", [128, 128], F32)
        combT_sb = sbt(root, "combT_sb", [8, NT * 128], F32)

        with ExitStack() as actx:
            xnT_sb = sbt(actx, "xnT_sb", [128, 16 * N], BF16)
            kpT_sb = sbt(actx, "kpT_sb", [128, NT * 128], BF16)
            vp_sb = sbt(actx, "vp_sb", [128, NT * 128], BF16)
            vhtok_sb = sbt(actx, "vhtok_sb", [128, NT * 16], BF16)
            vhtok32_sb = sbt(actx, "vhtok32_sb", [128, NT * 16], F32)
            khT_bf = sbt(actx, "khT_bf", [16, N], BF16)
            costk_sb = sbt(actx, "costk_sb", [128, NT * 64], BF16)
            pw_sb = [sbt(actx, f"pw_sb{i}", [128, 2048], BF16) for i in range(2)]
            sintk_sb = sbt(actx, "sintk_sb", [128, NT * 64], BF16)

            with ExitStack() as pctx:
                x_sb = [sbt(pctx, f"x_sb{i}", [128, E], F32) for i in range(2)]
                xntk_sb = [sbt(pctx, f"xntk{i}", [128, E], BF16) for i in range(2)]
                wld_sb = [sbt(pctx, f"wld{i}", [128, E], BF16) for i in range(2)]
                wstage_sb = [sbt(pctx, "wstage0", [128, E], BF16)]
                wkvT_sb = sbt(pctx, "wkvT_sb", [128, 16 * 256], BF16)
                wgT_sb = sbt(pctx, "wgT_sb", [128, 16 * 32], BF16)
                wkvhT_sb = sbt(pctx, "wkvhT_sb", [128, 16 * 32], BF16)
                gate_sb = sbt(pctx, "gate_sb", [128, 16 * 8], BF16)
                sumsq_sb = sbt(pctx, "sumsq_sb", [128, 16], F32)
                mean_sb = sbt(pctx, "mean_sb", [128, 1], F32)
                var_sb = sbt(pctx, "var_sb", [128, 1], F32)
                rstd_sb = sbt(pctx, "rstd_sb", [128, 1], F32)
                sq_sb = sbt(pctx, "sq_sb", [128, E], BF16)
                G = {}
                for nm in ("pa", "pb", "lgs", "Fx", "tmpa", "tmpb"):
                    G[nm] = sbt(pctx, "G_" + nm, [16, N], F32)
                vhT_bf = sbt(pctx, "vhT_bf", [16, N], BF16)
                kprot_sb = [sbt(pctx, f"kprot{i}", [128, 128], BF16) for i in range(2)]
                ktmp = [sbt(pctx, f"ktmp{i}", [128, 64], F32) for i in range(2)]
                eq1_sb = sbt(pctx, "eq1_sb", [128, NT * 8], F32)
                eq2_sb = sbt(pctx, "eq2_sb", [128, NT * 8], F32)
                comb_sb = sbt(pctx, "comb_sb", [128, NT * 8], F32)
                dcol_sb = sbt(pctx, "dcol_sb", [128, 16], F32)
                w1c_sb = sbt(pctx, "w1c_sb", [128, 16], F32)
                w2c_sb = sbt(pctx, "w2c_sb", [128, 16], F32)
                m1_sb = sbt(pctx, "m1_sb", [128, 1], F32)
                m2_sb = sbt(pctx, "m2_sb", [128, 1], F32)

                RX = [Region() for _ in range(2)]
                RXN = [Region() for _ in range(2)]
                RWL = [Region() for _ in range(2)]
                RWS = [Region()]
                RKP = [Region() for _ in range(2)]
                Rstats = Region()
                Rxnt = Region()
                Rscr = Region()

                const_ops = []

                def ld(dst, src):
                    o = prog.op("scalar", lambda e, d=dst, s=src: e.dma_start(out=d, in_=s),
                                dma=True, tag="cld")
                    const_ops.append(o)
                    return o
                ld(d8_sb[:], d8_d)
                ld(ones1_sb[:], ones1_d)
                ld(ident_sb[:], ident_d)
                ld(costk_sb[:].rearrange("p (c j) -> p c j", c=NT),
                   costk_d.rearrange("(c p) j -> p c j", p=128))
                ld(sintk_sb[:].rearrange("p (c j) -> p c j", c=NT),
                   sintk_d.rearrange("(c p) j -> p c j", p=128))
                op_consts = const_ops

                if PSTAGE >= 2:
                    wq_st = {}
                    wout_st_by_h = {h: [] for h in range(16)}
                    op_wout_tp = []
                    for c in range(16):
                        lop = prog.op("gpsimd",
                                      lambda e, c=c, b=c % 2: e.dma_start(
                                          out=wld_sb[b][:], in_=wq_d[c * 128:(c + 1) * 128, :]),
                                      deps=rw([], [RWL[c % 2]]), dma=True, tag=f"wqld{c}")
                        commit(lop, [], [RWL[c % 2]])
                        top = prog.op("sync",
                                      lambda e, b=c % 2: e.dma_start_transpose(
                                          wstage_sb[0][:].rearrange("p (k d) -> p k d", k=16),
                                          wld_sb[b][:]),
                                      deps=rw([RWL[c % 2]], [RWS[0]]), dma=True, tag=f"wqtp{c}")
                        commit(top, [RWL[c % 2]], [RWS[0]])
                        sop = prog.op("scalar",
                                      lambda e, c=c, b=c % 2: e.dma_start(
                                          out=wqT_s[c], in_=wstage_sb[0][:]),
                                      deps=rw([RWS[0]], []), dma=True, tag=f"wqst{c}")
                        commit(sop, [RWS[0]], [])
                        wq_st[c] = sop
                    for c in range(16):
                        lop = prog.op("gpsimd",
                                      lambda e, c=c, b=c % 2: e.dma_start(
                                          out=wld_sb[b][:], in_=wout_d[c * 128:(c + 1) * 128, :]),
                                      deps=rw([], [RWL[c % 2]]), dma=True, tag=f"wold{c}")
                        commit(lop, [], [RWL[c % 2]])
                        top = prog.op("sync",
                                      lambda e, b=c % 2: e.dma_start_transpose(
                                          wstage_sb[0][:].rearrange("p (k d) -> p k d", k=16),
                                          wld_sb[b][:]),
                                      deps=rw([RWL[c % 2]], [RWS[0]]), dma=True, tag=f"wotp{c}")
                        commit(top, [RWL[c % 2]], [RWS[0]])
                        op_wout_tp.append(top)
                        for h in range(16):
                            sop = prog.op("scalar",
                                          lambda e, c=c, h=h, b=c % 2: e.dma_start(
                                              out=woutT_s[h][:, c * 128:(c + 1) * 128],
                                              in_=wstage_sb[0][:, h * 128:(h + 1) * 128]),
                                          deps=rw([RWS[0]], []), dma=True, tag=f"wost{c}_{h}")
                            commit(sop, [RWS[0]], [])
                            wout_st_by_h[h].append(sop)
                if PSTAGE >= 3:
                    kv_ld = []
                    for c in range(2):
                        lop = prog.op("gpsimd",
                                      lambda e, c=c, b=c % 2: e.dma_start(
                                          out=wld_sb[b][:], in_=wkv_d[c * 128:(c + 1) * 128, :]),
                                      deps=rw([], [RWL[c % 2]]), dma=True, tag=f"kvld{c}")
                        commit(lop, [], [RWL[c % 2]])
                        top = prog.op("sync",
                                      lambda e, c=c, b=c % 2: e.dma_start_transpose(
                                          wkvT_sb[:].rearrange("p (k n) -> p k n", k=16)
                                          [:, :, c * 128:(c + 1) * 128],
                                          wld_sb[b][:]),
                                      deps=rw([RWL[c % 2]], []), dma=True, tag=f"kvtp{c}")
                        commit(top, [RWL[c % 2]], [])
                        kv_ld.append(top)
                    g32 = prog.op("gpsimd", lambda e: e.dma_start(out=wld_sb[0][0:32, :], in_=wg_d),
                                  deps=rw([], [RWL[0]]), dma=True, tag="g32")
                    commit(g32, [], [RWL[0]])
                    kvh32 = prog.op("gpsimd",
                                    lambda e: e.dma_start(out=wld_sb[1][0:32, :], in_=wkvh_d),
                                    deps=rw([], [RWL[1]]), dma=True, tag="kvh32")
                    commit(kvh32, [], [RWL[1]])
                    gtld = prog.op("gpsimd",
                                   lambda e: e.dma_start(
                                       out=gate_sb[:].rearrange("p (c k) -> p c k", c=16),
                                       in_=gate_d.rearrange("(c p) k -> p c k", p=128)),
                                   dma=True, tag="gtld")
                    wgtp = prog.op("sync",
                                   lambda e: e.dma_start_transpose(
                                       wgT_sb[:].rearrange("p (c k) -> p c k", c=16),
                                       wld_sb[0][0:32, :]),
                                   deps=[g32], dma=True, tag="wgtp")
                    commit(wgtp, [RWL[0]], [])
                    kvhtp = prog.op("sync",
                                    lambda e: e.dma_start_transpose(
                                        wkvhT_sb[:].rearrange("p (c k) -> p c k", c=16),
                                        wld_sb[1][0:32, :]),
                                    deps=[kvh32], dma=True, tag="kvhtp")
                    commit(kvhtp, [RWL[1]], [])

                if PSTAGE >= 4:
                    xn_tp = []
                    for t in range(NT):
                        lop = prog.op("scalar",
                                      lambda e, t=t, b=t % 2: e.dma_start(
                                          out=x_sb[b][:], in_=x_d[t * 128:(t + 1) * 128, :]),
                                      deps=rw([], [RX[t % 2]]), dma=True, tag=f"xld{t}")
                        commit(lop, [], [RX[t % 2]])
                        sqo = prog.op("scalar",
                                      lambda e, t=t, b=t % 2: e.activation(
                                          sq_sb[:], x_sb[b][:], AF.Square,
                                          accum_out=sumsq_sb[:, t:t + 1]),
                                      deps=rw([RX[t % 2]], []), tag=f"sq{t}")
                        commit(sqo, [RX[t % 2]], [])
                        vst = prog.op("vector",
                                      lambda e, b=t % 2: e.reduce_sum(mean_sb[:], x_sb[b][:], axis=AX.X),
                                      deps=rw([RX[t % 2]], [Rstats]), tag=f"vm{t}")
                        commit(vst, [RX[t % 2]], [Rstats])
                        prog.op("vector", lambda e: e.tensor_scalar(
                            mean_sb[:], mean_sb[:], 1.0 / E, None, OP.mult, OP.bypass), tag="vm2")
                        prog.op("vector", lambda e: e.tensor_tensor(
                            var_sb[:], mean_sb[:], mean_sb[:], OP.mult), tag="vm3")
                        vvar = prog.op("vector",
                                       lambda e, t=t: e.scalar_tensor_tensor(
                                           var_sb[:], sumsq_sb[:, t:t + 1], 1.0 / E, var_sb[:],
                                           OP.mult, OP.subtract),
                                       deps=[sqo], tag=f"vvar{t}")
                        vve = prog.op("vector", lambda e: e.tensor_scalar(
                            var_sb[:], var_sb[:], 1e-5, None, OP.add, OP.bypass), tag="veps")
                        sln = prog.op("scalar", lambda e: e.activation(rstd_sb[:], var_sb[:], AF.Ln),
                                      deps=[vve], tag=f"sln{t}")
                        sex = prog.op("scalar", lambda e: e.activation(
                            rstd_sb[:], rstd_sb[:], AF.Exp, scale=-0.5), tag=f"sex{t}")
                        vxn = prog.op("vector",
                                      lambda e, b=t % 2: e.tensor_scalar(
                                          xntk_sb[b][:], x_sb[b][:], mean_sb[:, 0:1],
                                          rstd_sb[:, 0:1], OP.subtract, OP.mult),
                                      deps=rw([RX[t % 2], Rstats], [RXN[t % 2]], extra=[sex]),
                                      tag=f"vxn{t}")
                        commit(vxn, [RX[t % 2], Rstats], [RXN[t % 2]])
                        commit(vxn, [], [RX[t % 2]])
                        top = prog.op("sync",
                                      lambda e, t=t, b=t % 2: e.dma_start_transpose(
                                          xnT_sb[:].rearrange("p (c n) -> p c n", c=16)
                                          [:, :, t * 128:(t + 1) * 128],
                                          xntk_sb[b][:]),
                                      deps=rw([RXN[t % 2]], [Rxnt]), dma=True, tag=f"xntp{t}")
                        commit(top, [RXN[t % 2]], [])
                        xn_tp.append(top)
                    for o in prog.fence("sync"):
                        Rxnt.on_write(o) if False else None
                    Rxnt.w = None
                    Rxnt.readers = []
                    xn_tp_fence = prog.fence("sync")

                    def xnT_store(e):
                        return [e.dma_start(
                            out=xnT_s[q * 8:(q + 1) * 8].rearrange("c p n -> p c n"),
                            in_=xnT_sb[:].rearrange("p (c n) -> p c n", c=16)
                            [:, q * 8:(q + 1) * 8, :]) for q in range(2)]
                    xns = prog.op("scalar", xnT_store, deps=xn_tp_fence, dma=True, ndma=2,
                                  tag="xnst")

                if PSTAGE >= 5:
                    dep_xnt = xn_tp_fence
                    kv_done = []
                    for t in range(NT):
                        def kvmm(e, t=t):
                            for ec in range(16):
                                inst = e.matmul(pslice(0, 256),
                                                xnT_sb[:, ec * N + t * 128: ec * N + t * 128 + 128],
                                                wkvT_sb[:, ec * 256:(ec + 1) * 256],
                                                start=(ec == 0), stop=(ec == 15))
                            return inst
                        mm = prog.op("tensor", kvmm,
                                     deps=rw([], [BK[0]], extra=dep_xnt + [kv_ld[1]] + op_consts),
                                     tag=f"kv{t}")
                        commit(mm, [], [BK[0]])
                        b = t % 2

                        if K5 == 1:
                            rp = prog.op("vector",
                                         lambda e, b=b: e.tensor_copy(
                                             kprot_sb[b][:], pslice(0, 128)),
                                         deps=rw([BK[0]], [RKP[b]], extra=[mm]),
                                         tag=f"ropek{t}cp")
                            commit(rp, [BK[0]], [RKP[b]])
                        elif K5 in (0, 4):
                            rp = mm
                        elif K5 == 5:
                            rp = prog.op("vector",
                                         lambda e, b=b: e.tensor_copy(
                                             kprot_sb[b][:], pslice(0, 128)),
                                         deps=rw([BK[0]], [RKP[b]], extra=[mm]),
                                         tag=f"ropek{t}cp")
                            commit(rp, [BK[0]], [RKP[b]])
                        if K5 >= 2:
                         prog.op("vector", lambda e, t=t: e.tensor_tensor(
                            ktmp[0][:], pslice(0, 64), costk_sb[:, t * 64:(t + 1) * 64], OP.mult),
                            deps=rw([BK[0]], [RKP[b]], extra=[mm]), tag=f"ropek{t}a")
                         prog.op("vector", lambda e, t=t: e.tensor_tensor(
                            ktmp[1][:], pslice(64, 128), sintk_sb[:, t * 64:(t + 1) * 64], OP.mult),
                            tag=f"ropek{t}b")
                         prog.op("vector", lambda e, b=b: e.tensor_tensor(
                            kprot_sb[b][:, 0:64], ktmp[0][:], ktmp[1][:], OP.subtract),
                            tag=f"ropek{t}c")
                         prog.op("vector", lambda e, t=t: e.tensor_tensor(
                            ktmp[0][:], pslice(64, 128), costk_sb[:, t * 64:(t + 1) * 64], OP.mult),
                            tag=f"ropek{t}d")
                         prog.op("vector", lambda e, t=t: e.tensor_tensor(
                            ktmp[1][:], pslice(0, 64), sintk_sb[:, t * 64:(t + 1) * 64], OP.mult),
                            tag=f"ropek{t}e")
                         rp = prog.op("vector", lambda e, b=b: e.tensor_tensor(
                            kprot_sb[b][:, 64:128], ktmp[0][:], ktmp[1][:], OP.add),
                            tag=f"ropek{t}f")
                         commit(rp, [BK[0]], [RKP[b]])
                        vpo = prog.op("vector",
                                      lambda e, t=t: e.tensor_copy(vp_sb[:, t * 128:(t + 1) * 128],
                                                                   pslice(128, 256)),
                                      deps=[mm], tag=f"vp{t}")
                        commit(vpo, [BK[0]], [])
                        tp = prog.op("sync",
                                     lambda e, t=t, b=b: e.dma_start_transpose(
                                         kpT_sb[:, t * 128:(t + 1) * 128], kprot_sb[b][:]),
                                     deps=rw([RKP[b]], []), dma=True, tag=f"kptp{t}")
                        commit(tp, [RKP[b]], [])
                        kv_done.append(tp)
                    kpT_ready = kv_done[-1]

                if PSTAGE >= 6:
                    comb_parts = []
                    for t in range(NT):
                        def gmm(e, t=t):
                            for ec in range(16):
                                inst = e.matmul(pslice(2560, 2568),
                                                xnT_sb[:, ec * N + t * 128: ec * N + t * 128 + 128],
                                                gate_sb[:, ec * 8:(ec + 1) * 8],
                                                start=(ec == 0), stop=(ec == 15))
                            return inst
                        mm = prog.op("tensor", gmm,
                                     deps=rw([], [BK[5]], extra=dep_xnt + [gtld]), tag=f"g8{t}")
                        commit(mm, [], [BK[5]])

                        prog.op("vector", lambda e: e.reduce_max(
                            m1_sb[:], pslice(2560, 2568), axis=AX.X), deps=[mm], tag=f"gv{t}a")
                        prog.op("vector", lambda e, t=t: e.tensor_scalar(
                            eq1_sb[:, t * 8:(t + 1) * 8], pslice(2560, 2568),
                            m1_sb[:, 0:1], None, OP.is_equal, OP.bypass), tag=f"gv{t}b")
                        prog.op("vector", lambda e, t=t: e.scalar_tensor_tensor(
                            comb_sb[:, t * 8:(t + 1) * 8], eq1_sb[:, t * 8:(t + 1) * 8], NEG,
                            pslice(2560, 2568), OP.mult, OP.add), tag=f"gv{t}c")
                        prog.op("vector", lambda e, t=t: e.reduce_max(
                            m2_sb[:], comb_sb[:, t * 8:(t + 1) * 8], axis=AX.X), tag=f"gv{t}d")
                        prog.op("vector", lambda e, t=t: e.tensor_scalar(
                            eq2_sb[:, t * 8:(t + 1) * 8], pslice(2560, 2568),
                            m2_sb[:, 0:1], None, OP.is_equal, OP.bypass), tag=f"gv{t}e")
                        gv = prog.op("vector", lambda e, t=t: e.tensor_tensor(
                            dcol_sb[:, t:t + 1], m1_sb[:], m2_sb[:], OP.subtract), tag=f"gv{t}f")
                        commit(gv, [BK[5]], [])
                        comb_parts.append(gv)
                    sig = prog.op("scalar", lambda e: e.activation(w1c_sb[:], dcol_sb[:], AF.Sigmoid),
                                  deps=[comb_parts[-1]], tag="sig")
                    vw2 = prog.op("vector", lambda e: e.tensor_scalar(
                        w2c_sb[:], w1c_sb[:], -1.0, 1.0, OP.mult, OP.add), deps=[sig], tag="vw2")
                    comb_done = []
                    for t in range(NT):
                        prog.op("vector",
                                lambda e, t=t: e.tensor_scalar(
                                    eq1_sb[:, t * 8:(t + 1) * 8], eq1_sb[:, t * 8:(t + 1) * 8],
                                    w1c_sb[:, t:t + 1], None, OP.mult, OP.bypass),
                                deps=[vw2], tag=f"c1{t}")
                        cop = prog.op("vector",
                                      lambda e, t=t: e.scalar_tensor_tensor(
                                          comb_sb[:, t * 8:(t + 1) * 8], eq2_sb[:, t * 8:(t + 1) * 8],
                                          w2c_sb[:, t:t + 1], eq1_sb[:, t * 8:(t + 1) * 8],
                                          OP.mult, OP.add),
                                      tag=f"c2{t}")
                        comb_done.append(cop)
                    ct_done = []
                    for t in range(NT):
                        tp = prog.op("tensor",
                                     lambda e, t=t: e.matmul(PSB[6][0:8, 0:128],
                                                             comb_sb[:, t * 8:(t + 1) * 8],
                                                             ident_sb[:], start=True, stop=True,
                                                             is_transpose=True),
                                     deps=rw([], [BK[6]], extra=[comb_done[t]] + op_consts),
                                     tag=f"ctT{t}")
                        commit(tp, [], [BK[6]])
                        cp = prog.op("vector",
                                     lambda e, t=t: e.tensor_copy(combT_sb[:, t * 128:(t + 1) * 128],
                                                                  PSB[6][0:8, 0:128]),
                                     deps=[tp], tag=f"ctC{t}")
                        commit(cp, [BK[6]], [])
                        ct_done.append(cp)

                if PSTAGE >= 7:
                    kh_ready = vh_ready = None
                    prev_chain_last = None
                    prev_ln1 = None
                    prev_vdc = None
                    for gi, outbf in enumerate([khT_bf, vhT_bf]):
                        off = 0 if gi == 0 else 16
                        evs = {}
                        for nm, wT in (("pa", wgT_sb), ("pb", wkvhT_sb)):
                            def smm(e, wT=wT, off=off):
                                for cc in range(4):
                                    for ec in range(16):
                                        inst = e.matmul(
                                            PSB[1 + cc][0:16, 0:512],
                                            wT[:, ec * 32 + off: ec * 32 + off + 16],
                                            xnT_sb[:, ec * N + cc * 512: ec * N + cc * 512 + 512],
                                            start=(ec == 0), stop=(ec == 15))
                                return inst
                            extra = dep_xnt + [wgtp, kvhtp]
                            if prev_chain_last is not None:
                                extra = extra + [prev_chain_last, prev_ln1, prev_vdc]
                            mm = prog.op("tensor", smm,
                                         deps=rw([], banks(512, 2560), extra=extra),
                                         tag=f"sm{gi}{nm}")
                            for b in banks(512, 2560):
                                b.on_write(mm)
                            ev = None
                            for cc in range(4):
                                ev = prog.op("vector",
                                             lambda e, nm=nm, cc=cc: e.tensor_copy(
                                                 G[nm][:, cc * 512:(cc + 1) * 512],
                                                 PSB[1 + cc][0:16, 0:512]),
                                             deps=([mm] if cc == 0 else []),
                                             tag=f"sme{gi}{nm}{cc}")
                                BK[1 + cc].on_read(ev)
                            evs[nm] = ev
                        small_evac = {"pa": evs["pa"], "pb": evs["pb"]}
                        gnm, pre = "pa", "pb"
                        sab = prog.op("scalar",
                                      lambda e, gnm=gnm: e.activation(G["tmpa"][:], G[gnm][:], AF.Abs),
                                      deps=[small_evac[gnm]] +
                                           ([prev_chain_last] if prev_chain_last else []),
                                      tag=f"abs{gi}")
                        prog.op("scalar", lambda e: e.activation(G["tmpa"][:], G["tmpa"][:],
                                                                 AF.Exp, scale=-1.0), tag=f"ex{gi}")
                        ln1 = prog.op("scalar", lambda e: e.activation(G["tmpa"][:], G["tmpa"][:],
                                                                       AF.Ln, bias=1.0), tag=f"ln{gi}")
                        vmin = prog.op("vector",
                                       lambda e, gnm=gnm: e.tensor_scalar(
                                           G["lgs"][:], G[gnm][:], 0.0, None, OP.min, OP.bypass),
                                       deps=[small_evac[gnm]] +
                                            ([prev_chain_last] if prev_chain_last else []),
                                       tag=f"mn{gi}")
                        vsub = prog.op("vector", lambda e: e.tensor_tensor(
                            G["lgs"][:], G["lgs"][:], G["tmpa"][:], OP.subtract),
                            deps=[ln1], tag=f"ls{gi}")
                        prog.op("vector", lambda e: e.memset(G["tmpb"][:, 0:1], 0.0), tag=f"z{gi}")
                        prog.op("vector", lambda e: e.tensor_copy(G["tmpb"][:, 1:N],
                                                                  G["lgs"][:, 0:N - 1]), tag=f"sh{gi}")
                        fsc = prog.op("vector", lambda e: e.tensor_tensor_scan(
                            G["Fx"][:], G["tmpb"][:], G["tmpb"][:], 0.0, OP.add, OP.bypass),
                            tag=f"fs{gi}")
                        sef = prog.op("scalar", lambda e: e.activation(G["tmpa"][:], G["Fx"][:], AF.Exp),
                                      deps=[fsc, vsub], tag=f"ef{gi}")
                        vdc = prog.op("vector",
                                      lambda e, pre=pre: e.tensor_tensor(
                                          G["tmpb"][:], G[pre][:], G["tmpa"][:], OP.mult),
                                      deps=[sef, small_evac[pre]], tag=f"dc{gi}")
                        vcs = prog.op("vector",
                                      lambda e: e.tensor_tensor_scan(
                                          G["lgs"][:], G["tmpb"][:], G["tmpb"][:], 0.0,
                                          OP.add, OP.bypass),
                                      tag=f"cs{gi}")
                        if gi == 0:
                            cpy = prog.op("vector",
                                          lambda e, outbf=outbf: e.tensor_scalar(
                                              outbf[:], G["lgs"][:], float(1.0 / np.sqrt(D)),
                                              None, OP.mult, OP.bypass),
                                          tag=f"obf{gi}")
                        else:
                            cpy = prog.op("vector",
                                          lambda e, outbf=outbf: e.tensor_copy(outbf[:], G["lgs"][:]),
                                          tag=f"obf{gi}")
                        if gi == 0:
                            kh_ready = cpy
                        else:
                            vh_ready = cpy
                        prev_chain_last = cpy
                        prev_ln1 = ln1
                        prev_vdc = vdc
                    vh_tp = prog.op("sync",
                                    lambda e: e.dma_start_transpose(
                                        vhtok_sb[:].rearrange("p (c j) -> p c j", c=NT), vhT_bf[:]),
                                    deps=[vh_ready], dma=True, tag="vhtp")
                    vh_tp = prog.op("vector",
                                    lambda e: e.tensor_copy(vhtok32_sb[:], vhtok_sb[:]),
                                    deps=[vh_tp], tag="vhtp32")

                p_last_preprep = prog.fence_all()

                prep_done = {}
                RPREP = [Region() for _ in range(2)]
                if PSTAGE >= 7:
                    for ex in range(NEXP):
                        for ec in range(16):
                            for mh in range(2):
                                bpp = (ec * 2 + mh) % 2
                                lw = prog.op("gpsimd",
                                             lambda e, ex=ex, ec=ec, mh=mh, b=bpp: e.dma_start(
                                                 out=pw_sb[b][:],
                                                 in_=w0_d[ex][ec * 128:(ec + 1) * 128,
                                                              mh * 2048:(mh + 1) * 2048]),
                                             deps=rw([], [RPREP[bpp]]), dma=True,
                                             tag=f"p0l{ex}_{ec}_{mh}")
                                commit(lw, [], [RPREP[bpp]])
                                sw = prog.op("gpsimd",
                                             lambda e, ex=ex, ec=ec, mh=mh, b=bpp: e.dma_start(
                                                 out=w0b_s[ex][mh * 8:(mh + 1) * 8, :,
                                                               ec * 256:(ec + 1) * 256]
                                                 .rearrange("mb p m -> p mb m"),
                                                 in_=pw_sb[b][:].rearrange(
                                                     "p (mb m) -> p mb m", mb=8)),
                                             deps=[lw], dma=True, tag=f"p0s{ex}_{ec}_{mh}")
                                commit(sw, [RPREP[bpp]], [])
                        for ec in range(16):
                            for mh in range(2):
                                bpp = (ec * 2 + mh) % 2
                                lw = prog.op("gpsimd",
                                             lambda e, ex=ex, ec=ec, mh=mh, b=bpp: e.dma_start(
                                                 out=pw_sb[b][:],
                                                 in_=w1_d[ex][ec * 128:(ec + 1) * 128,
                                                              mh * 2048:(mh + 1) * 2048]),
                                             deps=rw([], [RPREP[bpp]]), dma=True,
                                             tag=f"p1l{ex}_{ec}_{mh}")
                                commit(lw, [], [RPREP[bpp]])
                                sw = prog.op("gpsimd",
                                             lambda e, ex=ex, ec=ec, mh=mh, b=bpp: e.dma_start(
                                                 out=w1b_s[ex][mh * 8:(mh + 1) * 8, :,
                                                               ec * 256:(ec + 1) * 256]
                                                 .rearrange("mb p m -> p mb m"),
                                                 in_=pw_sb[b][:].rearrange(
                                                     "p (mb m) -> p mb m", mb=8)),
                                             deps=[lw], dma=True, tag=f"p1s{ex}_{ec}_{mh}")
                                commit(sw, [RPREP[bpp]], [])
                        for mc in range(32):
                            bpp = mc % 2
                            lw = prog.op("gpsimd",
                                         lambda e, ex=ex, mc=mc, b=bpp: e.dma_start(
                                             out=pw_sb[b][:],
                                             in_=wo_d[ex][mc * 128:(mc + 1) * 128, :]),
                                         deps=rw([], [RPREP[bpp]]), dma=True,
                                         tag=f"pol{ex}_{mc}")
                            commit(lw, [], [RPREP[bpp]])
                            sw = prog.op("gpsimd",
                                         lambda e, ex=ex, mc=mc, b=bpp: e.dma_start(
                                             out=wob_s[ex][:, :, mc * 128:(mc + 1) * 128]
                                             .rearrange("dt p d -> p dt d"),
                                             in_=pw_sb[b][:].rearrange(
                                                 "p (dt d) -> p dt d", dt=16)),
                                         deps=[lw], dma=True, tag=f"pos{ex}_{mc}")
                            commit(sw, [RPREP[bpp]], [])
                        prep_done[ex] = prog.fence("gpsimd")

                if PSTAGE < 7:
                    kh_ready = kpT_ready = vh_tp = None
                    if PSTAGE < 5:
                        kpT_ready = None
                    p_last_preprep = prog.fence_all()
                p_last = p_last_preprep
                for o in (p_last + [kh_ready, kpT_ready, vh_tp] if PSTAGE >= 7 else p_last):
                    o.needed = True
                emitted = prog.emit_phase(block, emitted)
                for b in BK:
                    b.w = None
                    b.readers = []

            if "A" not in PHASES:
                return nc
            with ExitStack() as awctx:
                cosT_sb = sbt(awctx, "cosT_sb", [64, N], BF16)
                sinT_sb = sbt(awctx, "sinT_sb", [64, N], BF16)
                mask4_sb = sbt(awctx, "mask4_sb", [128, 2048], BF16)
                maskT4_sb = sbt(awctx, "maskT4_sb", [128, 2048], BF16)
                d16_sb = sbt(awctx, "d16_sb", [16, 16 * 128], BF16)
                wqh_sb = [sbt(awctx, f"wqh{i}", [128, E], BF16) for i in range(2)]
                woh_sb = [sbt(awctx, f"woh{i}", [128, HALF], BF16) for i in range(2)]
                qh_sb = [sbt(awctx, f"qh{i}", [128, GS], BF16) for i in range(2)]
                khK_sb = [sbt(awctx, f"khK{i}", [128, N], BF16) for i in range(2)]
                vh_sb = [sbt(awctx, f"vhb{i}", [128, NT * 128], BF16) for i in range(2)]
                pt_sb = [sbt(awctx, f"ptb{i}", [128, GS], BF16) for i in range(2)]
                stmp_sb = [sbt(awctx, f"stmp{i}", [128, GS], F32) for i in range(2)]
                oall_sb = sbt(awctx, "oall_sb", [128, H * GS], BF16)
                trash_sb = sbt(awctx, "trash_sb", [128, 2048], BF16)
                axb_sb = [sbt(awctx, f"axb{i}", [128, HALF], F32) for i in range(2)]
                negmx_sb = sbt(awctx, "negmx_sb", [128, 4], F32)
                nmx4_sb = sbt(awctx, "nmx4_sb", [128, 4], F32)
                sstage_sb = sbt(awctx, "sstage_sb", [128, 2048], F32)
                sume4_sb = sbt(awctx, "sume4_sb", [128, 4], F32)
                sume_sb = sbt(awctx, "sume_sb", [128, 4], F32)
                lsum_sb = sbt(awctx, "lsum_sb", [128, 4], F32)
                lrow2_sb = sbt(awctx, "lrow2_sb", [1, 512], F32)
                lsb_sb = sbt(awctx, "lsb_sb", [128, 512], F32)

                RWQ = [Region() for _ in range(2)]
                RWOH = [Region() for _ in range(2)]
                RQH = [Region() for _ in range(2)]
                RKH = [Region() for _ in range(2)]
                RVH = [Region() for _ in range(2)]
                RPT = [Region() for _ in range(2)]
                RST = [Region() for _ in range(2)]
                ROA = [Region() for _ in range(H)]
                RAXB = [Region() for _ in range(2)]
                RLS = Region()
                dep_P = p_last

                aconst_ops = []

                def lda(dst, src):
                    o = prog.op("scalar",
                                lambda e, d=dst, s=src: e.dma_start(out=d, in_=s),
                                deps=dep_P, dma=True, tag="alda")
                    aconst_ops.append(o)
                    return o
                lda(cosT_sb[:], cosT_d)
                lda(sinT_sb[:], sinT_d)
                lda(mask4_sb[:], mask4_d)
                lda(maskT4_sb[:], maskT4_d)
                lda(d16_sb[:], d16_d)

                for g in range(NG):
                    mend = GS * (g + 1)
                    nch = g + 1
                    for h in range(H):
                        ih = g * H + h
                        bwq = ih % 2
                        wql = prog.op("scalar",
                                      lambda e, h=h, b=bwq: e.dma_start(out=wqh_sb[b][:],
                                                                        in_=wqT_s[h]),
                                      deps=rw([], [RWQ[bwq]],
                                              extra=[wq_st[h]] +
                                                    (dep_P if ih == 0 else [])),
                                      dma=True, tag=f"wql{g}_{h}")
                        commit(wql, [], [RWQ[bwq]])

                        def qmm(e, g=g, b=bwq):
                            for ec in range(16):
                                inst = e.matmul(pslice(0, GS),
                                                wqh_sb[b][:, ec * 128:(ec + 1) * 128],
                                                xnT_sb[:, ec * N + g * GS: ec * N + g * GS + GS],
                                                start=(ec == 0), stop=(ec == 15))
                            return inst
                        qp = prog.op("tensor", qmm,
                                     deps=rw([RWQ[bwq]], [BK[0]],
                                             extra=(dep_P if ih == 0 else [])),
                                     tag=f"qp{g}_{h}")
                        commit(qp, [RWQ[bwq]], [BK[0]])

                        bq = ih % 2

                        prog.op("vector", lambda e, g=g: e.tensor_tensor(
                            stmp_sb[0][0:64, :], PSB[0][0:64, 0:GS],
                            cosT_sb[:, g * GS:(g + 1) * GS], OP.mult),
                            deps=rw([BK[0]], [RQH[bq], RST[0], RST[1]],
                                    extra=[qp] + (aconst_ops if ih == 0 else [])),
                            tag=f"rq{g}_{h}a")
                        prog.op("vector", lambda e, g=g: e.tensor_tensor(
                            stmp_sb[1][0:64, :], PSB[0][64:128, 0:GS],
                            sinT_sb[:, g * GS:(g + 1) * GS], OP.mult), tag=f"rq{g}_{h}b")
                        prog.op("vector", lambda e, b=bq: e.tensor_tensor(
                            qh_sb[b][0:64, :], stmp_sb[0][0:64, :], stmp_sb[1][0:64, :],
                            OP.subtract), tag=f"rq{g}_{h}c")
                        prog.op("vector", lambda e, g=g: e.tensor_tensor(
                            stmp_sb[0][0:64, :], PSB[0][64:128, 0:GS],
                            cosT_sb[:, g * GS:(g + 1) * GS], OP.mult), tag=f"rq{g}_{h}d")
                        prog.op("vector", lambda e, g=g: e.tensor_tensor(
                            stmp_sb[1][0:64, :], PSB[0][0:64, 0:GS],
                            sinT_sb[:, g * GS:(g + 1) * GS], OP.mult), tag=f"rq{g}_{h}e")
                        rq = prog.op("vector", lambda e, b=bq: e.tensor_tensor(
                            qh_sb[b][64:128, :], stmp_sb[0][0:64, :], stmp_sb[1][0:64, :],
                            OP.add), tag=f"rq{g}_{h}f")
                        commit(rq, [BK[0]], [RQH[bq]])
                        commit(rq, [], [RST[0]])
                        commit(rq, [], [RST[1]])

                        def khb2(e, nch=nch, h=h):
                            for c in range(nch):
                                inst = e.matmul(pslice(512 + c * 512, 512 + (c + 1) * 512),
                                                d16_sb[:, h * 128:(h + 1) * 128],
                                                khT_bf[:, c * 512:(c + 1) * 512],
                                                start=True, stop=True)
                            return inst
                        kb = prog.op("tensor", khb2,
                                     deps=rw([], banks(512, 512 + mend),
                                             extra=([kh_ready] + aconst_ops if ih == 0 else [])),
                                     tag=f"kb{g}_{h}")
                        for b in banks(512, 512 + mend):
                            b.on_write(kb)
                        bk2 = ih % 2
                        vkh = None
                        for c in range(nch):
                            vkh = prog.op("vector",
                                          lambda e, c=c, b=bk2: e.tensor_tensor(
                                              khK_sb[b][:, c * 512:(c + 1) * 512],
                                              kpT_sb[:, c * 512:(c + 1) * 512],
                                              PSB[1 + c][:, 0:512], OP.mult),
                                          deps=(rw([RKH[bk2]], [],
                                                extra=[kb] + ([kpT_ready] if ih == 0 else []))
                                                if c == 0 else [kb]),
                                          tag=f"vkh{g}_{h}_{c}")
                            BK[1 + c].on_read(vkh)
                        commit(vkh, [], [RKH[bk2]])

                        def vhmul(e, g=g, h=h, b=bk2):
                            for mt in range(4 * (g + 1)):
                                inst = e.tensor_scalar(
                                    vh_sb[b][:, mt * 128:(mt + 1) * 128],
                                    vp_sb[:, mt * 128:(mt + 1) * 128],
                                    vhtok32_sb[:, mt * 16 + h: mt * 16 + h + 1], None,
                                    OP.mult, OP.bypass)
                            return inst
                        vvh = prog.op("vector", vhmul,
                                      deps=rw([RVH[bk2]], [],
                                              extra=([vh_tp] if ih == 0 else [])),
                                      tag=f"vvh{g}_{h}")
                        commit(vvh, [], [RVH[bk2]])

                        exp_ops = []
                        for t in range(4):
                            def smm(e, t=t, nch=nch, b=bq, bk=bk2):
                                for c in range(nch):
                                    inst = e.matmul(pslice(512 + c * 512, 512 + (c + 1) * 512),
                                                    qh_sb[b][:, t * 128:(t + 1) * 128],
                                                    khK_sb[bk][:, c * 512:(c + 1) * 512],
                                                    start=True, stop=True)
                                return inst
                            s1 = prog.op("tensor", smm,
                                         deps=rw([RQH[bq], RKH[bk2]], banks(512, 512 + mend),
                                                 extra=[rq, vkh]),
                                         tag=f"s1_{g}_{h}_{t}")
                            for b in banks(512, 512 + mend):
                                b.on_write(s1)
                            commit(s1, [RQH[bq]], [])
                            commit(s1, [RKH[bk2]], [])

                            prog.op("vector", lambda e, t=t, g=g: e.tensor_tensor(
                                PSB[1 + g][:, 0:512], PSB[1 + g][:, 0:512],
                                mask4_sb[:, t * 512:(t + 1) * 512], OP.add),
                                deps=[s1] + (aconst_ops if ih == 0 and t == 0 else []),
                                tag=f"vmA{g}_{h}_{t}")
                            for c in range(nch):
                                vm = prog.op("vector",
                                             lambda e, c=c: e.reduce_max(
                                                 nmx4_sb[:, c:c + 1], PSB[1 + c][:, 0:512],
                                                 axis=AX.X),
                                             deps=([s1] if c > 0 else []),
                                             tag=f"vmc{g}_{h}_{t}_{c}")
                                BK[1 + c].on_read(vm)
                                BK[1 + g].on_write(vm) if False else None
                            vm = prog.op("vector", lambda e, t=t, nch=nch: e.reduce_max(
                                negmx_sb[:, t:t + 1], nmx4_sb[:, 0:nch], axis=AX.X,
                                negate=True), tag=f"vm{g}_{h}_{t}")
                            BK[1 + g].on_write(vm)
                            se = None
                            for c in range(nch):
                                vcp = prog.op("vector",
                                              lambda e, c=c: e.tensor_copy(
                                                  sstage_sb[:, c * 512:(c + 1) * 512],
                                                  PSB[1 + c][:, 0:512]),
                                              deps=[vm], tag=f"sec{g}_{h}_{t}_{c}")
                                BK[1 + c].on_read(vcp)
                                se = prog.op("scalar",
                                             lambda e, t=t, c=c: e.activation(
                                                 trash_sb[:, c * 512:(c + 1) * 512],
                                                 sstage_sb[:, c * 512:(c + 1) * 512],
                                                 AF.Exp, bias=negmx_sb[:, t:t + 1],
                                                 accum_out=sume4_sb[:, c:c + 1]),
                                             deps=[vcp], tag=f"se{g}_{h}_{t}_{c}")
                            vsm = prog.op("vector",
                                          lambda e, t=t, nch=nch: e.reduce_sum(
                                              sume_sb[:, t:t + 1], sume4_sb[:, 0:nch],
                                              axis=AX.X),
                                          deps=[se], tag=f"vsm{g}_{h}_{t}")
                            sl = prog.op("scalar",
                                         lambda e, t=t: e.activation(
                                             lsum_sb[:, t:t + 1], sume_sb[:, t:t + 1], AF.Ln),
                                         deps=[vsm], tag=f"sl{g}_{h}_{t}")
                            vl = prog.op("vector",
                                         lambda e, t=t: e.tensor_tensor(
                                             lsum_sb[:, t:t + 1], lsum_sb[:, t:t + 1],
                                             negmx_sb[:, t:t + 1], OP.subtract),
                                         deps=[sl], tag=f"vl{g}_{h}_{t}")
                            tt = prog.op("tensor",
                                         lambda e, t=t: e.matmul(
                                             PSB[6][0:1, t * 128:(t + 1) * 128],
                                             lsum_sb[:, t:t + 1], ident_sb[:],
                                             start=True, stop=True, is_transpose=True),
                                         deps=rw([], [BK[6]], extra=[vl]),
                                         tag=f"tt{g}_{h}_{t}")
                            commit(tt, [], [BK[6]]) if t == 0 else BK[6].on_write(tt)
                            exp_ops.append((se, tt))
                        slr = prog.op("vector",
                                      lambda e: e.tensor_copy(lrow2_sb[:], PSB[6][0:1, 0:512]),
                                      deps=rw([], [RLS], extra=[exp_ops[3][1]]),
                                      tag=f"slr{g}_{h}")
                        BK[6].on_read(slr)
                        commit(slr, [], [RLS])
                        tlb = prog.op("tensor",
                                      lambda e: e.matmul(pslice(3072, 3584), ones1_sb[:],
                                                         lrow2_sb[:], start=True, stop=True),
                                      deps=rw([RLS], [BK[6]], extra=[slr]), tag=f"tlb{g}_{h}")
                        commit(tlb, [RLS], [BK[6]])
                        slb = prog.op("vector",
                                      lambda e: e.tensor_copy(lsb_sb[:], pslice(3072, 3584)),
                                      deps=[tlb], tag=f"slb{g}_{h}")
                        BK[6].on_read(slb)
                        RLS.on_write(slb)

                        pv_last = None
                        for mt in range(4 * (g + 1)):
                            pb = mt % 2
                            cols = (2560, 3072) if pb == 0 else (3072, 3584)
                            bnk = BK[5] if pb == 0 else BK[6]
                            p2 = prog.op("tensor",
                                         lambda e, mt=mt, bk=bk2, b=bq, cols=cols: e.matmul(
                                             pslice(*cols),
                                             khK_sb[bk][:, mt * 128:(mt + 1) * 128],
                                             qh_sb[b][:, 0:GS], start=True, stop=True),
                                         deps=rw([RQH[bq], RKH[bk2]], [bnk],
                                                 extra=([slb] if pb == 1 else [])),
                                         tag=f"p2_{g}_{h}_{mt}")
                            commit(p2, [RQH[bq]], [])
                            commit(p2, [RKH[bk2]], [])
                            bnk.on_write(p2)
                            dv = [p2]
                            if mt >= 4 * g:
                                v = mt - 4 * g
                                mk = prog.op("vector",
                                             lambda e, v=v, cols=cols: e.tensor_tensor(
                                                 pslice(*cols), pslice(*cols),
                                                 maskT4_sb[:, v * 512:(v + 1) * 512], OP.add),
                                             deps=[p2], tag=f"mk{g}_{h}_{mt}")
                                bnk.on_read(mk)
                                bnk.on_write(mk)
                                dv = [mk]
                            vs = prog.op("vector",
                                         lambda e, pb=pb, cols=cols: e.tensor_tensor(
                                             stmp_sb[pb][:], pslice(*cols), lsb_sb[:],
                                             OP.subtract),
                                         deps=rw([RST[pb]], [], extra=dv + [slb]),
                                         tag=f"vs{g}_{h}_{mt}")
                            bnk.on_read(vs)
                            commit(vs, [], [RST[pb]])
                            sx = prog.op("scalar",
                                         lambda e, pb=pb: e.activation(
                                             pt_sb[pb][:], stmp_sb[pb][:], AF.Exp),
                                         deps=rw([RST[pb], RPT[pb]], []), tag=f"sx{g}_{h}_{mt}")
                            commit(sx, [RST[pb]], [])
                            commit(sx, [], [RPT[pb]])
                            pv = prog.op("tensor",
                                         lambda e, mt=mt, pb=pb, bk=bk2, g=g: e.matmul(
                                             pslice(3584, 4096),
                                             vh_sb[bk][:, mt * 128:(mt + 1) * 128],
                                             pt_sb[pb][:], start=(mt == 0),
                                             stop=(mt == 4 * (g + 1) - 1)),
                                         deps=rw([RPT[pb], RVH[bk2]], [BK[7]] if mt == 0 else [],
                                                 extra=[sx, vvh]),
                                         tag=f"pv{g}_{h}_{mt}")
                            commit(pv, [RPT[pb]], [])
                            commit(pv, [RVH[bk2]], [])
                            if mt == 0:
                                commit(pv, [], [BK[7]])
                            else:
                                BK[7].on_write(pv)
                            pv_last = pv
                        so = prog.op("vector",
                                     lambda e, h=h: e.tensor_copy(oall_sb[:, h * GS:(h + 1) * GS],
                                                                  pslice(3584, 4096)),
                                     deps=rw([], [ROA[h]], extra=[pv_last]), tag=f"so{g}_{h}")
                        BK[7].on_read(so)
                        commit(so, [], [ROA[h]])

                    evacs = []
                    for eh in range(2):
                        for h in range(H):
                            iwh = (g * 2 + eh) * H + h
                            bw = iwh % 2
                            wol = prog.op("scalar",
                                          lambda e, h=h, eh=eh, b=bw: e.dma_start(
                                              out=woh_sb[b][:],
                                              in_=woutT_s[h][:, eh * HALF:(eh + 1) * HALF]),
                                          deps=rw([], [RWOH[bw]],
                                                  extra=wout_st_by_h[h]), dma=True,
                                          tag=f"wol{g}_{eh}_{h}")
                            commit(wol, [], [RWOH[bw]])
                            for t in range(4):
                                def wmm(e, t=t, h=h, b=bw):
                                    for c2 in range(2):
                                        inst = e.matmul(
                                            pslice(t * 1024 + c2 * 512, t * 1024 + (c2 + 1) * 512),
                                            oall_sb[:, h * GS + t * 128: h * GS + t * 128 + 128],
                                            woh_sb[b][:, c2 * 512:(c2 + 1) * 512],
                                            start=(h == 0), stop=(h == H - 1))
                                    return inst
                                wm = prog.op("tensor", wmm,
                                             deps=rw([RWOH[bw], ROA[h]],
                                                     banks(t * 1024, (t + 1) * 1024) if h == 0 else [],
                                                     extra=[wol]),
                                             tag=f"wm{g}_{eh}_{h}_{t}")
                                commit(wm, [RWOH[bw]], [])
                                commit(wm, [ROA[h]], [])
                                if h == 0:
                                    for b in banks(t * 1024, (t + 1) * 1024):
                                        b.on_write(wm)
                                else:
                                    for b in banks(t * 1024, (t + 1) * 1024):
                                        b.on_write(wm)
                        for t in range(4):
                            ba = (eh * 4 + t) % 2
                            ev = prog.op("vector",
                                         lambda e, t=t, b=ba: e.tensor_copy(
                                             axb_sb[b][:, 0:512],
                                             pslice(t * 1024, t * 1024 + 512)),
                                         deps=rw([], [RAXB[ba]],
                                                 extra=[prog.last["tensor"]]),
                                         tag=f"ev{g}_{eh}_{t}a")
                            BK[2 * t].on_read(ev)
                            ev = prog.op("vector",
                                         lambda e, t=t, b=ba: e.tensor_copy(
                                             axb_sb[b][:, 512:1024],
                                             pslice(t * 1024 + 512, (t + 1) * 1024)),
                                         tag=f"ev{g}_{eh}_{t}b")
                            BK[2 * t + 1].on_read(ev)
                            commit(ev, [], [RAXB[ba]])
                            st = prog.op("scalar",
                                         lambda e, g=g, eh=eh, t=t, b=ba: e.dma_start(
                                             out=attn_s[g * GS + t * 128: g * GS + (t + 1) * 128,
                                                        eh * HALF:(eh + 1) * HALF],
                                             in_=axb_sb[b][:]),
                                         deps=[ev], dma=True, tag=f"st{g}_{eh}_{t}")
                            commit(st, [RAXB[ba]], [])
                            evacs.append(ev)
                a_last = prog.fence_all()
                for o in a_last:
                    o.needed = True
                emitted = prog.emit_phase(block, emitted)
                for b in BK:
                    b.w = None
                    b.readers = []

        if "F" not in PHASES:
            return nc
        with ExitStack() as fctx:
            inter_sb = sbt(fctx, "inter_sb", [128, 32 * HALF], BF16)
            xnh_sb = sbt(fctx, "xnh_sb", [128, 16 * HALF], BF16)
            w0_sb = [sbt(fctx, f"w0b{i}", [128, 16 * 256], BF16) for i in range(2)]
            w1_sb = [sbt(fctx, f"w1b{i}", [128, 16 * 256], BF16) for i in range(2)]
            wo_sb = [sbt(fctx, f"wob{i}", [128, 32 * 128], BF16) for i in range(2)]
            cbc_sb = sbt(fctx, "cbc_sb", [128, HALF], F32)
            silu_sb = [sbt(fctx, f"silu{i}", [128, HALF], F32) for i in range(2)]
            wev_sb = sbt(fctx, "wev_sb", [128, HALF], F32)
            axf_sb = sbt(fctx, "axf_sb", [128, E], F32)
            xf_sb = sbt(fctx, "xf_sb", [128, E], F32)
            fcol_sb = sbt(fctx, "fcol_sb", [128, 16 * 128], F32)

            RXH = Region()
            RW0 = [Region() for _ in range(2)]
            RW1 = [Region() for _ in range(2)]
            RWO = [Region() for _ in range(2)]
            RCB = Region()
            RSI = [Region() for _ in range(2)]
            RIN = [Region() for _ in range(32)]
            RWE = Region()
            RFFS = Region()
            RAXF = Region()
            RXF = Region()
            RFC = Region()

            dep_A = a_last
            y_stores = []
            for half in range(2):
                def xnh_ld(e, half=half):
                    return [e.dma_start(
                        out=xnh_sb[:].rearrange("p (c n) -> p c n", c=16)[:, q * 8:(q + 1) * 8, :],
                        in_=xnT_s[q * 8:(q + 1) * 8, :, half * HALF:(half + 1) * HALF]
                        .rearrange("c p n -> p c n")) for q in range(2)]
                xl = prog.op("scalar", xnh_ld,
                             deps=rw([], [RXH], extra=(dep_A if half == 0 else [])),
                             dma=True, ndma=2, tag=f"xnh{half}")
                commit(xl, [], [RXH])
                for ex in range(NEXP):
                    iex = half * NEXP + ex
                    cb = prog.op("tensor",
                                 lambda e, ex=ex, half=half: [
                                     e.matmul(pslice(0, 512), d8_sb[:, ex * 128:(ex + 1) * 128],
                                              combT_sb[:, half * HALF: half * HALF + 512],
                                              start=True, stop=True),
                                     e.matmul(pslice(512, 1024), d8_sb[:, ex * 128:(ex + 1) * 128],
                                              combT_sb[:, half * HALF + 512: half * HALF + 1024],
                                              start=True, stop=True)][-1],
                                 deps=rw([], banks(0, 1024),
                                         extra=(dep_A if half == 0 and ex == 0 else [])),
                                 tag=f"cb{half}_{ex}")
                    for b in banks(0, 1024):
                        b.on_write(cb)
                    scb = prog.op("vector",
                                  lambda e: e.tensor_copy(cbc_sb[:, 0:512], pslice(0, 512)),
                                  deps=rw([], [RCB], extra=[cb]), tag=f"scb{half}_{ex}a")
                    BK[0].on_read(scb)
                    scb = prog.op("vector",
                                  lambda e: e.tensor_copy(cbc_sb[:, 512:1024], pslice(512, 1024)),
                                  tag=f"scb{half}_{ex}b")
                    BK[1].on_read(scb)
                    commit(scb, [], [RCB])

                    for mb in range(16):
                        bwb = mb % 2
                        l0 = prog.op("gpsimd",
                                     lambda e, ex=ex, mb=mb, b=bwb: e.dma_start(
                                         out=w0_sb[b][:], in_=w0b_s[ex][mb]),
                                     deps=rw([], [RW0[bwb]],
                                             extra=prep_done[ex] +
                                             (dep_A if iex == 0 and mb == 0 else [])),
                                     dma=True, tag=f"l0_{iex}_{mb}")
                        commit(l0, [], [RW0[bwb]])
                        l1 = prog.op("gpsimd",
                                     lambda e, ex=ex, mb=mb, b=bwb: e.dma_start(
                                         out=w1_sb[b][:], in_=w1b_s[ex][mb]),
                                     deps=rw([], [RW1[bwb]]), dma=True,
                                     tag=f"l1_{iex}_{mb}")
                        commit(l1, [], [RW1[bwb]])
                        for mi in range(2):
                            mt = mb * 2 + mi
                            par = mt % 2
                            c0 = 0 if par == 0 else 2048
                            c1 = 1024 if par == 0 else 3072

                            def h0mm(e, b=bwb, mi=mi, c0=c0):
                                for ec in range(16):
                                    for c2 in range(2):
                                        inst = e.matmul(
                                            pslice(c0 + c2 * 512, c0 + (c2 + 1) * 512),
                                            w0_sb[b][:, ec * 256 + mi * 128: ec * 256 + mi * 128 + 128],
                                            xnh_sb[:, ec * HALF + c2 * 512: ec * HALF + c2 * 512 + 512],
                                            start=(ec == 0), stop=(ec == 15))
                                return inst
                            extra0 = [l0, xl] + ([scb] if (par == 0) else [])
                            hm0 = prog.op("tensor", h0mm,
                                          deps=rw([RW0[bwb], RXH], banks(c0, c0 + 1024),
                                                  extra=extra0),
                                          tag=f"h0_{iex}_{mt}")
                            commit(hm0, [RW0[bwb]], [])
                            commit(hm0, [RXH], [])
                            for b in banks(c0, c0 + 1024):
                                b.on_write(hm0)

                            def h1mm(e, b=bwb, mi=mi, c1=c1):
                                for ec in range(16):
                                    for c2 in range(2):
                                        inst = e.matmul(
                                            pslice(c1 + c2 * 512, c1 + (c2 + 1) * 512),
                                            w1_sb[b][:, ec * 256 + mi * 128: ec * 256 + mi * 128 + 128],
                                            xnh_sb[:, ec * HALF + c2 * 512: ec * HALF + c2 * 512 + 512],
                                            start=(ec == 0), stop=(ec == 15))
                                return inst
                            hm1 = prog.op("tensor", h1mm,
                                          deps=rw([RW1[bwb], RXH], banks(c1, c1 + 1024),
                                                  extra=[l1]),
                                          tag=f"h1_{iex}_{mt}")
                            commit(hm1, [RW1[bwb]], [])
                            commit(hm1, [RXH], [])
                            for b in banks(c1, c1 + 1024):
                                b.on_write(hm1)
                            sic = prog.op("vector",
                                          lambda e, par=par, c0=c0: e.tensor_copy(
                                              silu_sb[par][:, 0:512],
                                              pslice(c0, c0 + 512)),
                                          deps=rw([], [RSI[par]], extra=[hm0]),
                                          tag=f"sic_{iex}_{mt}a")
                            BK[c0 // 512].on_read(sic)
                            sic = prog.op("vector",
                                          lambda e, par=par, c0=c0: e.tensor_copy(
                                              silu_sb[par][:, 512:1024],
                                              pslice(c0 + 512, c0 + 1024)),
                                          tag=f"sic_{iex}_{mt}b")
                            BK[c0 // 512 + 1].on_read(sic)
                            si = prog.op("scalar",
                                         lambda e, par=par: e.activation(
                                             silu_sb[par][:], silu_sb[par][:], AF.Silu),
                                         deps=[sic], tag=f"si_{iex}_{mt}")
                            commit(si, [], [RSI[par]])
                            v1 = prog.op("vector",
                                         lambda e, par=par, c1=c1: e.tensor_tensor(
                                             silu_sb[par][:, 0:512], silu_sb[par][:, 0:512],
                                             pslice(c1, c1 + 512), OP.mult),
                                         deps=rw([RSI[par]], [], extra=[si, hm1]),
                                         tag=f"v1_{iex}_{mt}a")
                            BK[c1 // 512].on_read(v1)
                            v1 = prog.op("vector",
                                         lambda e, par=par, c1=c1: e.tensor_tensor(
                                             silu_sb[par][:, 512:1024], silu_sb[par][:, 512:1024],
                                             pslice(c1 + 512, c1 + 1024), OP.mult),
                                         tag=f"v1_{iex}_{mt}b")
                            BK[c1 // 512 + 1].on_read(v1)
                            commit(v1, [RSI[par]], [RSI[par]])
                            v2 = prog.op("vector",
                                         lambda e, mt=mt, par=par: e.tensor_tensor(
                                             inter_sb[:, mt * HALF:(mt + 1) * HALF],
                                             silu_sb[par][:], cbc_sb[:], OP.mult),
                                         deps=rw([RSI[par], RCB], [RIN[mt]]),
                                         tag=f"v2_{iex}_{mt}")
                            commit(v2, [RSI[par]], [])
                            commit(v2, [RCB], [])
                            commit(v2, [], [RIN[mt]])

                    first_acc_dep = [RFFS.w] if RFFS.w is not None else []
                    for dt in range(16):
                        bwo = dt % 2
                        lwo = prog.op("gpsimd",
                                      lambda e, ex=ex, dt=dt, b=bwo: e.dma_start(
                                          out=wo_sb[b][:], in_=wob_s[ex][dt]),
                                      deps=rw([], [RWO[bwo]]), dma=True,
                                      tag=f"lwo{iex}_{dt}")
                        commit(lwo, [], [RWO[bwo]])
                        pc = (dt % 2) * 1024

                        def womm(e, b=bwo, pc=pc):
                            for mt in range(32):
                                for c2 in range(2):
                                    inst = e.matmul(
                                        pslice(pc + c2 * 512, pc + (c2 + 1) * 512),
                                        wo_sb[b][:, mt * 128:(mt + 1) * 128],
                                        inter_sb[:, mt * HALF + c2 * 512: mt * HALF + c2 * 512 + 512],
                                        start=(mt == 0), stop=(mt == 31))
                                return_inst = inst
                            return return_inst
                        wdeps = rw([RWO[bwo]] + RIN, banks(pc, pc + 1024), extra=[lwo])
                        wm = prog.op("tensor", womm, deps=wdeps, tag=f"wo{iex}_{dt}")
                        commit(wm, [RWO[bwo]], [])
                        for r in RIN:
                            r.on_read(wm)
                        for b in banks(pc, pc + 1024):
                            b.on_write(wm)
                        we = prog.op("vector",
                                     lambda e, pc=pc: e.tensor_copy(wev_sb[:, 0:512],
                                                                    pslice(pc, pc + 512)),
                                     deps=rw([], [RWE], extra=[wm]), tag=f"we{iex}_{dt}a")
                        BK[pc // 512].on_read(we)
                        we = prog.op("vector",
                                     lambda e, pc=pc: e.tensor_copy(wev_sb[:, 512:1024],
                                                                    pslice(pc + 512, pc + 1024)),
                                     tag=f"we{iex}_{dt}b")
                        BK[pc // 512 + 1].on_read(we)
                        commit(we, [], [RWE])
                        ac = prog.op("gpsimd",
                                     lambda e, half=half, ex=ex, dt=dt: e.dma_start(
                                         out=ffnT_s[dt * 128:(dt + 1) * 128,
                                                    half * HALF:(half + 1) * HALF],
                                         in_=wev_sb[:],
                                         **({} if ex == 0 else {"accum_op": OP.add})),
                                     deps=[we] + (first_acc_dep if dt == 0 else []),
                                     dma=True, tag=f"ac{iex}_{dt}")
                        commit(ac, [RWE], [])
                        if dt == 15:
                            RFFS.on_write(ac)

                ffn_done = RFFS.w
                for t in range(8):
                    row0 = half * HALF + t * 128
                    def fcol_ld(e, row0=row0):
                        return [e.dma_start(
                            out=fcol_sb[:].rearrange("p (c n) -> p c n", c=16)
                            [:, q * 8:(q + 1) * 8, :],
                            in_=ffnT_s[q * 1024:(q + 1) * 1024, row0:row0 + 128]
                            .rearrange("(c p) n -> p c n", p=128)) for q in range(2)]
                    fl = prog.op("scalar", fcol_ld,
                                 deps=rw([], [RFC], extra=[ffn_done]), dma=True, ndma=2,
                                 tag=f"fl{half}_{t}")
                    commit(fl, [], [RFC])
                    al = prog.op("scalar",
                                 lambda e, row0=row0: e.dma_start(
                                     out=axf_sb[:], in_=attn_s[row0:row0 + 128, :]),
                                 deps=rw([], [RAXF]), dma=True, tag=f"al{half}_{t}")
                    commit(al, [], [RAXF])
                    xlo = prog.op("scalar",
                                  lambda e, row0=row0: e.dma_start(
                                      out=xf_sb[:], in_=x_d[row0:row0 + 128, :]),
                                  deps=rw([], [RXF]), dma=True, tag=f"xl{half}_{t}")
                    commit(xlo, [], [RXF])

                    def ftp(e):
                        for c in range(16):
                            inst = e.matmul(pslice(c * 128, (c + 1) * 128),
                                            fcol_sb[:, c * 128:(c + 1) * 128], ident_sb[:],
                                            start=True, stop=True, is_transpose=True)
                        return inst
                    tf = prog.op("tensor", ftp,
                                 deps=rw([RFC], banks(0, 2048)), tag=f"tf{half}_{t}")
                    commit(tf, [RFC], [])
                    for b in banks(0, 2048):
                        b.on_write(tf)
                    va = None
                    for q in range(4):
                        va = prog.op("vector",
                                     lambda e, q=q: e.tensor_tensor(
                                         axf_sb[:, q * 512:(q + 1) * 512],
                                         pslice(q * 512, (q + 1) * 512),
                                         axf_sb[:, q * 512:(q + 1) * 512], OP.add),
                                     deps=(rw([RAXF], [], extra=[tf, al]) if q == 0 else []),
                                     tag=f"va{half}_{t}_{q}")
                        BK[q].on_read(va)
                    commit(va, [RAXF], [RAXF])
                    vb = prog.op("vector",
                                 lambda e: e.tensor_tensor(axf_sb[:], axf_sb[:], xf_sb[:],
                                                           OP.add),
                                 deps=rw([RXF], [], extra=[xlo]), tag=f"vb{half}_{t}")
                    commit(vb, [RXF], [RAXF])
                    ys = prog.op("scalar",
                                 lambda e, row0=row0: e.dma_start(
                                     out=y_d[row0:row0 + 128, :], in_=axf_sb[:]),
                                 deps=[vb], dma=True, tag=f"ys{half}_{t}")
                    commit(ys, [RAXF], [])
                    y_stores.append(ys)
            prog.op("scalar", lambda e: e.nop(), deps=[y_stores[-1]], tag="fin")
            emitted = prog.emit_phase(block, emitted)

    return nc


def _consts():
    bf = ml_dtypes.bfloat16
    inv = (10000.0 ** (-np.arange(0, D, 2, dtype=np.float64) / D))
    ang = np.arange(N, dtype=np.float64)[:, None] * inv[None, :]
    cos = np.cos(ang).astype(np.float32)
    sin = np.sin(ang).astype(np.float32)
    cos_tok = cos.astype(bf)
    sin_tok = sin.astype(bf)
    cosT = cos.T.copy().astype(bf)
    sinT = sin.T.copy().astype(bf)
    p = np.arange(128)[:, None]
    c = np.arange(512)[None, :]
    mask4 = np.zeros((128, 4, 512), np.float32)
    maskT4 = np.zeros((128, 4, 512), np.float32)
    for v in range(4):
        vis = (c <= v * 128 + p) & True
        vis = (c < v * 128) | ((c >= v * 128) & (c < (v + 1) * 128) & ((c - v * 128) <= p))
        mask4[:, v, :] = np.where(vis, 0.0, NEG)
        t_ = c // 128
        jj = c % 128
        visT = (t_ > v) | ((t_ == v) & (jj >= p))
        maskT4[:, v, :] = np.where(visT, 0.0, NEG)
    mask4 = mask4.reshape(128, 2048).astype(bf)
    maskT4 = maskT4.reshape(128, 2048).astype(bf)
    d16 = np.zeros((16, 16 * 128), np.float32)
    for h in range(16):
        d16[h, h * 128:(h + 1) * 128] = 1.0
    d8 = np.zeros((8, 8 * 128), np.float32)
    for ex in range(8):
        d8[ex, ex * 128:(ex + 1) * 128] = 1.0
    return {
        "cos_tok": cos_tok, "sin_tok": sin_tok, "cosT": cosT, "sinT": sinT,
        "mask4": mask4, "maskT4": maskT4, "delta16": d16.astype(bf),
        "delta8": d8.astype(np.float32),
        "ones1": np.ones((1, 128), np.float32),
        "ident": np.eye(128, dtype=np.float32),
    }


def kernel(L0_attn_x, L0_attn_Wq, L0_attn_Wkv, L0_attn_Wkvh, L0_attn_Wg,
           L0_attn_Wout, L0_ffn_gate_kernel, L0_ffn_w0_kernel, L0_ffn_w1_kernel,
           L0_ffn_wo_kernel):
    x = np.ascontiguousarray(np.asarray(L0_attn_x, dtype=np.float32))
    shared = {
        "Wq": np.ascontiguousarray(np.asarray(L0_attn_Wq, np.float32)),
        "Wkv": np.ascontiguousarray(np.asarray(L0_attn_Wkv, np.float32)),
        "Wkvh": np.ascontiguousarray(np.asarray(L0_attn_Wkvh, np.float32)),
        "Wg": np.ascontiguousarray(np.asarray(L0_attn_Wg, np.float32)),
        "Wout": np.ascontiguousarray(np.asarray(L0_attn_Wout, np.float32)),
        "gate": np.ascontiguousarray(np.asarray(L0_ffn_gate_kernel, np.float32)),
        "w0": np.ascontiguousarray(np.asarray(L0_ffn_w0_kernel, np.float32)),
        "w1": np.ascontiguousarray(np.asarray(L0_ffn_w1_kernel, np.float32)),
        "wo": np.ascontiguousarray(np.asarray(L0_ffn_wo_kernel, np.float32)),
    }
    shared.update(_consts())
    nc = build_program()
    in_maps = [dict(shared, x=np.ascontiguousarray(x[i])) for i in range(8)]
    res = run_bass_kernel_spmd(nc, in_maps, core_ids=list(range(8)))
    return np.stack([np.asarray(r["y"], np.float32) for r in res.results], axis=0)


# version 06
# speedup vs baseline: 1.1283x; 1.0612x over previous
import numpy as np
import ml_dtypes
import concourse.bass as bass
import concourse.mybir as mybir
from concourse.bass_utils import run_bass_kernel_spmd
from contextlib import ExitStack

F32 = mybir.dt.float32
BF16 = mybir.dt.bfloat16
AF = mybir.ActivationFunctionType
OP = mybir.AluOpType
AX = mybir.AxisListType

PHASES = "PAF"
PSTAGE = 9
K5 = 3
N = 2048
E = 2048
H = 16
D = 128
NEXP = 8
MLP = 4096
NT = 16
NG = 4
GS = 512
HALF = 1024
NEG = -1.0e30
ENGINES = ("tensor", "vector", "scalar", "gpsimd", "sync")


class Op:
    __slots__ = ("engine", "fn", "deps", "dma", "ndma", "lane", "ev", "needed", "tag")

    def __init__(self, engine, fn, deps, dma, ndma, tag):
        self.engine = engine
        self.fn = fn
        self.deps = [d for d in deps if d is not None]
        self.dma = dma
        self.ndma = ndma
        self.lane = None
        self.ev = None
        self.needed = False
        self.tag = tag


class Prog:

    NDMA_SEMS = 16

    def __init__(self, nc, ctx):
        self.nc = nc
        self.ops = []
        self.sems = {}
        self.sem_vals = {}
        self.dma_idx = {}
        for e in ENGINES:
            name = f"s_{e}_c"
            self.sems[name] = ctx.enter_context(nc.semaphore(name))
            self.sem_vals[name] = 0
            self.dma_idx[e] = 0
            for i in range(self.NDMA_SEMS):
                name = f"s_{e}_d{i}"
                self.sems[name] = ctx.enter_context(nc.semaphore(name))
                self.sem_vals[name] = 0
        self.last = {e: None for e in ENGINES}
        self.last_compute = {e: None for e in ENGINES}
        self.lane_last = {e: [None] * self.NDMA_SEMS for e in ENGINES}

    def op(self, engine, fn, deps=(), dma=False, ndma=1, tag=""):
        o = Op(engine, fn, list(deps), dma, ndma, tag)
        if engine == "sync" and self.last[engine] is not None:
            o.deps.append(self.last[engine])
        elif engine in ("vector", "scalar") and not dma \
                and self.last_compute[engine] is not None:
            o.deps.append(self.last_compute[engine])
        if dma:
            o.lane = self.dma_idx[engine]
            self.dma_idx[engine] = (o.lane + 1) % self.NDMA_SEMS
            self.lane_last[engine][o.lane] = o
        else:
            self.last_compute[engine] = o
        self.ops.append(o)
        self.last[engine] = o
        return o

    def fence(self, engine):
        out = [o for o in self.lane_last[engine] if o is not None]
        if self.last_compute[engine] is not None:
            out.append(self.last_compute[engine])
        return out

    def fence_all(self):
        return [o for e in ENGINES for o in self.fence(e)]

    def emit_phase(self, block, start_idx):
        ops = self.ops[start_idx:]
        cur = set(id(o) for o in ops)
        for o in ops:
            for d in o.deps:
                if id(d) not in cur and d.ev is None:
                    raise RuntimeError(
                        f"dep {d.tag} of {o.tag} is from an already-emitted phase "
                        f"but was not pinned")
                d.needed = True
        for o in ops:
            if (o.needed or o.dma) and o.ev is None:
                if o.dma:
                    name = f"s_{o.engine}_d{o.lane}"
                    step = 16 * o.ndma
                else:
                    name = f"s_{o.engine}_c"
                    step = 1
                self.sem_vals[name] += step
                o.ev = (name, self.sem_vals[name])
        per_engine = {e: [] for e in ENGINES}
        for o in ops:
            per_engine[o.engine].append(o)
        for e in ENGINES:
            recs = per_engine[e]
            if not recs:
                continue
            sec = getattr(block, e)

            def section(eng, recs=recs):
                seen = {}
                for o in recs:
                    for d in o.deps:
                        if d.ev is None:
                            raise RuntimeError(f"dep of {o.tag} has no event: {d.tag}")
                        s, v = d.ev
                        if seen.get(s, 0) < v:
                            eng.wait_ge(self.sems[s], v)
                            seen[s] = v
                    if o.dma and o.ev is not None:
                        s, v = o.ev
                        prev = v - 16 * o.ndma
                        if prev > 0 and seen.get(s, 0) < prev:
                            eng.wait_ge(self.sems[s], prev)
                            seen[s] = prev
                    inst = o.fn(eng)
                    if o.ev is not None:
                        insts = inst if isinstance(inst, list) else [inst]
                        assert len(insts) == (o.ndma if o.dma else 1), o.tag
                        for i2 in insts:
                            i2.then_inc(self.sems[o.ev[0]], 16 if o.dma else 1)

            sec(section)
        return len(self.ops)


class Region:

    def __init__(self):
        self.w = None
        self.readers = []

    def wdeps(self):
        return ([self.w] if self.w is not None else []) + self.readers

    def on_write(self, op):
        self.w = op
        self.readers = []

    def rdeps(self):
        return [self.w] if self.w is not None else []

    def on_read(self, op):
        self.readers.append(op)


def rw(regions_read, regions_written, extra=()):
    deps = list(extra)
    for r in regions_read:
        deps += r.rdeps()
    for r in regions_written:
        deps += r.wdeps()
    return deps


def commit(op, regions_read, regions_written):
    for r in regions_read:
        r.on_read(op)
    for r in regions_written:
        r.on_write(op)
    return op


def build_program():
    nc = bass.Bass("TRN2", target_bir_lowering=False)

    x_d = nc.dram_tensor("x", [N, E], F32, kind="ExternalInput").ap()
    wq_d = nc.dram_tensor("Wq", [E, E], F32, kind="ExternalInput").ap()
    wkv_d = nc.dram_tensor("Wkv", [256, E], F32, kind="ExternalInput").ap()
    wkvh_d = nc.dram_tensor("Wkvh", [32, E], F32, kind="ExternalInput").ap()
    wg_d = nc.dram_tensor("Wg", [32, E], F32, kind="ExternalInput").ap()
    wout_d = nc.dram_tensor("Wout", [E, E], F32, kind="ExternalInput").ap()
    gate_d = nc.dram_tensor("gate", [E, NEXP], F32, kind="ExternalInput").ap()
    w0_d = nc.dram_tensor("w0", [NEXP, E, MLP], F32, kind="ExternalInput").ap()
    w1_d = nc.dram_tensor("w1", [NEXP, E, MLP], F32, kind="ExternalInput").ap()
    wo_d = nc.dram_tensor("wo", [NEXP, MLP, E], F32, kind="ExternalInput").ap()
    costk_d = nc.dram_tensor("cos_tok", [N, 64], BF16, kind="ExternalInput").ap()
    sintk_d = nc.dram_tensor("sin_tok", [N, 64], BF16, kind="ExternalInput").ap()
    cosT_d = nc.dram_tensor("cosT", [64, N], BF16, kind="ExternalInput").ap()
    sinT_d = nc.dram_tensor("sinT", [64, N], BF16, kind="ExternalInput").ap()
    mask4_d = nc.dram_tensor("mask4", [128, 2048], BF16, kind="ExternalInput").ap()
    maskT4_d = nc.dram_tensor("maskT4", [128, 2048], BF16, kind="ExternalInput").ap()
    d16_d = nc.dram_tensor("delta16", [16, 16 * 128], BF16, kind="ExternalInput").ap()
    d8_d = nc.dram_tensor("delta8", [8, 8 * 128], F32, kind="ExternalInput").ap()
    ones1_d = nc.dram_tensor("ones1", [1, 128], F32, kind="ExternalInput").ap()
    ident_d = nc.dram_tensor("ident", [128, 128], F32, kind="ExternalInput").ap()
    y_d = nc.dram_tensor("y", [N, E], F32, kind="ExternalOutput").ap()

    xnT_s = nc.dram_tensor("xnT_s", [16, 128, N], BF16, kind="Internal").ap()
    wqT_s = nc.dram_tensor("wqT_s", [16, 128, E], BF16, kind="Internal").ap()
    woutT_s = nc.dram_tensor("woutT_s", [16, 128, E], BF16, kind="Internal").ap()
    w0b_s = nc.dram_tensor("w0b_s", [NEXP, 16, 128, 4096], BF16, kind="Internal").ap()
    w1b_s = nc.dram_tensor("w1b_s", [NEXP, 16, 128, 4096], BF16, kind="Internal").ap()
    wob_s = nc.dram_tensor("wob_s", [NEXP, 16, 128, 4096], BF16, kind="Internal").ap()
    attn_s = nc.dram_tensor("attn_s", [N, E], F32, kind="Internal").ap()
    ffnT_s = nc.dram_tensor("ffnT_s", [E, N], F32, kind="Internal").ap()

    with ExitStack() as root:
        def sbt(ctx, name, shape, dt):
            return ctx.enter_context(nc.sbuf_tensor(name, shape, dt))

        PSB = [root.enter_context(nc.psum_tensor(f"psb{b}", [128, 512], F32))
               for b in range(8)]
        BK = [Region() for _ in range(8)]

        def pslice(c0, c1):
            b = c0 // 512
            assert (c1 - 1) // 512 == b, (c0, c1)
            return PSB[b][:, c0 - b * 512: c1 - b * 512]

        def banks(c0, c1):
            return [BK[b] for b in range(c0 // 512, (c1 + 511) // 512)]

        prog = Prog(nc, root)
        block = root.enter_context(nc.Block())
        emitted = 0

        d8_sb = sbt(root, "d8_sb", [8, 8 * 128], F32)
        ones1_sb = sbt(root, "ones1_sb", [1, 128], F32)
        ident_sb = sbt(root, "ident_sb", [128, 128], F32)
        combT_sb = sbt(root, "combT_sb", [8, NT * 128], F32)

        with ExitStack() as actx:
            xnT_sb = sbt(actx, "xnT_sb", [128, 16 * N], BF16)
            kpT_sb = sbt(actx, "kpT_sb", [128, NT * 128], BF16)
            vp_sb = sbt(actx, "vp_sb", [128, NT * 128], BF16)
            vhtok_sb = sbt(actx, "vhtok_sb", [128, NT * 16], BF16)
            vhtok32_sb = sbt(actx, "vhtok32_sb", [128, NT * 16], F32)
            khT_bf = sbt(actx, "khT_bf", [16, N], BF16)
            costk_sb = sbt(actx, "costk_sb", [128, NT * 64], BF16)
            pw_sb = [sbt(actx, f"pw_sb{i}", [128, 2048], BF16) for i in range(2)]
            sintk_sb = sbt(actx, "sintk_sb", [128, NT * 64], BF16)

            with ExitStack() as pctx:
                x_sb = [sbt(pctx, f"x_sb{i}", [128, E], F32) for i in range(2)]
                xntk_sb = [sbt(pctx, f"xntk{i}", [128, E], BF16) for i in range(2)]
                wld_sb = [sbt(pctx, f"wld{i}", [128, E], BF16) for i in range(2)]
                wstage_sb = [sbt(pctx, "wstage0", [128, E], BF16)]
                wkvT_sb = sbt(pctx, "wkvT_sb", [128, 16 * 256], BF16)
                wgT_sb = sbt(pctx, "wgT_sb", [128, 16 * 32], BF16)
                wkvhT_sb = sbt(pctx, "wkvhT_sb", [128, 16 * 32], BF16)
                gate_sb = sbt(pctx, "gate_sb", [128, 16 * 8], BF16)
                sumsq_sb = sbt(pctx, "sumsq_sb", [128, 16], F32)
                mean_sb = sbt(pctx, "mean_sb", [128, 1], F32)
                var_sb = sbt(pctx, "var_sb", [128, 1], F32)
                rstd_sb = sbt(pctx, "rstd_sb", [128, 1], F32)
                sq_sb = sbt(pctx, "sq_sb", [128, E], BF16)
                G = {}
                for nm in ("pa", "pb", "lgs", "Fx", "tmpa", "tmpb"):
                    G[nm] = sbt(pctx, "G_" + nm, [16, N], F32)
                vhT_bf = sbt(pctx, "vhT_bf", [16, N], BF16)
                kprot_sb = [sbt(pctx, f"kprot{i}", [128, 128], BF16) for i in range(2)]
                ktmp = [sbt(pctx, f"ktmp{i}", [128, 64], F32) for i in range(2)]
                eq1_sb = sbt(pctx, "eq1_sb", [128, NT * 8], F32)
                eq2_sb = sbt(pctx, "eq2_sb", [128, NT * 8], F32)
                comb_sb = sbt(pctx, "comb_sb", [128, NT * 8], F32)
                dcol_sb = sbt(pctx, "dcol_sb", [128, 16], F32)
                w1c_sb = sbt(pctx, "w1c_sb", [128, 16], F32)
                w2c_sb = sbt(pctx, "w2c_sb", [128, 16], F32)
                m1_sb = sbt(pctx, "m1_sb", [128, 1], F32)
                m2_sb = sbt(pctx, "m2_sb", [128, 1], F32)

                RX = [Region() for _ in range(2)]
                RXN = [Region() for _ in range(2)]
                RWL = [Region() for _ in range(2)]
                RWS = [Region()]
                RKP = [Region() for _ in range(2)]
                Rstats = Region()
                Rxnt = Region()
                Rscr = Region()

                const_ops = []

                def ld(dst, src):
                    o = prog.op("scalar", lambda e, d=dst, s=src: e.dma_start(out=d, in_=s),
                                dma=True, tag="cld")
                    const_ops.append(o)
                    return o
                ld(d8_sb[:], d8_d)
                ld(ones1_sb[:], ones1_d)
                ld(ident_sb[:], ident_d)
                ld(costk_sb[:].rearrange("p (c j) -> p c j", c=NT),
                   costk_d.rearrange("(c p) j -> p c j", p=128))
                ld(sintk_sb[:].rearrange("p (c j) -> p c j", c=NT),
                   sintk_d.rearrange("(c p) j -> p c j", p=128))
                op_consts = const_ops

                if PSTAGE >= 2:
                    wq_st = {}
                    wout_st_by_h = {h: [] for h in range(16)}
                    op_wout_tp = []
                    for c in range(16):
                        lop = prog.op("gpsimd",
                                      lambda e, c=c, b=c % 2: e.dma_start(
                                          out=wld_sb[b][:], in_=wq_d[c * 128:(c + 1) * 128, :]),
                                      deps=rw([], [RWL[c % 2]]), dma=True, tag=f"wqld{c}")
                        commit(lop, [], [RWL[c % 2]])
                        top = prog.op("sync",
                                      lambda e, b=c % 2: e.dma_start_transpose(
                                          wstage_sb[0][:].rearrange("p (k d) -> p k d", k=16),
                                          wld_sb[b][:]),
                                      deps=rw([RWL[c % 2]], [RWS[0]]), dma=True, tag=f"wqtp{c}")
                        commit(top, [RWL[c % 2]], [RWS[0]])
                        sop = prog.op("scalar",
                                      lambda e, c=c, b=c % 2: e.dma_start(
                                          out=wqT_s[c], in_=wstage_sb[0][:]),
                                      deps=rw([RWS[0]], []), dma=True, tag=f"wqst{c}")
                        commit(sop, [RWS[0]], [])
                        wq_st[c] = sop
                    for c in range(16):
                        lop = prog.op("gpsimd",
                                      lambda e, c=c, b=c % 2: e.dma_start(
                                          out=wld_sb[b][:], in_=wout_d[c * 128:(c + 1) * 128, :]),
                                      deps=rw([], [RWL[c % 2]]), dma=True, tag=f"wold{c}")
                        commit(lop, [], [RWL[c % 2]])
                        top = prog.op("sync",
                                      lambda e, b=c % 2: e.dma_start_transpose(
                                          wstage_sb[0][:].rearrange("p (k d) -> p k d", k=16),
                                          wld_sb[b][:]),
                                      deps=rw([RWL[c % 2]], [RWS[0]]), dma=True, tag=f"wotp{c}")
                        commit(top, [RWL[c % 2]], [RWS[0]])
                        op_wout_tp.append(top)
                        for h in range(16):
                            sop = prog.op("scalar",
                                          lambda e, c=c, h=h, b=c % 2: e.dma_start(
                                              out=woutT_s[h][:, c * 128:(c + 1) * 128],
                                              in_=wstage_sb[0][:, h * 128:(h + 1) * 128]),
                                          deps=rw([RWS[0]], []), dma=True, tag=f"wost{c}_{h}")
                            commit(sop, [RWS[0]], [])
                            wout_st_by_h[h].append(sop)
                if PSTAGE >= 3:
                    kv_ld = []
                    for c in range(2):
                        lop = prog.op("gpsimd",
                                      lambda e, c=c, b=c % 2: e.dma_start(
                                          out=wld_sb[b][:], in_=wkv_d[c * 128:(c + 1) * 128, :]),
                                      deps=rw([], [RWL[c % 2]]), dma=True, tag=f"kvld{c}")
                        commit(lop, [], [RWL[c % 2]])
                        top = prog.op("sync",
                                      lambda e, c=c, b=c % 2: e.dma_start_transpose(
                                          wkvT_sb[:].rearrange("p (k n) -> p k n", k=16)
                                          [:, :, c * 128:(c + 1) * 128],
                                          wld_sb[b][:]),
                                      deps=rw([RWL[c % 2]], []), dma=True, tag=f"kvtp{c}")
                        commit(top, [RWL[c % 2]], [])
                        kv_ld.append(top)
                    g32 = prog.op("gpsimd", lambda e: e.dma_start(out=wld_sb[0][0:32, :], in_=wg_d),
                                  deps=rw([], [RWL[0]]), dma=True, tag="g32")
                    commit(g32, [], [RWL[0]])
                    kvh32 = prog.op("gpsimd",
                                    lambda e: e.dma_start(out=wld_sb[1][0:32, :], in_=wkvh_d),
                                    deps=rw([], [RWL[1]]), dma=True, tag="kvh32")
                    commit(kvh32, [], [RWL[1]])
                    gtld = prog.op("gpsimd",
                                   lambda e: e.dma_start(
                                       out=gate_sb[:].rearrange("p (c k) -> p c k", c=16),
                                       in_=gate_d.rearrange("(c p) k -> p c k", p=128)),
                                   dma=True, tag="gtld")
                    wgtp = prog.op("sync",
                                   lambda e: e.dma_start_transpose(
                                       wgT_sb[:].rearrange("p (c k) -> p c k", c=16),
                                       wld_sb[0][0:32, :]),
                                   deps=[g32], dma=True, tag="wgtp")
                    commit(wgtp, [RWL[0]], [])
                    kvhtp = prog.op("sync",
                                    lambda e: e.dma_start_transpose(
                                        wkvhT_sb[:].rearrange("p (c k) -> p c k", c=16),
                                        wld_sb[1][0:32, :]),
                                    deps=[kvh32], dma=True, tag="kvhtp")
                    commit(kvhtp, [RWL[1]], [])

                if PSTAGE >= 4:
                    xn_tp = []
                    for t in range(NT):
                        lop = prog.op("scalar",
                                      lambda e, t=t, b=t % 2: e.dma_start(
                                          out=x_sb[b][:], in_=x_d[t * 128:(t + 1) * 128, :]),
                                      deps=rw([], [RX[t % 2]]), dma=True, tag=f"xld{t}")
                        commit(lop, [], [RX[t % 2]])
                        sqo = prog.op("scalar",
                                      lambda e, t=t, b=t % 2: e.activation(
                                          sq_sb[:], x_sb[b][:], AF.Square,
                                          accum_out=sumsq_sb[:, t:t + 1]),
                                      deps=rw([RX[t % 2]], []), tag=f"sq{t}")
                        commit(sqo, [RX[t % 2]], [])
                        vst = prog.op("vector",
                                      lambda e, b=t % 2: e.reduce_sum(mean_sb[:], x_sb[b][:], axis=AX.X),
                                      deps=rw([RX[t % 2]], [Rstats]), tag=f"vm{t}")
                        commit(vst, [RX[t % 2]], [Rstats])
                        prog.op("vector", lambda e: e.tensor_scalar(
                            mean_sb[:], mean_sb[:], 1.0 / E, None, OP.mult, OP.bypass), tag="vm2")
                        prog.op("vector", lambda e: e.tensor_tensor(
                            var_sb[:], mean_sb[:], mean_sb[:], OP.mult), tag="vm3")
                        vvar = prog.op("vector",
                                       lambda e, t=t: e.scalar_tensor_tensor(
                                           var_sb[:], sumsq_sb[:, t:t + 1], 1.0 / E, var_sb[:],
                                           OP.mult, OP.subtract),
                                       deps=[sqo], tag=f"vvar{t}")
                        vve = prog.op("vector", lambda e: e.tensor_scalar(
                            var_sb[:], var_sb[:], 1e-5, None, OP.add, OP.bypass), tag="veps")
                        sln = prog.op("scalar", lambda e: e.activation(rstd_sb[:], var_sb[:], AF.Ln),
                                      deps=[vve], tag=f"sln{t}")
                        sex = prog.op("scalar", lambda e: e.activation(
                            rstd_sb[:], rstd_sb[:], AF.Exp, scale=-0.5), tag=f"sex{t}")
                        vxn = prog.op("vector",
                                      lambda e, b=t % 2: e.tensor_scalar(
                                          xntk_sb[b][:], x_sb[b][:], mean_sb[:, 0:1],
                                          rstd_sb[:, 0:1], OP.subtract, OP.mult),
                                      deps=rw([RX[t % 2], Rstats], [RXN[t % 2]], extra=[sex]),
                                      tag=f"vxn{t}")
                        commit(vxn, [RX[t % 2], Rstats], [RXN[t % 2]])
                        commit(vxn, [], [RX[t % 2]])
                        top = prog.op("sync",
                                      lambda e, t=t, b=t % 2: e.dma_start_transpose(
                                          xnT_sb[:].rearrange("p (c n) -> p c n", c=16)
                                          [:, :, t * 128:(t + 1) * 128],
                                          xntk_sb[b][:]),
                                      deps=rw([RXN[t % 2]], [Rxnt]), dma=True, tag=f"xntp{t}")
                        commit(top, [RXN[t % 2]], [])
                        xn_tp.append(top)
                    for o in prog.fence("sync"):
                        Rxnt.on_write(o) if False else None
                    Rxnt.w = None
                    Rxnt.readers = []
                    xn_tp_fence = prog.fence("sync")

                    def xnT_store(e):
                        return [e.dma_start(
                            out=xnT_s[q * 8:(q + 1) * 8].rearrange("c p n -> p c n"),
                            in_=xnT_sb[:].rearrange("p (c n) -> p c n", c=16)
                            [:, q * 8:(q + 1) * 8, :]) for q in range(2)]
                    xns = prog.op("scalar", xnT_store, deps=xn_tp_fence, dma=True, ndma=2,
                                  tag="xnst")

                if PSTAGE >= 5:
                    dep_xnt = xn_tp_fence
                    kv_done = []
                    for t in range(NT):
                        def kvmm(e, t=t):
                            for ec in range(16):
                                inst = e.matmul(pslice(0, 256),
                                                xnT_sb[:, ec * N + t * 128: ec * N + t * 128 + 128],
                                                wkvT_sb[:, ec * 256:(ec + 1) * 256],
                                                start=(ec == 0), stop=(ec == 15))
                            return inst
                        mm = prog.op("tensor", kvmm,
                                     deps=rw([], [BK[0]], extra=dep_xnt + [kv_ld[1]] + op_consts),
                                     tag=f"kv{t}")
                        commit(mm, [], [BK[0]])
                        b = t % 2

                        if K5 == 1:
                            rp = prog.op("vector",
                                         lambda e, b=b: e.tensor_copy(
                                             kprot_sb[b][:], pslice(0, 128)),
                                         deps=rw([BK[0]], [RKP[b]], extra=[mm]),
                                         tag=f"ropek{t}cp")
                            commit(rp, [BK[0]], [RKP[b]])
                        elif K5 in (0, 4):
                            rp = mm
                        elif K5 == 5:
                            rp = prog.op("vector",
                                         lambda e, b=b: e.tensor_copy(
                                             kprot_sb[b][:], pslice(0, 128)),
                                         deps=rw([BK[0]], [RKP[b]], extra=[mm]),
                                         tag=f"ropek{t}cp")
                            commit(rp, [BK[0]], [RKP[b]])
                        if K5 >= 2:
                         prog.op("vector", lambda e, t=t: e.tensor_tensor(
                            ktmp[0][:], pslice(0, 64), costk_sb[:, t * 64:(t + 1) * 64], OP.mult),
                            deps=rw([BK[0]], [RKP[b]], extra=[mm]), tag=f"ropek{t}a")
                         prog.op("vector", lambda e, t=t: e.tensor_tensor(
                            ktmp[1][:], pslice(64, 128), sintk_sb[:, t * 64:(t + 1) * 64], OP.mult),
                            tag=f"ropek{t}b")
                         prog.op("vector", lambda e, b=b: e.tensor_tensor(
                            kprot_sb[b][:, 0:64], ktmp[0][:], ktmp[1][:], OP.subtract),
                            tag=f"ropek{t}c")
                         prog.op("vector", lambda e, t=t: e.tensor_tensor(
                            ktmp[0][:], pslice(64, 128), costk_sb[:, t * 64:(t + 1) * 64], OP.mult),
                            tag=f"ropek{t}d")
                         prog.op("vector", lambda e, t=t: e.tensor_tensor(
                            ktmp[1][:], pslice(0, 64), sintk_sb[:, t * 64:(t + 1) * 64], OP.mult),
                            tag=f"ropek{t}e")
                         rp = prog.op("vector", lambda e, b=b: e.tensor_tensor(
                            kprot_sb[b][:, 64:128], ktmp[0][:], ktmp[1][:], OP.add),
                            tag=f"ropek{t}f")
                         commit(rp, [BK[0]], [RKP[b]])
                        vpo = prog.op("vector",
                                      lambda e, t=t: e.tensor_copy(vp_sb[:, t * 128:(t + 1) * 128],
                                                                   pslice(128, 256)),
                                      deps=[mm], tag=f"vp{t}")
                        commit(vpo, [BK[0]], [])
                        tp = prog.op("sync",
                                     lambda e, t=t, b=b: e.dma_start_transpose(
                                         kpT_sb[:, t * 128:(t + 1) * 128], kprot_sb[b][:]),
                                     deps=rw([RKP[b]], []), dma=True, tag=f"kptp{t}")
                        commit(tp, [RKP[b]], [])
                        kv_done.append(tp)
                    kpT_ready = kv_done[-1]

                if PSTAGE >= 6:
                    comb_parts = []
                    for t in range(NT):
                        def gmm(e, t=t):
                            for ec in range(16):
                                inst = e.matmul(pslice(2560, 2568),
                                                xnT_sb[:, ec * N + t * 128: ec * N + t * 128 + 128],
                                                gate_sb[:, ec * 8:(ec + 1) * 8],
                                                start=(ec == 0), stop=(ec == 15))
                            return inst
                        mm = prog.op("tensor", gmm,
                                     deps=rw([], [BK[5]], extra=dep_xnt + [gtld]), tag=f"g8{t}")
                        commit(mm, [], [BK[5]])

                        prog.op("vector", lambda e: e.reduce_max(
                            m1_sb[:], pslice(2560, 2568), axis=AX.X), deps=[mm], tag=f"gv{t}a")
                        prog.op("vector", lambda e, t=t: e.tensor_scalar(
                            eq1_sb[:, t * 8:(t + 1) * 8], pslice(2560, 2568),
                            m1_sb[:, 0:1], None, OP.is_equal, OP.bypass), tag=f"gv{t}b")
                        prog.op("vector", lambda e, t=t: e.scalar_tensor_tensor(
                            comb_sb[:, t * 8:(t + 1) * 8], eq1_sb[:, t * 8:(t + 1) * 8], NEG,
                            pslice(2560, 2568), OP.mult, OP.add), tag=f"gv{t}c")
                        prog.op("vector", lambda e, t=t: e.reduce_max(
                            m2_sb[:], comb_sb[:, t * 8:(t + 1) * 8], axis=AX.X), tag=f"gv{t}d")
                        prog.op("vector", lambda e, t=t: e.tensor_scalar(
                            eq2_sb[:, t * 8:(t + 1) * 8], pslice(2560, 2568),
                            m2_sb[:, 0:1], None, OP.is_equal, OP.bypass), tag=f"gv{t}e")
                        gv = prog.op("vector", lambda e, t=t: e.tensor_tensor(
                            dcol_sb[:, t:t + 1], m1_sb[:], m2_sb[:], OP.subtract), tag=f"gv{t}f")
                        commit(gv, [BK[5]], [])
                        comb_parts.append(gv)
                    sig = prog.op("scalar", lambda e: e.activation(w1c_sb[:], dcol_sb[:], AF.Sigmoid),
                                  deps=[comb_parts[-1]], tag="sig")
                    vw2 = prog.op("vector", lambda e: e.tensor_scalar(
                        w2c_sb[:], w1c_sb[:], -1.0, 1.0, OP.mult, OP.add), deps=[sig], tag="vw2")
                    comb_done = []
                    for t in range(NT):
                        prog.op("vector",
                                lambda e, t=t: e.tensor_scalar(
                                    eq1_sb[:, t * 8:(t + 1) * 8], eq1_sb[:, t * 8:(t + 1) * 8],
                                    w1c_sb[:, t:t + 1], None, OP.mult, OP.bypass),
                                deps=[vw2], tag=f"c1{t}")
                        cop = prog.op("vector",
                                      lambda e, t=t: e.scalar_tensor_tensor(
                                          comb_sb[:, t * 8:(t + 1) * 8], eq2_sb[:, t * 8:(t + 1) * 8],
                                          w2c_sb[:, t:t + 1], eq1_sb[:, t * 8:(t + 1) * 8],
                                          OP.mult, OP.add),
                                      tag=f"c2{t}")
                        comb_done.append(cop)
                    ct_done = []
                    for t in range(NT):
                        tp = prog.op("tensor",
                                     lambda e, t=t: e.matmul(PSB[6][0:8, 0:128],
                                                             comb_sb[:, t * 8:(t + 1) * 8],
                                                             ident_sb[:], start=True, stop=True,
                                                             is_transpose=True),
                                     deps=rw([], [BK[6]], extra=[comb_done[t]] + op_consts),
                                     tag=f"ctT{t}")
                        commit(tp, [], [BK[6]])
                        cp = prog.op("vector",
                                     lambda e, t=t: e.tensor_copy(combT_sb[:, t * 128:(t + 1) * 128],
                                                                  PSB[6][0:8, 0:128]),
                                     deps=[tp], tag=f"ctC{t}")
                        commit(cp, [BK[6]], [])
                        ct_done.append(cp)

                if PSTAGE >= 7:
                    kh_ready = vh_ready = None
                    prev_chain_last = None
                    prev_ln1 = None
                    prev_vdc = None
                    for gi, outbf in enumerate([khT_bf, vhT_bf]):
                        off = 0 if gi == 0 else 16
                        evs = {}
                        for nm, wT in (("pa", wgT_sb), ("pb", wkvhT_sb)):
                            def smm(e, wT=wT, off=off):
                                for cc in range(4):
                                    for ec in range(16):
                                        inst = e.matmul(
                                            PSB[1 + cc][0:16, 0:512],
                                            wT[:, ec * 32 + off: ec * 32 + off + 16],
                                            xnT_sb[:, ec * N + cc * 512: ec * N + cc * 512 + 512],
                                            start=(ec == 0), stop=(ec == 15))
                                return inst
                            extra = dep_xnt + [wgtp, kvhtp]
                            if prev_chain_last is not None:
                                extra = extra + [prev_chain_last, prev_ln1, prev_vdc]
                            mm = prog.op("tensor", smm,
                                         deps=rw([], banks(512, 2560), extra=extra),
                                         tag=f"sm{gi}{nm}")
                            for b in banks(512, 2560):
                                b.on_write(mm)
                            ev = None
                            for cc in range(4):
                                ev = prog.op("vector",
                                             lambda e, nm=nm, cc=cc: e.tensor_copy(
                                                 G[nm][:, cc * 512:(cc + 1) * 512],
                                                 PSB[1 + cc][0:16, 0:512]),
                                             deps=([mm] if cc == 0 else []),
                                             tag=f"sme{gi}{nm}{cc}")
                                BK[1 + cc].on_read(ev)
                            evs[nm] = ev
                        small_evac = {"pa": evs["pa"], "pb": evs["pb"]}
                        gnm, pre = "pa", "pb"
                        sab = prog.op("scalar",
                                      lambda e, gnm=gnm: e.activation(G["tmpa"][:], G[gnm][:], AF.Abs),
                                      deps=[small_evac[gnm]] +
                                           ([prev_chain_last] if prev_chain_last else []),
                                      tag=f"abs{gi}")
                        prog.op("scalar", lambda e: e.activation(G["tmpa"][:], G["tmpa"][:],
                                                                 AF.Exp, scale=-1.0), tag=f"ex{gi}")
                        ln1 = prog.op("scalar", lambda e: e.activation(G["tmpa"][:], G["tmpa"][:],
                                                                       AF.Ln, bias=1.0), tag=f"ln{gi}")
                        vmin = prog.op("vector",
                                       lambda e, gnm=gnm: e.tensor_scalar(
                                           G["lgs"][:], G[gnm][:], 0.0, None, OP.min, OP.bypass),
                                       deps=[small_evac[gnm]] +
                                            ([prev_chain_last] if prev_chain_last else []),
                                       tag=f"mn{gi}")
                        vsub = prog.op("vector", lambda e: e.tensor_tensor(
                            G["lgs"][:], G["lgs"][:], G["tmpa"][:], OP.subtract),
                            deps=[ln1], tag=f"ls{gi}")
                        prog.op("vector", lambda e: e.memset(G["tmpb"][:, 0:1], 0.0), tag=f"z{gi}")
                        prog.op("vector", lambda e: e.tensor_copy(G["tmpb"][:, 1:N],
                                                                  G["lgs"][:, 0:N - 1]), tag=f"sh{gi}")
                        fsc = prog.op("vector", lambda e: e.tensor_tensor_scan(
                            G["Fx"][:], G["tmpb"][:], G["tmpb"][:], 0.0, OP.add, OP.bypass),
                            tag=f"fs{gi}")
                        sef = prog.op("scalar", lambda e: e.activation(G["tmpa"][:], G["Fx"][:], AF.Exp),
                                      deps=[fsc, vsub], tag=f"ef{gi}")
                        vdc = prog.op("vector",
                                      lambda e, pre=pre: e.tensor_tensor(
                                          G["tmpb"][:], G[pre][:], G["tmpa"][:], OP.mult),
                                      deps=[sef, small_evac[pre]], tag=f"dc{gi}")
                        vcs = prog.op("vector",
                                      lambda e: e.tensor_tensor_scan(
                                          G["lgs"][:], G["tmpb"][:], G["tmpb"][:], 0.0,
                                          OP.add, OP.bypass),
                                      tag=f"cs{gi}")
                        if gi == 0:
                            cpy = prog.op("vector",
                                          lambda e, outbf=outbf: e.tensor_scalar(
                                              outbf[:], G["lgs"][:], float(1.0 / np.sqrt(D)),
                                              None, OP.mult, OP.bypass),
                                          tag=f"obf{gi}")
                        else:
                            cpy = prog.op("vector",
                                          lambda e, outbf=outbf: e.tensor_copy(outbf[:], G["lgs"][:]),
                                          tag=f"obf{gi}")
                        if gi == 0:
                            kh_ready = cpy
                        else:
                            vh_ready = cpy
                        prev_chain_last = cpy
                        prev_ln1 = ln1
                        prev_vdc = vdc
                    vh_tp = prog.op("sync",
                                    lambda e: e.dma_start_transpose(
                                        vhtok_sb[:].rearrange("p (c j) -> p c j", c=NT), vhT_bf[:]),
                                    deps=[vh_ready], dma=True, tag="vhtp")
                    vh_tp = prog.op("vector",
                                    lambda e: e.tensor_copy(vhtok32_sb[:], vhtok_sb[:]),
                                    deps=[vh_tp], tag="vhtp32")

                p_last_preprep = prog.fence_all()

                prep_done = {}
                RPREP = [Region() for _ in range(2)]
                if PSTAGE >= 7:
                    for ex in range(NEXP):
                        for ec in range(16):
                            for mh in range(2):
                                bpp = (ec * 2 + mh) % 2
                                lw = prog.op("gpsimd",
                                             lambda e, ex=ex, ec=ec, mh=mh, b=bpp: e.dma_start(
                                                 out=pw_sb[b][:],
                                                 in_=w0_d[ex][ec * 128:(ec + 1) * 128,
                                                              mh * 2048:(mh + 1) * 2048]),
                                             deps=rw([], [RPREP[bpp]]), dma=True,
                                             tag=f"p0l{ex}_{ec}_{mh}")
                                commit(lw, [], [RPREP[bpp]])
                                sw = prog.op("gpsimd",
                                             lambda e, ex=ex, ec=ec, mh=mh, b=bpp: e.dma_start(
                                                 out=w0b_s[ex][mh * 8:(mh + 1) * 8, :,
                                                               ec * 256:(ec + 1) * 256]
                                                 .rearrange("mb p m -> p mb m"),
                                                 in_=pw_sb[b][:].rearrange(
                                                     "p (mb m) -> p mb m", mb=8)),
                                             deps=[lw], dma=True, tag=f"p0s{ex}_{ec}_{mh}")
                                commit(sw, [RPREP[bpp]], [])
                        for ec in range(16):
                            for mh in range(2):
                                bpp = (ec * 2 + mh) % 2
                                lw = prog.op("gpsimd",
                                             lambda e, ex=ex, ec=ec, mh=mh, b=bpp: e.dma_start(
                                                 out=pw_sb[b][:],
                                                 in_=w1_d[ex][ec * 128:(ec + 1) * 128,
                                                              mh * 2048:(mh + 1) * 2048]),
                                             deps=rw([], [RPREP[bpp]]), dma=True,
                                             tag=f"p1l{ex}_{ec}_{mh}")
                                commit(lw, [], [RPREP[bpp]])
                                sw = prog.op("gpsimd",
                                             lambda e, ex=ex, ec=ec, mh=mh, b=bpp: e.dma_start(
                                                 out=w1b_s[ex][mh * 8:(mh + 1) * 8, :,
                                                               ec * 256:(ec + 1) * 256]
                                                 .rearrange("mb p m -> p mb m"),
                                                 in_=pw_sb[b][:].rearrange(
                                                     "p (mb m) -> p mb m", mb=8)),
                                             deps=[lw], dma=True, tag=f"p1s{ex}_{ec}_{mh}")
                                commit(sw, [RPREP[bpp]], [])
                        for mc in range(32):
                            bpp = mc % 2
                            lw = prog.op("gpsimd",
                                         lambda e, ex=ex, mc=mc, b=bpp: e.dma_start(
                                             out=pw_sb[b][:],
                                             in_=wo_d[ex][mc * 128:(mc + 1) * 128, :]),
                                         deps=rw([], [RPREP[bpp]]), dma=True,
                                         tag=f"pol{ex}_{mc}")
                            commit(lw, [], [RPREP[bpp]])
                            sw = prog.op("gpsimd",
                                         lambda e, ex=ex, mc=mc, b=bpp: e.dma_start(
                                             out=wob_s[ex][:, :, mc * 128:(mc + 1) * 128]
                                             .rearrange("dt p d -> p dt d"),
                                             in_=pw_sb[b][:].rearrange(
                                                 "p (dt d) -> p dt d", dt=16)),
                                         deps=[lw], dma=True, tag=f"pos{ex}_{mc}")
                            commit(sw, [RPREP[bpp]], [])
                        prep_done[ex] = prog.fence("gpsimd")

                if PSTAGE < 7:
                    kh_ready = kpT_ready = vh_tp = None
                    if PSTAGE < 5:
                        kpT_ready = None
                    p_last_preprep = prog.fence_all()
                p_last = p_last_preprep
                for o in (p_last + [kh_ready, kpT_ready, vh_tp] if PSTAGE >= 7 else p_last):
                    o.needed = True
                emitted = prog.emit_phase(block, emitted)
                for b in BK:
                    b.w = None
                    b.readers = []

            if "A" not in PHASES:
                return nc
            with ExitStack() as awctx:
                cosT_sb = sbt(awctx, "cosT_sb", [64, N], BF16)
                sinT_sb = sbt(awctx, "sinT_sb", [64, N], BF16)
                mask4_sb = sbt(awctx, "mask4_sb", [128, 2048], BF16)
                maskT4_sb = sbt(awctx, "maskT4_sb", [128, 2048], BF16)
                d16_sb = sbt(awctx, "d16_sb", [16, 16 * 128], BF16)
                wqh_sb = [sbt(awctx, f"wqh{i}", [128, E], BF16) for i in range(2)]
                woh_sb = [sbt(awctx, f"woh{i}", [128, HALF], BF16) for i in range(2)]
                qh_sb = [sbt(awctx, f"qh{i}", [128, GS], BF16) for i in range(2)]
                khK_sb = [sbt(awctx, f"khK{i}", [128, N], BF16) for i in range(2)]
                vh_sb = [sbt(awctx, f"vhb{i}", [128, NT * 128], BF16) for i in range(2)]
                pt_sb = [sbt(awctx, f"ptb{i}", [128, GS], BF16) for i in range(2)]
                stmp_sb = [sbt(awctx, f"stmp{i}", [128, GS], F32) for i in range(2)]
                rtmp_sb = [sbt(awctx, f"rtmp{i}", [64, GS], F32) for i in range(2)]
                oall_sb = sbt(awctx, "oall_sb", [128, H * GS], BF16)
                trash_sb = sbt(awctx, "trash_sb", [128, 2048], BF16)
                axb_sb = [sbt(awctx, f"axb{i}", [128, HALF], F32) for i in range(2)]
                negmx_sb = sbt(awctx, "negmx_sb", [128, 4], F32)
                nmx4_sb = sbt(awctx, "nmx4_sb", [128, 4], F32)
                sstage_sb = sbt(awctx, "sstage_sb", [128, 2048], F32)
                sume4_sb = sbt(awctx, "sume4_sb", [128, 4], F32)
                sume_sb = sbt(awctx, "sume_sb", [128, 4], F32)
                lsum_sb = [sbt(awctx, f"lsum{i}", [128, 4], F32) for i in range(2)]
                lrow2_sb = sbt(awctx, "lrow2_sb", [1, 512], F32)
                lsb_sb = [sbt(awctx, f"lsb{i}", [128, 512], F32) for i in range(2)]

                RWQ = [Region() for _ in range(2)]
                RWOH = [Region() for _ in range(2)]
                RQH = [Region() for _ in range(2)]
                RKH = [Region() for _ in range(2)]
                RVH = [Region() for _ in range(2)]
                RPT = [Region() for _ in range(2)]
                RST = [Region() for _ in range(2)]
                RRT = [Region() for _ in range(2)]
                ROA = [Region() for _ in range(H)]
                RAXB = [Region() for _ in range(2)]
                RLS = [Region() for _ in range(2)]
                RLSUM = [Region() for _ in range(2)]
                RNEG = Region()
                RNMX = Region()
                RSST = Region()
                RSM4 = Region()
                RSME = Region()
                RLR = Region()
                dep_P = p_last

                aconst_ops = []

                def lda(dst, src):
                    o = prog.op("scalar",
                                lambda e, d=dst, s=src: e.dma_start(out=d, in_=s),
                                deps=dep_P, dma=True, tag="alda")
                    aconst_ops.append(o)
                    return o
                lda(cosT_sb[:], cosT_d)
                lda(sinT_sb[:], sinT_d)
                lda(mask4_sb[:], mask4_d)
                lda(maskT4_sb[:], maskT4_d)
                lda(d16_sb[:], d16_d)

                def make_pass2(st):
                    g = st["g"]; h = st["h"]; hb = st["hb"]
                    nmt = 4 * (g + 1)
                    steps = []
                    for mt in range(nmt):
                        def step(mt=mt, g=g, h=h, hb=hb, nmt=nmt):
                            pb = mt % 2
                            cols = (2560, 3072) if pb == 0 else (3072, 3584)
                            bnk = BK[5] if pb == 0 else BK[6]
                            p2 = prog.op("tensor",
                                         lambda e, mt=mt, hb=hb, cols=cols: e.matmul(
                                             pslice(*cols),
                                             khK_sb[hb][:, mt * 128:(mt + 1) * 128],
                                             qh_sb[hb][:, 0:GS], start=True, stop=True),
                                         deps=rw([RQH[hb], RKH[hb]], [bnk],
                                                 extra=[st["slb"]] if pb == 1 else []),
                                         tag=f"p2_{g}_{h}_{mt}")
                            commit(p2, [RQH[hb]], [])
                            commit(p2, [RKH[hb]], [])
                            bnk.on_write(p2)
                            dv = [p2]
                            if mt >= 4 * g:
                                v = mt - 4 * g
                                mk = prog.op("vector",
                                             lambda e, v=v, cols=cols: e.tensor_tensor(
                                                 pslice(*cols), pslice(*cols),
                                                 maskT4_sb[:, v * 512:(v + 1) * 512], OP.add),
                                             deps=[p2], tag=f"mk{g}_{h}_{mt}")
                                bnk.on_read(mk)
                                bnk.on_write(mk)
                                dv = [mk]
                            vs = prog.op("vector",
                                         lambda e, pb=pb, cols=cols, hb=hb: e.tensor_tensor(
                                             stmp_sb[pb][:], pslice(*cols), lsb_sb[hb][:],
                                             OP.subtract),
                                         deps=rw([RST[pb], RLS[hb]], [], extra=dv + [st["slb"]]),
                                         tag=f"vs{g}_{h}_{mt}")
                            bnk.on_read(vs)
                            commit(vs, [RLS[hb]], [])
                            commit(vs, [], [RST[pb]])
                            sx = prog.op("scalar",
                                         lambda e, pb=pb: e.activation(
                                             pt_sb[pb][:], stmp_sb[pb][:], AF.Exp),
                                         deps=rw([RST[pb], RPT[pb]], []), tag=f"sx{g}_{h}_{mt}")
                            commit(sx, [RST[pb]], [])
                            commit(sx, [], [RPT[pb]])
                            pv = prog.op("tensor",
                                         lambda e, mt=mt, pb=pb, hb=hb, nmt=nmt: e.matmul(
                                             pslice(3584, 4096),
                                             vh_sb[hb][:, mt * 128:(mt + 1) * 128],
                                             pt_sb[pb][:], start=(mt == 0),
                                             stop=(mt == nmt - 1)),
                                         deps=rw([RPT[pb], RVH[hb]],
                                                 [BK[7]] if mt == 0 else [],
                                                 extra=[sx]),
                                         tag=f"pv{g}_{h}_{mt}")
                            commit(pv, [RPT[pb]], [])
                            commit(pv, [RVH[hb]], [])
                            if mt == 0:
                                commit(pv, [], [BK[7]])
                            else:
                                BK[7].on_write(pv)
                            st["pv_last"] = pv
                        steps.append(step)
                    return steps

                def finish_pass2(st):
                    h = st["h"]
                    so = prog.op("vector",
                                 lambda e, h=h: e.tensor_copy(
                                     oall_sb[:, h * GS:(h + 1) * GS], pslice(3584, 4096)),
                                 deps=rw([], [ROA[h]], extra=[st["pv_last"]]),
                                 tag=f"so{st['g']}_{h}")
                    BK[7].on_read(so)
                    commit(so, [], [ROA[h]])

                pend = None
                for g in range(NG):
                    mend = GS * (g + 1)
                    nch = g + 1
                    for h in range(H):
                        ih = g * H + h
                        hb = ih % 2
                        wql = prog.op("scalar",
                                      lambda e, h=h, b=hb: e.dma_start(out=wqh_sb[b][:],
                                                                       in_=wqT_s[h]),
                                      deps=rw([], [RWQ[hb]],
                                              extra=[wq_st[h]] +
                                                    (dep_P if ih == 0 else [])),
                                      dma=True, tag=f"wql{g}_{h}")
                        commit(wql, [], [RWQ[hb]])

                        def qmm(e, g=g, b=hb):
                            for ec in range(16):
                                inst = e.matmul(pslice(0, GS),
                                                wqh_sb[b][:, ec * 128:(ec + 1) * 128],
                                                xnT_sb[:, ec * N + g * GS: ec * N + g * GS + GS],
                                                start=(ec == 0), stop=(ec == 15))
                            return inst
                        qp = prog.op("tensor", qmm,
                                     deps=rw([RWQ[hb]], [BK[0]],
                                             extra=(dep_P if ih == 0 else [])),
                                     tag=f"qp{g}_{h}")
                        commit(qp, [RWQ[hb]], [BK[0]])

                        prog.op("vector", lambda e, g=g: e.tensor_tensor(
                            rtmp_sb[0][:, :], PSB[0][0:64, 0:GS],
                            cosT_sb[:, g * GS:(g + 1) * GS], OP.mult),
                            deps=rw([BK[0], RRT[0], RRT[1]], [],
                                    extra=[qp] + (aconst_ops if ih == 0 else [])),
                            tag=f"rq{g}_{h}a")
                        prog.op("vector", lambda e, g=g: e.tensor_tensor(
                            rtmp_sb[1][:, :], PSB[0][64:128, 0:GS],
                            sinT_sb[:, g * GS:(g + 1) * GS], OP.mult), tag=f"rq{g}_{h}b")
                        prog.op("vector", lambda e, b=hb: e.tensor_tensor(
                            qh_sb[b][0:64, :], rtmp_sb[0][:, :], rtmp_sb[1][:, :],
                            OP.subtract), deps=rw([RQH[hb]], []), tag=f"rq{g}_{h}c")
                        prog.op("vector", lambda e, g=g: e.tensor_tensor(
                            rtmp_sb[0][:, :], PSB[0][64:128, 0:GS],
                            cosT_sb[:, g * GS:(g + 1) * GS], OP.mult), tag=f"rq{g}_{h}d")
                        prog.op("vector", lambda e, g=g: e.tensor_tensor(
                            rtmp_sb[1][:, :], PSB[0][0:64, 0:GS],
                            sinT_sb[:, g * GS:(g + 1) * GS], OP.mult), tag=f"rq{g}_{h}e")
                        rq = prog.op("vector", lambda e, b=hb: e.tensor_tensor(
                            qh_sb[b][64:128, :], rtmp_sb[0][:, :], rtmp_sb[1][:, :],
                            OP.add), tag=f"rq{g}_{h}f")
                        commit(rq, [BK[0]], [RQH[hb]])
                        commit(rq, [], [RRT[0]])
                        commit(rq, [], [RRT[1]])

                        def khb2(e, nch=nch, h=h):
                            for c in range(nch):
                                inst = e.matmul(pslice(512 + c * 512, 512 + (c + 1) * 512),
                                                d16_sb[:, h * 128:(h + 1) * 128],
                                                khT_bf[:, c * 512:(c + 1) * 512],
                                                start=True, stop=True)
                            return inst
                        kb = prog.op("tensor", khb2,
                                     deps=rw([], banks(512, 512 + mend),
                                             extra=([kh_ready] + aconst_ops if ih == 0 else [])),
                                     tag=f"kb{g}_{h}")
                        for b in banks(512, 512 + mend):
                            b.on_write(kb)
                        vkh = None
                        for c in range(nch):
                            vkh = prog.op("vector",
                                          lambda e, c=c, b=hb: e.tensor_tensor(
                                              khK_sb[b][:, c * 512:(c + 1) * 512],
                                              kpT_sb[:, c * 512:(c + 1) * 512],
                                              PSB[1 + c][:, 0:512], OP.mult),
                                          deps=(rw([RKH[hb]], [],
                                                extra=[kb] + ([kpT_ready] if ih == 0 else []))
                                                if c == 0 else [kb]),
                                          tag=f"vkh{g}_{h}_{c}")
                            BK[1 + c].on_read(vkh)
                        commit(vkh, [], [RKH[hb]])

                        def vhmul(e, g=g, h=h, b=hb):
                            for mt in range(4 * (g + 1)):
                                inst = e.tensor_scalar(
                                    vh_sb[b][:, mt * 128:(mt + 1) * 128],
                                    vp_sb[:, mt * 128:(mt + 1) * 128],
                                    vhtok32_sb[:, mt * 16 + h: mt * 16 + h + 1], None,
                                    OP.mult, OP.bypass)
                            return inst
                        vvh = prog.op("vector", vhmul,
                                      deps=rw([], [RVH[hb]],
                                              extra=([vh_tp] if ih == 0 else [])),
                                      tag=f"vvh{g}_{h}")
                        commit(vvh, [], [RVH[hb]])

                        csteps = make_pass2(pend) if pend is not None else []
                        ci = 0

                        vl_ops = []
                        for t in range(4):
                            def smm(e, t=t, nch=nch, b=hb):
                                for c in range(nch):
                                    inst = e.matmul(pslice(512 + c * 512, 512 + (c + 1) * 512),
                                                    qh_sb[b][:, t * 128:(t + 1) * 128],
                                                    khK_sb[b][:, c * 512:(c + 1) * 512],
                                                    start=True, stop=True)
                                return inst
                            s1 = prog.op("tensor", smm,
                                         deps=rw([RQH[hb], RKH[hb]], banks(512, 512 + mend),
                                                 extra=[rq, vkh]),
                                         tag=f"s1_{g}_{h}_{t}")
                            for b in banks(512, 512 + mend):
                                b.on_write(s1)
                            commit(s1, [RQH[hb]], [])
                            commit(s1, [RKH[hb]], [])

                            vA = prog.op("vector", lambda e, t=t, g=g: e.tensor_tensor(
                                PSB[1 + g][:, 0:512], PSB[1 + g][:, 0:512],
                                mask4_sb[:, t * 512:(t + 1) * 512], OP.add),
                                deps=[s1], tag=f"vmA{g}_{h}_{t}")
                            BK[1 + g].on_read(vA)
                            BK[1 + g].on_write(vA)
                            for c in range(nch):
                                vmc = prog.op("vector",
                                              lambda e, c=c: e.reduce_max(
                                                  nmx4_sb[:, c:c + 1], PSB[1 + c][:, 0:512],
                                                  axis=AX.X),
                                              deps=rw([], [RNMX]) + ([s1] if c != g else [vA]),
                                              tag=f"vmc{g}_{h}_{t}_{c}")
                                BK[1 + c].on_read(vmc)
                                RNMX.on_read(vmc) if False else None
                            commit(vmc, [], [RNMX])
                            vm = prog.op("vector", lambda e, t=t, nch=nch: e.reduce_max(
                                negmx_sb[:, t:t + 1], nmx4_sb[:, 0:nch], axis=AX.X,
                                negate=True), deps=rw([RNMX], [RNEG]), tag=f"vm{g}_{h}_{t}")
                            commit(vm, [RNMX], [RNEG])
                            se = None
                            for c in range(nch):
                                vcp = prog.op("vector",
                                              lambda e, c=c: e.tensor_copy(
                                                  sstage_sb[:, c * 512:(c + 1) * 512],
                                                  PSB[1 + c][:, 0:512]),
                                              deps=rw([], [RSST], extra=[vm]),
                                              tag=f"sec{g}_{h}_{t}_{c}")
                                BK[1 + c].on_read(vcp)
                                RSST.on_write(vcp)
                                se = prog.op("scalar",
                                             lambda e, t=t, c=c: e.activation(
                                                 trash_sb[:, c * 512:(c + 1) * 512],
                                                 sstage_sb[:, c * 512:(c + 1) * 512],
                                                 AF.Exp, bias=negmx_sb[:, t:t + 1],
                                                 accum_out=sume4_sb[:, c:c + 1]),
                                             deps=rw([RNEG, RSM4], [], extra=[vcp]),
                                             tag=f"se{g}_{h}_{t}_{c}")
                                RSST.on_read(se)
                                RNEG.on_read(se)
                                RSM4.on_write(se)
                            vsm = prog.op("vector",
                                          lambda e, t=t, nch=nch: e.reduce_sum(
                                              sume_sb[:, t:t + 1], sume4_sb[:, 0:nch],
                                              axis=AX.X),
                                          deps=rw([RSM4], [RSME], extra=[se]),
                                          tag=f"vsm{g}_{h}_{t}")
                            commit(vsm, [RSM4], [RSME])
                            sl = prog.op("scalar",
                                         lambda e, t=t, b=hb: e.activation(
                                             lsum_sb[b][:, t:t + 1], sume_sb[:, t:t + 1],
                                             AF.Ln),
                                         deps=rw([RSME], [RLSUM[hb]], extra=[vsm]),
                                         tag=f"sl{g}_{h}_{t}")
                            RSME.on_read(sl)
                            RLSUM[hb].on_write(sl)
                            vl = prog.op("vector",
                                         lambda e, t=t, b=hb: e.tensor_tensor(
                                             lsum_sb[b][:, t:t + 1], lsum_sb[b][:, t:t + 1],
                                             negmx_sb[:, t:t + 1], OP.subtract),
                                         deps=[sl], tag=f"vl{g}_{h}_{t}")
                            RNEG.on_read(vl)
                            RLSUM[hb].on_write(vl)
                            vl_ops.append(vl)
                            nrem = len(csteps) - ci
                            k = (nrem + (3 - t)) // (4 - t)
                            for _ in range(k):
                                csteps[ci]()
                                ci += 1
                        while ci < len(csteps):
                            csteps[ci]()
                            ci += 1
                        if pend is not None:
                            finish_pass2(pend)

                        tts = []
                        for t in range(4):
                            tt = prog.op("tensor",
                                         lambda e, t=t, b=hb: e.matmul(
                                             PSB[6][0:1, t * 128:(t + 1) * 128],
                                             lsum_sb[b][:, t:t + 1], ident_sb[:],
                                             start=True, stop=True, is_transpose=True),
                                         deps=rw([RLSUM[hb]], [BK[6]], extra=[vl_ops[t]]),
                                         tag=f"tt{g}_{h}_{t}")
                            RLSUM[hb].on_read(tt)
                            BK[6].on_write(tt)
                            tts.append(tt)
                        slr = prog.op("vector",
                                      lambda e: e.tensor_copy(lrow2_sb[:], PSB[6][0:1, 0:512]),
                                      deps=rw([], [RLR], extra=[tts[3]]),
                                      tag=f"slr{g}_{h}")
                        BK[6].on_read(slr)
                        commit(slr, [], [RLR])
                        tlb = prog.op("tensor",
                                      lambda e: e.matmul(pslice(3072, 3584), ones1_sb[:],
                                                         lrow2_sb[:], start=True, stop=True),
                                      deps=rw([RLR], [BK[6]], extra=[slr]), tag=f"tlb{g}_{h}")
                        commit(tlb, [RLR], [BK[6]])
                        slb = prog.op("vector",
                                      lambda e, b=hb: e.tensor_copy(lsb_sb[b][:],
                                                                    pslice(3072, 3584)),
                                      deps=rw([], [RLS[hb]], extra=[tlb]), tag=f"slb{g}_{h}")
                        BK[6].on_read(slb)
                        commit(slb, [], [RLS[hb]])

                        pend = {"g": g, "h": h, "hb": hb, "slb": slb, "pv_last": None}

                    for step in make_pass2(pend):
                        step()
                    finish_pass2(pend)
                    pend = None

                    for eh in range(2):
                        for h in range(H):
                            iwh = (g * 2 + eh) * H + h
                            bw = iwh % 2
                            wol = prog.op("scalar",
                                          lambda e, h=h, eh=eh, b=bw: e.dma_start(
                                              out=woh_sb[b][:],
                                              in_=woutT_s[h][:, eh * HALF:(eh + 1) * HALF]),
                                          deps=rw([], [RWOH[bw]],
                                                  extra=wout_st_by_h[h]), dma=True,
                                          tag=f"wol{g}_{eh}_{h}")
                            commit(wol, [], [RWOH[bw]])
                            for t in range(4):
                                def wmm(e, t=t, h=h, b=bw):
                                    for c2 in range(2):
                                        inst = e.matmul(
                                            pslice(t * 1024 + c2 * 512, t * 1024 + (c2 + 1) * 512),
                                            oall_sb[:, h * GS + t * 128: h * GS + t * 128 + 128],
                                            woh_sb[b][:, c2 * 512:(c2 + 1) * 512],
                                            start=(h == 0), stop=(h == H - 1))
                                    return inst
                                wm = prog.op("tensor", wmm,
                                             deps=rw([RWOH[bw], ROA[h]],
                                                     banks(t * 1024, (t + 1) * 1024) if h == 0 else [],
                                                     extra=[wol]),
                                             tag=f"wm{g}_{eh}_{h}_{t}")
                                commit(wm, [RWOH[bw]], [])
                                commit(wm, [ROA[h]], [])
                                for b in banks(t * 1024, (t + 1) * 1024):
                                    b.on_write(wm)
                        for t in range(4):
                            ba = (eh * 4 + t) % 2
                            ev = prog.op("vector",
                                         lambda e, t=t, b=ba: e.tensor_copy(
                                             axb_sb[b][:, 0:512],
                                             pslice(t * 1024, t * 1024 + 512)),
                                         deps=rw([], [RAXB[ba]],
                                                 extra=[prog.last["tensor"]]),
                                         tag=f"ev{g}_{eh}_{t}a")
                            BK[2 * t].on_read(ev)
                            ev = prog.op("vector",
                                         lambda e, t=t, b=ba: e.tensor_copy(
                                             axb_sb[b][:, 512:1024],
                                             pslice(t * 1024 + 512, (t + 1) * 1024)),
                                         tag=f"ev{g}_{eh}_{t}b")
                            BK[2 * t + 1].on_read(ev)
                            commit(ev, [], [RAXB[ba]])
                            st2 = prog.op("scalar",
                                          lambda e, g=g, eh=eh, t=t, b=ba: e.dma_start(
                                              out=attn_s[g * GS + t * 128: g * GS + (t + 1) * 128,
                                                         eh * HALF:(eh + 1) * HALF],
                                              in_=axb_sb[b][:]),
                                          deps=[ev], dma=True, tag=f"st{g}_{eh}_{t}")
                            commit(st2, [RAXB[ba]], [])
                a_last = prog.fence_all()
                for o in a_last:
                    o.needed = True
                emitted = prog.emit_phase(block, emitted)
                for b in BK:
                    b.w = None
                    b.readers = []

        if "F" not in PHASES:
            return nc
        with ExitStack() as fctx:
            inter_sb = sbt(fctx, "inter_sb", [128, 32 * HALF], BF16)
            xnh_sb = sbt(fctx, "xnh_sb", [128, 16 * HALF], BF16)
            w0_sb = [sbt(fctx, f"w0b{i}", [128, 16 * 256], BF16) for i in range(2)]
            w1_sb = [sbt(fctx, f"w1b{i}", [128, 16 * 256], BF16) for i in range(2)]
            wo_sb = [sbt(fctx, f"wob{i}", [128, 32 * 128], BF16) for i in range(2)]
            cbc_sb = sbt(fctx, "cbc_sb", [128, HALF], F32)
            silu_sb = [sbt(fctx, f"silu{i}", [128, HALF], F32) for i in range(2)]
            wev_sb = sbt(fctx, "wev_sb", [128, HALF], F32)
            axf_sb = sbt(fctx, "axf_sb", [128, E], F32)
            xf_sb = sbt(fctx, "xf_sb", [128, E], F32)
            fcol_sb = sbt(fctx, "fcol_sb", [128, 16 * 128], F32)

            RXH = Region()
            RW0 = [Region() for _ in range(2)]
            RW1 = [Region() for _ in range(2)]
            RWO = [Region() for _ in range(2)]
            RCB = Region()
            RSI = [Region() for _ in range(2)]
            RIN = [Region() for _ in range(32)]
            RWE = Region()
            RFFS = Region()
            RAXF = Region()
            RXF = Region()
            RFC = Region()

            dep_A = a_last
            y_stores = []
            for half in range(2):
                def xnh_ld(e, half=half):
                    return [e.dma_start(
                        out=xnh_sb[:].rearrange("p (c n) -> p c n", c=16)[:, q * 8:(q + 1) * 8, :],
                        in_=xnT_s[q * 8:(q + 1) * 8, :, half * HALF:(half + 1) * HALF]
                        .rearrange("c p n -> p c n")) for q in range(2)]
                xl = prog.op("scalar", xnh_ld,
                             deps=rw([], [RXH], extra=(dep_A if half == 0 else [])),
                             dma=True, ndma=2, tag=f"xnh{half}")
                commit(xl, [], [RXH])
                for ex in range(NEXP):
                    iex = half * NEXP + ex
                    cb = prog.op("tensor",
                                 lambda e, ex=ex, half=half: [
                                     e.matmul(pslice(0, 512), d8_sb[:, ex * 128:(ex + 1) * 128],
                                              combT_sb[:, half * HALF: half * HALF + 512],
                                              start=True, stop=True),
                                     e.matmul(pslice(512, 1024), d8_sb[:, ex * 128:(ex + 1) * 128],
                                              combT_sb[:, half * HALF + 512: half * HALF + 1024],
                                              start=True, stop=True)][-1],
                                 deps=rw([], banks(0, 1024),
                                         extra=(dep_A if half == 0 and ex == 0 else [])),
                                 tag=f"cb{half}_{ex}")
                    for b in banks(0, 1024):
                        b.on_write(cb)
                    scb = prog.op("vector",
                                  lambda e: e.tensor_copy(cbc_sb[:, 0:512], pslice(0, 512)),
                                  deps=rw([], [RCB], extra=[cb]), tag=f"scb{half}_{ex}a")
                    BK[0].on_read(scb)
                    scb = prog.op("vector",
                                  lambda e: e.tensor_copy(cbc_sb[:, 512:1024], pslice(512, 1024)),
                                  tag=f"scb{half}_{ex}b")
                    BK[1].on_read(scb)
                    commit(scb, [], [RCB])

                    for mb in range(16):
                        bwb = mb % 2
                        l0 = prog.op("scalar",
                                     lambda e, ex=ex, mb=mb, b=bwb: e.dma_start(
                                         out=w0_sb[b][:], in_=w0b_s[ex][mb]),
                                     deps=rw([], [RW0[bwb]],
                                             extra=prep_done[ex] +
                                             (dep_A if iex == 0 and mb == 0 else [])),
                                     dma=True, tag=f"l0_{iex}_{mb}")
                        commit(l0, [], [RW0[bwb]])
                        l1 = prog.op("scalar",
                                     lambda e, ex=ex, mb=mb, b=bwb: e.dma_start(
                                         out=w1_sb[b][:], in_=w1b_s[ex][mb]),
                                     deps=rw([], [RW1[bwb]]), dma=True,
                                     tag=f"l1_{iex}_{mb}")
                        commit(l1, [], [RW1[bwb]])
                        for mi in range(2):
                            mt = mb * 2 + mi
                            par = mt % 2
                            c0 = 0 if par == 0 else 2048
                            c1 = 1024 if par == 0 else 3072

                            def h0mm(e, b=bwb, mi=mi, c0=c0):
                                for ec in range(16):
                                    for c2 in range(2):
                                        inst = e.matmul(
                                            pslice(c0 + c2 * 512, c0 + (c2 + 1) * 512),
                                            w0_sb[b][:, ec * 256 + mi * 128: ec * 256 + mi * 128 + 128],
                                            xnh_sb[:, ec * HALF + c2 * 512: ec * HALF + c2 * 512 + 512],
                                            start=(ec == 0), stop=(ec == 15))
                                return inst
                            extra0 = [l0, xl] + ([scb] if (par == 0) else [])
                            hm0 = prog.op("tensor", h0mm,
                                          deps=rw([RW0[bwb], RXH], banks(c0, c0 + 1024),
                                                  extra=extra0),
                                          tag=f"h0_{iex}_{mt}")
                            commit(hm0, [RW0[bwb]], [])
                            commit(hm0, [RXH], [])
                            for b in banks(c0, c0 + 1024):
                                b.on_write(hm0)

                            def h1mm(e, b=bwb, mi=mi, c1=c1):
                                for ec in range(16):
                                    for c2 in range(2):
                                        inst = e.matmul(
                                            pslice(c1 + c2 * 512, c1 + (c2 + 1) * 512),
                                            w1_sb[b][:, ec * 256 + mi * 128: ec * 256 + mi * 128 + 128],
                                            xnh_sb[:, ec * HALF + c2 * 512: ec * HALF + c2 * 512 + 512],
                                            start=(ec == 0), stop=(ec == 15))
                                return inst
                            hm1 = prog.op("tensor", h1mm,
                                          deps=rw([RW1[bwb], RXH], banks(c1, c1 + 1024),
                                                  extra=[l1]),
                                          tag=f"h1_{iex}_{mt}")
                            commit(hm1, [RW1[bwb]], [])
                            commit(hm1, [RXH], [])
                            for b in banks(c1, c1 + 1024):
                                b.on_write(hm1)
                            sic = prog.op("vector",
                                          lambda e, par=par, c0=c0: e.tensor_copy(
                                              silu_sb[par][:, 0:512],
                                              pslice(c0, c0 + 512)),
                                          deps=rw([], [RSI[par]], extra=[hm0]),
                                          tag=f"sic_{iex}_{mt}a")
                            BK[c0 // 512].on_read(sic)
                            sic = prog.op("vector",
                                          lambda e, par=par, c0=c0: e.tensor_copy(
                                              silu_sb[par][:, 512:1024],
                                              pslice(c0 + 512, c0 + 1024)),
                                          tag=f"sic_{iex}_{mt}b")
                            BK[c0 // 512 + 1].on_read(sic)
                            si = prog.op("scalar",
                                         lambda e, par=par: e.activation(
                                             silu_sb[par][:], silu_sb[par][:], AF.Silu),
                                         deps=[sic], tag=f"si_{iex}_{mt}")
                            commit(si, [], [RSI[par]])
                            v1 = prog.op("vector",
                                         lambda e, par=par, c1=c1: e.tensor_tensor(
                                             silu_sb[par][:, 0:512], silu_sb[par][:, 0:512],
                                             pslice(c1, c1 + 512), OP.mult),
                                         deps=rw([RSI[par]], [], extra=[si, hm1]),
                                         tag=f"v1_{iex}_{mt}a")
                            BK[c1 // 512].on_read(v1)
                            v1 = prog.op("vector",
                                         lambda e, par=par, c1=c1: e.tensor_tensor(
                                             silu_sb[par][:, 512:1024], silu_sb[par][:, 512:1024],
                                             pslice(c1 + 512, c1 + 1024), OP.mult),
                                         tag=f"v1_{iex}_{mt}b")
                            BK[c1 // 512 + 1].on_read(v1)
                            commit(v1, [RSI[par]], [RSI[par]])
                            v2 = prog.op("vector",
                                         lambda e, mt=mt, par=par: e.tensor_tensor(
                                             inter_sb[:, mt * HALF:(mt + 1) * HALF],
                                             silu_sb[par][:], cbc_sb[:], OP.mult),
                                         deps=rw([RSI[par], RCB], [RIN[mt]]),
                                         tag=f"v2_{iex}_{mt}")
                            commit(v2, [RSI[par]], [])
                            commit(v2, [RCB], [])
                            commit(v2, [], [RIN[mt]])

                    first_acc_dep = [RFFS.w] if RFFS.w is not None else []
                    for dt in range(16):
                        bwo = dt % 2
                        lwo = prog.op("scalar",
                                      lambda e, ex=ex, dt=dt, b=bwo: e.dma_start(
                                          out=wo_sb[b][:], in_=wob_s[ex][dt]),
                                      deps=rw([], [RWO[bwo]]), dma=True,
                                      tag=f"lwo{iex}_{dt}")
                        commit(lwo, [], [RWO[bwo]])
                        pc = (dt % 2) * 1024

                        def womm(e, b=bwo, pc=pc):
                            for mt in range(32):
                                for c2 in range(2):
                                    inst = e.matmul(
                                        pslice(pc + c2 * 512, pc + (c2 + 1) * 512),
                                        wo_sb[b][:, mt * 128:(mt + 1) * 128],
                                        inter_sb[:, mt * HALF + c2 * 512: mt * HALF + c2 * 512 + 512],
                                        start=(mt == 0), stop=(mt == 31))
                                return_inst = inst
                            return return_inst
                        wdeps = rw([RWO[bwo]] + RIN, banks(pc, pc + 1024), extra=[lwo])
                        wm = prog.op("tensor", womm, deps=wdeps, tag=f"wo{iex}_{dt}")
                        commit(wm, [RWO[bwo]], [])
                        for r in RIN:
                            r.on_read(wm)
                        for b in banks(pc, pc + 1024):
                            b.on_write(wm)
                        we = prog.op("vector",
                                     lambda e, pc=pc: e.tensor_copy(wev_sb[:, 0:512],
                                                                    pslice(pc, pc + 512)),
                                     deps=rw([], [RWE], extra=[wm]), tag=f"we{iex}_{dt}a")
                        BK[pc // 512].on_read(we)
                        we = prog.op("vector",
                                     lambda e, pc=pc: e.tensor_copy(wev_sb[:, 512:1024],
                                                                    pslice(pc + 512, pc + 1024)),
                                     tag=f"we{iex}_{dt}b")
                        BK[pc // 512 + 1].on_read(we)
                        commit(we, [], [RWE])
                        ac = prog.op("gpsimd",
                                     lambda e, half=half, ex=ex, dt=dt: e.dma_start(
                                         out=ffnT_s[dt * 128:(dt + 1) * 128,
                                                    half * HALF:(half + 1) * HALF],
                                         in_=wev_sb[:],
                                         **({} if ex == 0 else {"accum_op": OP.add})),
                                     deps=[we] + (first_acc_dep if dt == 0 else []),
                                     dma=True, tag=f"ac{iex}_{dt}")
                        commit(ac, [RWE], [])
                        if dt == 15:
                            RFFS.on_write(ac)

                ffn_done = RFFS.w
                for t in range(8):
                    row0 = half * HALF + t * 128
                    def fcol_ld(e, row0=row0):
                        return [e.dma_start(
                            out=fcol_sb[:].rearrange("p (c n) -> p c n", c=16)
                            [:, q * 8:(q + 1) * 8, :],
                            in_=ffnT_s[q * 1024:(q + 1) * 1024, row0:row0 + 128]
                            .rearrange("(c p) n -> p c n", p=128)) for q in range(2)]
                    fl = prog.op("scalar", fcol_ld,
                                 deps=rw([], [RFC], extra=[ffn_done]), dma=True, ndma=2,
                                 tag=f"fl{half}_{t}")
                    commit(fl, [], [RFC])
                    al = prog.op("scalar",
                                 lambda e, row0=row0: e.dma_start(
                                     out=axf_sb[:], in_=attn_s[row0:row0 + 128, :]),
                                 deps=rw([], [RAXF]), dma=True, tag=f"al{half}_{t}")
                    commit(al, [], [RAXF])
                    xlo = prog.op("scalar",
                                  lambda e, row0=row0: e.dma_start(
                                      out=xf_sb[:], in_=x_d[row0:row0 + 128, :]),
                                  deps=rw([], [RXF]), dma=True, tag=f"xl{half}_{t}")
                    commit(xlo, [], [RXF])

                    def ftp(e):
                        for c in range(16):
                            inst = e.matmul(pslice(c * 128, (c + 1) * 128),
                                            fcol_sb[:, c * 128:(c + 1) * 128], ident_sb[:],
                                            start=True, stop=True, is_transpose=True)
                        return inst
                    tf = prog.op("tensor", ftp,
                                 deps=rw([RFC], banks(0, 2048)), tag=f"tf{half}_{t}")
                    commit(tf, [RFC], [])
                    for b in banks(0, 2048):
                        b.on_write(tf)
                    va = None
                    for q in range(4):
                        va = prog.op("vector",
                                     lambda e, q=q: e.tensor_tensor(
                                         axf_sb[:, q * 512:(q + 1) * 512],
                                         pslice(q * 512, (q + 1) * 512),
                                         axf_sb[:, q * 512:(q + 1) * 512], OP.add),
                                     deps=(rw([RAXF], [], extra=[tf, al]) if q == 0 else []),
                                     tag=f"va{half}_{t}_{q}")
                        BK[q].on_read(va)
                    commit(va, [RAXF], [RAXF])
                    vb = prog.op("vector",
                                 lambda e: e.tensor_tensor(axf_sb[:], axf_sb[:], xf_sb[:],
                                                           OP.add),
                                 deps=rw([RXF], [], extra=[xlo]), tag=f"vb{half}_{t}")
                    commit(vb, [RXF], [RAXF])
                    ys = prog.op("scalar",
                                 lambda e, row0=row0: e.dma_start(
                                     out=y_d[row0:row0 + 128, :], in_=axf_sb[:]),
                                 deps=[vb], dma=True, tag=f"ys{half}_{t}")
                    commit(ys, [RAXF], [])
                    y_stores.append(ys)
            prog.op("scalar", lambda e: e.nop(), deps=[y_stores[-1]], tag="fin")
            emitted = prog.emit_phase(block, emitted)

    return nc


def _consts():
    bf = ml_dtypes.bfloat16
    inv = (10000.0 ** (-np.arange(0, D, 2, dtype=np.float64) / D))
    ang = np.arange(N, dtype=np.float64)[:, None] * inv[None, :]
    cos = np.cos(ang).astype(np.float32)
    sin = np.sin(ang).astype(np.float32)
    cos_tok = cos.astype(bf)
    sin_tok = sin.astype(bf)
    cosT = cos.T.copy().astype(bf)
    sinT = sin.T.copy().astype(bf)
    p = np.arange(128)[:, None]
    c = np.arange(512)[None, :]
    mask4 = np.zeros((128, 4, 512), np.float32)
    maskT4 = np.zeros((128, 4, 512), np.float32)
    for v in range(4):
        vis = (c <= v * 128 + p) & True
        vis = (c < v * 128) | ((c >= v * 128) & (c < (v + 1) * 128) & ((c - v * 128) <= p))
        mask4[:, v, :] = np.where(vis, 0.0, NEG)
        t_ = c // 128
        jj = c % 128
        visT = (t_ > v) | ((t_ == v) & (jj >= p))
        maskT4[:, v, :] = np.where(visT, 0.0, NEG)
    mask4 = mask4.reshape(128, 2048).astype(bf)
    maskT4 = maskT4.reshape(128, 2048).astype(bf)
    d16 = np.zeros((16, 16 * 128), np.float32)
    for h in range(16):
        d16[h, h * 128:(h + 1) * 128] = 1.0
    d8 = np.zeros((8, 8 * 128), np.float32)
    for ex in range(8):
        d8[ex, ex * 128:(ex + 1) * 128] = 1.0
    return {
        "cos_tok": cos_tok, "sin_tok": sin_tok, "cosT": cosT, "sinT": sinT,
        "mask4": mask4, "maskT4": maskT4, "delta16": d16.astype(bf),
        "delta8": d8.astype(np.float32),
        "ones1": np.ones((1, 128), np.float32),
        "ident": np.eye(128, dtype=np.float32),
    }


def kernel(L0_attn_x, L0_attn_Wq, L0_attn_Wkv, L0_attn_Wkvh, L0_attn_Wg,
           L0_attn_Wout, L0_ffn_gate_kernel, L0_ffn_w0_kernel, L0_ffn_w1_kernel,
           L0_ffn_wo_kernel):
    x = np.ascontiguousarray(np.asarray(L0_attn_x, dtype=np.float32))
    shared = {
        "Wq": np.ascontiguousarray(np.asarray(L0_attn_Wq, np.float32)),
        "Wkv": np.ascontiguousarray(np.asarray(L0_attn_Wkv, np.float32)),
        "Wkvh": np.ascontiguousarray(np.asarray(L0_attn_Wkvh, np.float32)),
        "Wg": np.ascontiguousarray(np.asarray(L0_attn_Wg, np.float32)),
        "Wout": np.ascontiguousarray(np.asarray(L0_attn_Wout, np.float32)),
        "gate": np.ascontiguousarray(np.asarray(L0_ffn_gate_kernel, np.float32)),
        "w0": np.ascontiguousarray(np.asarray(L0_ffn_w0_kernel, np.float32)),
        "w1": np.ascontiguousarray(np.asarray(L0_ffn_w1_kernel, np.float32)),
        "wo": np.ascontiguousarray(np.asarray(L0_ffn_wo_kernel, np.float32)),
    }
    shared.update(_consts())
    nc = build_program()
    in_maps = [dict(shared, x=np.ascontiguousarray(x[i])) for i in range(8)]
    res = run_bass_kernel_spmd(nc, in_maps, core_ids=list(range(8)))
    return np.stack([np.asarray(r["y"], np.float32) for r in res.results], axis=0)
